# version 05
# baseline (speedup 1.0000x reference)
import contextlib
import numpy as np
import ml_dtypes
import concourse.bass as bass
import concourse.mybir as mybir
from concourse.bass_utils import run_bass_kernel_spmd

F32 = mybir.dt.float32
BF16 = mybir.dt.bfloat16
AF = mybir.ActivationFunctionType
ALU = mybir.AluOpType
AX = mybir.AxisListType.X
BF = ml_dtypes.bfloat16

S, D = 2048, 1024
NT = S // 128
G3 = 3072
TP1 = S + 1
NB = S // 2
FF = 2048
NE = 9


def _perm_w():
    idx = []
    for g in range(4):
        idx.extend(range(256 * g, 256 * (g + 1)))
        idx.extend(range(1024 + 256 * g, 1024 + 256 * (g + 1)))
    idx.extend(range(2048, 3072))
    return np.array(idx)


def _perm_xw():
    idx = []
    for g in range(4):
        idx.extend(range(256 * g, 256 * (g + 1)))
        idx.extend(range(1024 + 256 * g, 1024 + 256 * (g + 1)))
        idx.extend(range(2048 + 256 * g, 2048 + 256 * (g + 1)))
    return np.array(idx)


def build():
    nc = bass.Bass("TRN2", target_bir_lowering=False, debug=False)
    dt = nc.dram_tensor
    x_in = dt("x", [S, D], F32, kind="ExternalInput").ap()
    wqT_in = dt("wqT", [D, D], F32, kind="ExternalInput").ap()
    wkT_in = dt("wkT", [D, D], F32, kind="ExternalInput").ap()
    wvT_in = dt("wvT", [D, D], F32, kind="ExternalInput").ap()
    woT_in = dt("woT", [D, D], F32, kind="ExternalInput").ap()
    brow_in = dt("brow", [4, D], F32, kind="ExternalInput").ap()
    bo_f_in = dt("bo_f", [128, 8], F32, kind="ExternalInput").ap()
    whhT_in = dt("whhT", [D, G3], F32, kind="ExternalInput").ap()
    wihT_in = dt("wihT", [D, G3], F32, kind="ExternalInput").ap()
    ctxwT_in = dt("ctxwT", [D, D], F32, kind="ExternalInput").ap()
    ctxbf_in = dt("ctxbf", [128, 8], F32, kind="ExternalInput").ap()
    exprwT_in = dt("exprwT", [D, D], F32, kind="ExternalInput").ap()
    gL_in = dt("g9L", [NE * 16, 128, 1024], F32, kind="ExternalInput").ap()
    uL_in = dt("u9L", [NE * 16, 128, 1024], F32, kind="ExternalInput").ap()
    dL_in = dt("d9L", [NE * 8, 128, 2048], F32, kind="ExternalInput").ap()
    cosE_in = dt("cosE", [S, 256], F32, kind="ExternalInput").ap()
    sinE_in = dt("sinE", [S, 256], F32, kind="ExternalInput").ap()
    eyeb_in = dt("eyeb", [128, 128], BF16, kind="ExternalInput").ap()
    eyef_in = dt("eyef", [128, 128], F32, kind="ExternalInput").ap()
    umask_in = dt("umask", [128, 128], BF16, kind="ExternalInput").ap()
    sel8_in = dt("sel8", [128, 8 * 128], BF16, kind="ExternalInput").ap()
    out_d = dt("y", [S, D], F32, kind="ExternalOutput").ap()
    xw_d = dt("xw_scr", [S + 2, G3], BF16, kind="Internal").ap()
    gB_d = dt("gB", [NE * 16, 128, 1024], BF16, kind="Internal").ap()
    uB_d = dt("uB", [NE * 16, 128, 1024], BF16, kind="Internal").ap()
    dB_d = dt("dB", [NE * 8, 128, 2048], BF16, kind="Internal").ap()
    h0tmp_d = dt("h0tmp", [128, 8], F32, kind="Internal").ap()
    pen_in_d = dt("pen_in", [1, 128], F32, kind="Internal").ap()
    pen_out_d = dt("pen_out", [1, 128], F32, kind="Internal").ap()

    es = contextlib.ExitStack()
    with es:
        XNT = es.enter_context(nc.sbuf_tensor("XNT", [128, 8 * S], BF16))
        ATT = es.enter_context(nc.sbuf_tensor("ATT", [128, 8 * S], BF16))
        EYEB = es.enter_context(nc.sbuf_tensor("EYEB", [128, 128], BF16))
        UMASK = es.enter_context(nc.sbuf_tensor("UMASK", [128, 128], BF16))
        SELRT = es.enter_context(nc.sbuf_tensor("SELRT", [128, 128], BF16))
        SEL64 = es.enter_context(nc.sbuf_tensor("SEL64", [128, 128], BF16))
        E32 = es.enter_context(nc.sbuf_tensor("E32c", [128, 4], BF16))
        ONE1 = es.enter_context(nc.sbuf_tensor("ONE1", [128, 1], F32))
        ONEB = es.enter_context(nc.sbuf_tensor("ONEB", [128, 1], BF16))
        MISC = es.enter_context(nc.sbuf_tensor("MISC", [128, 32], F32))
        CW = es.enter_context(nc.sbuf_tensor("CWtm", [128, NT * 8], F32))
        PEN = es.enter_context(nc.sbuf_tensor("PENt", [128, NT], F32))
        H0F = es.enter_context(nc.sbuf_tensor("H0F", [128, 8], F32))
        H0S = es.enter_context(nc.sbuf_tensor("H0S", [128, 256], F32))
        BOF = es.enter_context(nc.sbuf_tensor("BOF", [128, 8], F32))
        SER = es.enter_context(nc.semaphore("SER"))
        LD = es.enter_context(nc.semaphore("LDd"))
        LS = es.enter_context(nc.semaphore("LSd"))
        SM = es.enter_context(nc.semaphore("SMg"))
        STp = es.enter_context(nc.semaphore("STg"))
        SA = es.enter_context(nc.semaphore("SAg"))
        SV = es.enter_context(nc.semaphore("SVg"))
        SD0 = es.enter_context(nc.semaphore("SD0"))
        SD1 = es.enter_context(nc.semaphore("SD1"))
        SGp = es.enter_context(nc.semaphore("SGg"))
        WD = es.enter_context(nc.semaphore("WDd"))
        CCS = es.enter_context(nc.semaphore("CCS"))
        PCD = es.enter_context(nc.semaphore("PCD"))

        ld_cnt = [0]
        ls_cnt = [0]
        ser_base = [0]
        wd_cum = []

        def emit_block(runs, extra_gpsimd=None):
            base = ser_base[0]
            with nc.Block() as blk:
                for ekey, attr in [("v", "vector"), ("s", "scalar"), ("t", "tensor"),
                                   ("g", "gpsimd"), ("y", "sync")]:
                    if ekey == "g" and extra_gpsimd is not None:
                        blk.gpsimd(lambda g: extra_gpsimd(g, base))
                        continue

                    def cb(eng, ekey=ekey):
                        for i, (k, fn) in enumerate(runs):
                            if k != ekey:
                                continue
                            eng.wait_ge(SER, base + i)
                            fn(eng).then_inc(SER, 1)
                    getattr(blk, attr)(cb)
            ser_base[0] = base + len(runs)
            return base

        def mk_gdma(pairs, slow=False):
            def f(g):
                for dst, src in pairs:
                    g.dma_start(out=dst, in_=src,
                                allow_slow_non_contiguous=slow).then_inc(LD, 16)
                    ld_cnt[0] += 16
                g.wait_ge(LD, ld_cnt[0])
                return g.nop()
            return f

        def mk_sdma(pairs, slow=False):
            def f(y):
                for dst, src in pairs:
                    y.dma_start(out=dst, in_=src,
                                allow_slow_non_contiguous=slow).then_inc(LS, 16)
                    ls_cnt[0] += 16
                y.wait_ge(LS, ls_cnt[0])
                return y.nop()
            return f

        pre = contextlib.ExitStack()
        SC1 = pre.enter_context(nc.sbuf_tensor("SC1", [128, D], F32))
        SC3 = pre.enter_context(nc.sbuf_tensor("SC3", [128, 256], F32))
        SC4 = pre.enter_context(nc.sbuf_tensor("SC4", [128, 256], F32))
        SCB = pre.enter_context(nc.sbuf_tensor("SCBa", [128, 1024], BF16))
        STT = pre.enter_context(nc.sbuf_tensor("STT", [128, 8], F32))
        WSTG = pre.enter_context(nc.sbuf_tensor("WSTG", [128, 8 * D], BF16))
        QT = pre.enter_context(nc.sbuf_tensor("QTf", [128, 4 * S], BF16))
        KT = pre.enter_context(nc.sbuf_tensor("KTf", [128, 4 * S], BF16))
        VA = pre.enter_context(nc.sbuf_tensor("VAf", [128, NT * 16 * 65], BF16))
        CCT = pre.enter_context(nc.sbuf_tensor("CCT", [128, 8 * S], BF16))
        PTB = pre.enter_context(nc.sbuf_tensor("PTBb", [128, 1024], BF16))
        CTRG = pre.enter_context(nc.sbuf_tensor("CTRG", [128, 512], F32))
        SPS = pre.enter_context(nc.psum_tensor("SPS", [128, 1024], F32))
        TPB = pre.enter_context(nc.psum_tensor("TPB", [128, 1024], BF16))
        AVP = pre.enter_context(nc.psum_tensor("AVP", [128, 2048], F32))

        runs = []
        R = lambda k, f: runs.append((k, f))

        R("g", mk_gdma([(EYEB[:, :], eyeb_in), (UMASK[:, :], umask_in),
                        (BOF[:, :], bo_f_in)]))

        def f_init(v):
            v.memset(SELRT[:, :], 0.0)
            v.memset(SEL64[:, :], 0.0)
            v.memset(E32[:, :], 0.0)
            v.memset(PTB[:, :], 0.0)
            v.drain()
            v.memset(SELRT[0:1, :], 1.0)
            v.memset(SEL64[64:65, :], 1.0)
            for g in range(4):
                v.memset(E32[32 * g:32 * g + 1, g:g + 1], 1.0)
            v.memset(ONE1[:, :], 1.0)
            v.memset(ONEB[:, :], 1.0)
            v.memset(H0S[:, :], 0.0)
            v.memset(VA[:, :], 1.0)
            v.drain()
            return v.nop()
        R("v", f_init)

        for tt in range(NT):
            R("g", mk_gdma([(SC1[:, :], x_in[tt * 128:(tt + 1) * 128, :])]))

            def f1(v):
                v.reduce_sum(STT[:, 0:1], SC1[:, :], axis=AX)
                v.drain()
                v.tensor_scalar_mul(STT[:, 1:2], STT[:, 0:1], 1.0 / D)
                v.drain()
                v.tensor_scalar(SC1[:, :], SC1[:, :], STT[:, 1:2], None, op0=ALU.subtract)
                v.drain()
                return v.nop()
            R("v", f1)

            def f2(s):
                return s.activation(SCB[:, 0:D], SC1[:, :], AF.Square, accum_out=STT[:, 2:3])
            R("s", f2)

            def f2b(v):
                v.tensor_scalar(STT[:, 5:6], STT[:, 2:3], 1.0 / D, 1e-5,
                                op0=ALU.mult, op1=ALU.add)
                v.drain()
                return v.nop()
            R("v", f2b)

            def f2c(s):
                return s.activation(STT[:, 3:4], STT[:, 5:6], AF.Sqrt)
            R("s", f2c)

            def f3(v):
                v.reciprocal(STT[:, 4:5], STT[:, 3:4])
                v.drain()
                v.tensor_scalar(SC1[:, :], SC1[:, :], STT[:, 4:5], None, op0=ALU.mult)
                v.drain()
                v.tensor_copy(SCB[:, 0:D], SC1[:, :])
                v.drain()
                return v.nop()
            R("v", f3)

            def f4(t):
                for i in range(8):
                    ins = t.matmul(TPB[:, i * 128:(i + 1) * 128],
                                   SCB[:, i * 128:(i + 1) * 128], EYEB[:, :],
                                   start=True, stop=True, is_transpose=True)
                return ins
            R("t", f4)

            def f5(v, tt=tt):
                for i in range(8):
                    ins = v.tensor_copy(XNT[:, i * S + tt * 128:i * S + (tt + 1) * 128],
                                        TPB[:, i * 128:(i + 1) * 128])
                return ins
            R("v", f5)

        R("g", mk_gdma([(WSTG[:, :].rearrange("p (c j) -> p c j", c=8),
                        wvT_in.rearrange("(c p) j -> p c j", p=128)),
                       (PTB[0:1, 0:D], brow_in[2:3, :])]))
        for tt in range(NT):
            def fv_t(t, tt=tt):
                for jh in range(2):
                    for c in range(8):
                        t.matmul(SPS[:, jh * 512:jh * 512 + 512],
                                 XNT[:, c * S + tt * 128:c * S + (tt + 1) * 128],
                                 WSTG[:, c * D + jh * 512:c * D + jh * 512 + 512],
                                 start=(c == 0), stop=False)
                    ins = t.matmul(SPS[:, jh * 512:jh * 512 + 512], SELRT[:, :],
                                   PTB[:, jh * 512:jh * 512 + 512],
                                   start=False, stop=True)
                return ins
            R("t", fv_t)

            def fv_v(v, tt=tt):
                for jh in range(2):
                    base = (tt * 16 + jh * 8) * 65
                    ins = v.tensor_copy(
                        VA[:, base:base + 8 * 65].rearrange("p (h e) -> p h e", h=8)[:, :, 0:64],
                        SPS[:, jh * 512:jh * 512 + 512].rearrange("p (h e) -> p h e", h=8))
                return ins
            R("v", fv_v)

        for hg in range(2):
            R("g", mk_gdma([
                (WSTG[:, 0:4096].rearrange("p (c j) -> p c j", c=8),
                 wqT_in[:, hg * 512:(hg + 1) * 512].rearrange("(c p) j -> p c j", p=128)),
                (WSTG[:, 4096:8192].rearrange("p (c j) -> p c j", c=8),
                 wkT_in[:, hg * 512:(hg + 1) * 512].rearrange("(c p) j -> p c j", p=128)),
                (PTB[0:1, 0:512], brow_in[0:1, hg * 512:(hg + 1) * 512]),
                (PTB[0:1, 512:1024], brow_in[1:2, hg * 512:(hg + 1) * 512]),
            ]))
            for tt in range(NT):
                R("y", mk_sdma([(CTRG[:, 0:256], cosE_in[tt * 128:(tt + 1) * 128, :]),
                                (CTRG[:, 256:512], sinE_in[tt * 128:(tt + 1) * 128, :])]))

                def fp_t(t, tt=tt):
                    for qk in range(2):
                        for c in range(8):
                            t.matmul(SPS[:, qk * 512:qk * 512 + 512],
                                     XNT[:, c * S + tt * 128:c * S + (tt + 1) * 128],
                                     WSTG[:, qk * 4096 + c * 512:qk * 4096 + c * 512 + 512],
                                     start=(c == 0), stop=False)
                        ins = t.matmul(SPS[:, qk * 512:qk * 512 + 512], SELRT[:, :],
                                       PTB[:, qk * 512:qk * 512 + 512],
                                       start=False, stop=True)
                    return ins
                R("t", fp_t)

                for qk in range(2):
                    scl = 0.125 if qk == 0 else 1.0

                    def fp_rope(v, qk=qk, scl=scl):
                        v.tensor_copy(SC1[:, 0:512], SPS[:, qk * 512:qk * 512 + 512])
                        v.drain()
                        QE = SC1[:, 0:512].rearrange("p (h d) -> p h d", h=8)[:, :, 0:64:2]
                        QO = SC1[:, 0:512].rearrange("p (h d) -> p h d", h=8)[:, :, 1:64:2]
                        CO = CTRG[:, 0:256].rearrange("p (h i) -> p h i", h=8)
                        SI = CTRG[:, 256:512].rearrange("p (h i) -> p h i", h=8)
                        S3 = SC3[:, :].rearrange("p (h i) -> p h i", h=8)
                        S4 = SC4[:, :].rearrange("p (h i) -> p h i", h=8)
                        RE = SC1[:, 512:1024].rearrange("p (h d) -> p h d", h=8)[:, :, 0:64:2]
                        RO = SC1[:, 512:1024].rearrange("p (h d) -> p h d", h=8)[:, :, 1:64:2]
                        v.tensor_tensor(S3, QE, CO, op=ALU.mult)
                        v.drain()
                        v.tensor_tensor(S4, QO, SI, op=ALU.mult)
                        v.drain()
                        v.tensor_tensor(RE, S3, S4, op=ALU.subtract)
                        v.drain()
                        v.tensor_tensor(S3, QE, SI, op=ALU.mult)
                        v.drain()
                        v.tensor_tensor(S4, QO, CO, op=ALU.mult)
                        v.drain()
                        v.tensor_tensor(RO, S3, S4, op=ALU.add)
                        v.drain()
                        v.tensor_scalar_mul(SCB[:, 0:512], SC1[:, 512:1024], scl)
                        v.drain()
                        return v.nop()
                    R("v", fp_rope)

                    def fp_tr(t, qk=qk):
                        for jj in range(4):
                            o0 = qk * 512 + jj * 128
                            ins = t.matmul(TPB[:, o0:o0 + 128],
                                           SCB[:, jj * 128:(jj + 1) * 128], EYEB[:, :],
                                           start=True, stop=True, is_transpose=True)
                        return ins
                    R("t", fp_tr)

                    def fp_ev(v, tt=tt, qk=qk):
                        DSTQK = QT if qk == 0 else KT
                        for jj in range(4):
                            o0 = qk * 512 + jj * 128
                            ins = v.tensor_copy(DSTQK[:, jj * S + tt * 128:jj * S + (tt + 1) * 128],
                                                TPB[:, o0:o0 + 128])
                        return ins
                    R("v", fp_ev)

            for hl in range(8):
                h = hg * 8 + hl
                p0 = (hl % 2) * 64
                cj = hl // 2
                ccj = hg * 4 + cj
                for kt in range(NT):
                    q0 = kt * 128
                    for b2 in range(q0 // 1024, 2):
                        lo = max(q0, b2 * 1024)
                        hi = (b2 + 1) * 1024
                        rel = b2 * 1024

                        def fs_t(t, p0=p0, cj=cj, kt=kt, q0=q0, lo=lo, hi=hi, rel=rel):
                            kl = KT[p0:p0 + 64, cj * S + kt * 128:cj * S + (kt + 1) * 128]
                            c0 = lo
                            ins = None
                            if lo == q0:
                                t.matmul(SPS[:, q0 - rel:q0 - rel + 128], kl,
                                         QT[p0:p0 + 64, cj * S + q0:cj * S + q0 + 128],
                                         start=True, stop=False)
                                ins = t.matmul(SPS[:, q0 - rel:q0 - rel + 128],
                                               EYEB[:, :], UMASK[:, :],
                                               start=False, stop=True)
                                c0 = q0 + 128
                            while c0 < hi:
                                c1 = min(((c0 - rel) // 512 + 1) * 512 + rel, hi)
                                ins = t.matmul(SPS[:, c0 - rel:c1 - rel], kl,
                                               QT[p0:p0 + 64, cj * S + c0:cj * S + c1],
                                               start=True, stop=True)
                                c0 = c1
                            return ins
                        R("t", fs_t)

                        if lo > b2 * 1024:
                            def fs_pad(v, lo=lo, rel=rel):
                                v.memset(PTB[:, 0:lo - rel], 0.0)
                                v.drain()
                                return v.nop()
                            R("v", fs_pad)

                        def fs_exp(s, lo=lo, hi=hi, rel=rel):
                            return s.activation(PTB[:, lo - rel:1024],
                                                SPS[:, lo - rel:hi - rel], AF.Exp)
                        R("s", fs_exp)

                        def fs_av(t, h=h, kt=kt, b2=b2):
                            for sb in range(2):
                                qs = b2 * 1024 + sb * 512
                                ins = t.matmul(AVP[0:65, qs:qs + 512],
                                               VA[:, (kt * 16 + h) * 65:(kt * 16 + h) * 65 + 65],
                                               PTB[:, sb * 512:sb * 512 + 512],
                                               start=(kt == 0), stop=(kt == qs // 128 + 3))
                            return ins
                        R("t", fs_av)

                for b2 in range(2):
                    def fn_rec(v, b2=b2):
                        v.reciprocal(SC1[64:65, 0:1024], AVP[64:65, b2 * 1024:b2 * 1024 + 1024])
                        v.drain()
                        v.tensor_copy(PTB[64:65, 0:1024], SC1[64:65, 0:1024])
                        v.drain()
                        return v.nop()
                    R("v", fn_rec)

                    def fn_bc(t):
                        for sb in range(2):
                            ins = t.matmul(SPS[:, sb * 512:(sb + 1) * 512], SEL64[:, :],
                                           PTB[:, sb * 512:(sb + 1) * 512], start=True, stop=True)
                        return ins
                    R("t", fn_bc)

                    def fn_out(v, p0=p0, ccj=ccj, b2=b2):
                        v.tensor_copy(SC1[:, 0:1024], SPS[:, 0:1024])
                        v.drain()
                        if p0 == 0:
                            v.tensor_tensor(CCT[0:64, ccj * S + b2 * 1024:ccj * S + b2 * 1024 + 1024],
                                            AVP[0:64, b2 * 1024:b2 * 1024 + 1024],
                                            SC1[0:64, 0:1024], op=ALU.mult)
                        else:
                            v.tensor_tensor(SCB[0:64, 0:1024],
                                            AVP[0:64, b2 * 1024:b2 * 1024 + 1024],
                                            SC1[0:64, 0:1024], op=ALU.mult)
                        v.drain()
                        return v.nop()
                    R("v", fn_out)
                    if p0 != 0:
                        R("y", mk_sdma([(CCT[64:128, ccj * S + b2 * 1024:ccj * S + b2 * 1024 + 1024],
                                         SCB[0:64, 0:1024])]))

        R("g", mk_gdma([(WSTG[:, :].rearrange("p (c j) -> p c j", c=8),
                        woT_in.rearrange("(c p) j -> p c j", p=128))]))
        for jc in range(8):
            for tsp in range(2):
                def fo_t(t, jc=jc, tsp=tsp):
                    for half in range(2):
                        t0 = (tsp * 2 + half) * 512
                        for c in range(8):
                            ins = t.matmul(SPS[:, half * 512:half * 512 + 512],
                                           WSTG[:, c * D + jc * 128:c * D + jc * 128 + 128],
                                           CCT[:, c * S + t0:c * S + t0 + 512],
                                           start=(c == 0), stop=(c == 7))
                    return ins
                R("t", fo_t)

                def fo_v(v, jc=jc, tsp=tsp):
                    for half in range(2):
                        t0 = (tsp * 2 + half) * 512
                        ins = v.tensor_scalar(ATT[:, jc * S + t0:jc * S + t0 + 512],
                                              SPS[:, half * 512:half * 512 + 512],
                                              BOF[:, jc:jc + 1], None, op0=ALU.add)
                    return ins
                R("v", fo_v)

        emit_block(runs)
        pre.close()

        g4 = contextlib.ExitStack()
        HST = g4.enter_context(nc.sbuf_tensor("HST", [128, 8 * TP1], BF16))

        xsc = contextlib.ExitStack()
        WIH = xsc.enter_context(nc.sbuf_tensor("WIH", [128, 8 * G3], BF16))
        XSC = xsc.enter_context(nc.sbuf_tensor("XSC", [128, G3], BF16))
        CXB = xsc.enter_context(nc.sbuf_tensor("CXBx", [128, 8], F32))
        xPSa = xsc.enter_context(nc.psum_tensor("xPSa", [128, 512], F32))
        xPSb = xsc.enter_context(nc.psum_tensor("xPSb", [128, 512], F32))

        runs = []
        R = lambda k, f: runs.append((k, f))
        R("g", mk_gdma([(WIH[:, :].rearrange("p (c j) -> p c j", c=8),
                        wihT_in.rearrange("(c p) j -> p c j", p=128)),
                       (CXB[:, :], ctxbf_in)]))
        for tt in range(NT):
            for spp in range(3):
                def fx_t(t, tt=tt, spp=spp):
                    for half in range(2):
                        sl = spp * 2 + half
                        ps = xPSa if half == 0 else xPSb
                        for c in range(8):
                            ins = t.matmul(ps[:, :],
                                           XNT[:, c * S + tt * 128:c * S + (tt + 1) * 128],
                                           WIH[:, c * G3 + sl * 512:c * G3 + sl * 512 + 512],
                                           start=(c == 0), stop=(c == 7))
                    return ins
                R("t", fx_t)

                def fx_v(v, spp=spp):
                    for half in range(2):
                        sl = spp * 2 + half
                        ps = xPSa if half == 0 else xPSb
                        ins = v.tensor_copy(XSC[:, sl * 512:sl * 512 + 512], ps[:, :])
                    return ins
                R("v", fx_v)
            R("y", mk_sdma([(xw_d[tt * 128:(tt + 1) * 128, :], XSC[:, :])]))

        def fz(v):
            v.memset(XSC[:, :], 0.0)
            v.drain()
            return v.nop()
        R("v", fz)
        R("y", mk_sdma([(xw_d[S:S + 1, :], XSC[0:1, :]),
                        (xw_d[S + 1:S + 2, :], XSC[1:2, :])]))

        R("g", mk_gdma([(WIH[:, 0:8 * D].rearrange("p (c j) -> p c j", c=8),
                        ctxwT_in.rearrange("(c p) j -> p c j", p=128))]))

        def fh_v1(v):
            for c in range(8):
                v.reduce_sum(MISC[:, c:c + 1], XNT[:, c * S:(c + 1) * S], axis=AX)
                v.drain()
            v.tensor_scalar_mul(MISC[:, 0:8], MISC[:, 0:8], 1.0 / S)
            v.drain()
            v.tensor_copy(XSC[:, 0:8], MISC[:, 0:8])
            v.drain()
            return v.nop()
        R("v", fh_v1)

        def fh_t(t):
            for j in range(8):
                for k in range(8):
                    ins = t.matmul(xPSa[:, j:j + 1],
                                   WIH[:, k * D + j * 128:k * D + j * 128 + 128],
                                   XSC[:, k:k + 1], start=(k == 0), stop=(k == 7))
            return ins
        R("t", fh_t)

        def fh_v2(v):
            return v.tensor_tensor(H0F[:, :], xPSa[:, 0:8], CXB[:, :], op=ALU.add)
        R("v", fh_v2)
        R("y", mk_sdma([(h0tmp_d, H0F[:, :])]))
        R("g", mk_gdma([(H0S[0:128:32, :].rearrange("g (h f) -> g h f", h=2),
                         h0tmp_d.rearrange("p (g h) -> g h p", g=4))], slow=True))

        def f_pcast(g):
            for e in range(NE):
                g.dma_start(out=gB_d[e * 16:(e + 1) * 16],
                            in_=gL_in[e * 16:(e + 1) * 16]).then_inc(PCD, 16)
                g.dma_start(out=uB_d[e * 16:(e + 1) * 16],
                            in_=uL_in[e * 16:(e + 1) * 16]).then_inc(PCD, 16)
                g.dma_start(out=dB_d[e * 8:(e + 1) * 8],
                            in_=dL_in[e * 8:(e + 1) * 8]).then_inc(PCD, 16)
            return g.nop()
        R("g", f_pcast)
        emit_block(runs)
        xsc.close()
        P_gru = ser_base[0]

        gsc = contextlib.ExitStack()
        WT = gsc.enter_context(nc.sbuf_tensor("WTg", [128, 8 * G3], BF16))
        HTR = gsc.enter_context(nc.sbuf_tensor("HTR", [128, 16], BF16))
        SGb = gsc.enter_context(nc.sbuf_tensor("SGb", [128, 1024], F32))
        NG1 = gsc.enter_context(nc.sbuf_tensor("NG1", [128, 512], F32))
        T1 = gsc.enter_context(nc.sbuf_tensor("T1g", [128, 512], F32))
        T2 = gsc.enter_context(nc.sbuf_tensor("T2g", [128, 512], F32))
        NN1 = gsc.enter_context(nc.sbuf_tensor("NN1", [128, 512], F32))
        ZH = gsc.enter_context(nc.sbuf_tensor("ZHg", [128, 512], F32))
        NT2 = gsc.enter_context(nc.sbuf_tensor("NT2", [128, 512], F32))
        SG2 = gsc.enter_context(nc.sbuf_tensor("SG2", [128, 512], F32))
        RGS = gsc.enter_context(nc.sbuf_tensor("RGS", [128, 2 * 768], BF16))
        psA = [gsc.enter_context(nc.psum_tensor(f"psA{p}", [128, 512], F32)) for p in range(2)]
        psB = [gsc.enter_context(nc.psum_tensor(f"psB{p}", [128, 256], F32)) for p in range(2)]
        psTg = [gsc.enter_context(nc.psum_tensor(f"psTg{g}", [128, 4], F32)) for g in range(4)]
        H = H0S

        with nc.Block() as blk:
            @blk.gpsimd
            def _(g):
                g.wait_ge(SER, P_gru)
                g.dma_start(out=WT[:, :].rearrange("p (c j) -> p c j", c=8),
                            in_=whhT_in.rearrange("(c p) j -> p c j", p=128)).then_inc(LD, 16)
                ld_cnt[0] += 16
                g.wait_ge(LD, ld_cnt[0])
                g.nop().then_inc(SGp, 1)
                with g.Fori(0, NB) as i:
                    for par in range(2):
                        step = i * 2 + par
                        sveb = i * 14 + par * 7 + 1
                        g.wait_ge(SV, sveb + 7)
                        col = step + 1
                        g.tensor_copy(HST[:, :].rearrange("p (c t) -> p c t", c=8)[:, :, bass.ts(col, 1)],
                                      HTR[:, par * 8:par * 8 + 8].rearrange("p (c o) -> p c o", o=1)).then_inc(SGp, 1)

            @blk.vector
            def _(v):
                v.wait_ge(SER, P_gru)
                for sb in [SGb, NG1, NT2, SG2, T2, T1, NN1, ZH]:
                    v.memset(sb[:, :], 0.0)
                v.memset(RGS[:, :], 0.0)
                for ps in psA + psB + psTg:
                    v.memset(ps[:, :], 0.0)
                v.drain()
                v.tensor_copy(HST[:, :].rearrange("p (c t) -> p c t", c=8)[:, :, 0:1],
                              H0F[:, :].rearrange("p (c o) -> p c o", o=1))
                v.tensor_copy(HTR[:, 8:16], H0F[:, :])
                v.drain()
                v.nop().then_inc(SV, 1)
                with v.Fori(0, NB) as i:
                    for par in range(2):
                        sveb = i * 14 + par * 7 + 1
                        sab = i * 6 + par * 3 + 6
                        pB = psB[par]
                        q = par * 256
                        q2 = par * 512
                        v.wait_ge(SA, sab + 1)
                        v.tensor_tensor(T1[0:97, q:q + 256], SGb[0:97, q2:q2 + 256],
                                        pB[0:97, :], op=ALU.mult).then_inc(SV, 1)
                        v.tensor_tensor(ZH[0:97, q:q + 256], SGb[0:97, q2 + 256:q2 + 512],
                                        H[0:97, :], op=ALU.mult).then_inc(SV, 1)
                        v.wait_ge(SV, sveb + 1)
                        v.tensor_tensor(T2[0:97, q:q + 256], T1[0:97, q:q + 256],
                                        RGS[0:97, par * 768 + 512:par * 768 + 768], op=ALU.add).then_inc(SV, 1)
                        v.wait_ge(SA, sab + 3)
                        v.tensor_scalar(NT2[0:97, q:q + 256], SG2[0:97, q:q + 256], 2.0, 1.0,
                                        op0=ALU.mult, op1=ALU.subtract).then_inc(SV, 1)
                        v.wait_ge(SV, sveb + 4)
                        v.tensor_tensor(NN1[0:97, q:q + 256], NT2[0:97, q:q + 256],
                                        NG1[0:97, q:q + 256], op=ALU.mult).then_inc(SV, 1)
                        v.wait_ge(SV, sveb + 5)
                        v.tensor_tensor(H[0:97, :], NN1[0:97, q:q + 256],
                                        ZH[0:97, q:q + 256], op=ALU.add).then_inc(SV, 1)
                        v.wait_ge(STp, i * 2 + par + 1)
                        for g4i in range(3):
                            v.tensor_copy(HTR[:, par * 8 + 2 * g4i:par * 8 + 2 * g4i + 2],
                                          psTg[g4i][:, par * 2:par * 2 + 2])
                        v.tensor_copy(HTR[:, par * 8 + 6:par * 8 + 8],
                                      psTg[3][:, par * 2:par * 2 + 2]).then_inc(SV, 1)

            @blk.scalar
            def _(s):
                s.wait_ge(SER, P_gru)
                s.sem_inc(SA, 6)
                with s.Fori(0, NB) as i:
                    for par in range(2):
                        sveb = i * 14 + par * 7 + 1
                        q = par * 256
                        q2 = par * 512
                        s.wait_ge(SM, i * 2 + par + 1)
                        s.activation(SGb[0:97, q2:q2 + 512], psA[par][0:97, :],
                                     AF.Sigmoid).then_inc(SA, 1)
                        s.activation(NG1[0:97, q:q + 256], psA[par][0:97, 256:512],
                                     AF.Sigmoid, scale=-1.0).then_inc(SA, 1)
                        s.wait_ge(SV, sveb + 3)
                        s.activation(SG2[0:97, q:q + 256], T2[0:97, q:q + 256],
                                     AF.Sigmoid, scale=2.0).then_inc(SA, 1)

            @blk.tensor
            def _(t):
                t.wait_ge(SER, P_gru)
                t.wait_ge(SGp, 1)
                with t.Fori(0, NB) as i:
                    for par in range(2):
                        sveb = i * 14 + par * 7 + 1
                        pA, pB = psA[par], psB[par]
                        opar = 1 - par
                        t.wait_ge(SV, sveb)
                        t.wait_ge(SA, i * 6 + par * 3 + 2)
                        for c in range(8):
                            lh = HTR[:, opar * 8 + c:opar * 8 + c + 1]
                            base = c * G3
                            for g in range(4):
                                t.matmul(pA[32 * g:32 * g + 1, 0:512], lh,
                                         WT[:, base + 512 * g:base + 512 * g + 512],
                                         start=(c == 0), stop=False, tile_position=(0, 32 * g))
                                t.matmul(pB[32 * g:32 * g + 1, 0:256], lh,
                                         WT[:, base + 2048 + 256 * g:base + 2048 + 256 * (g + 1)],
                                         start=(c == 0), stop=(c == 7), tile_position=(0, 32 * g))
                        SDp = SD0 if par == 0 else SD1
                        t.wait_ge(SDp, i * 16 + 16)
                        for g in range(4):
                            ins = t.matmul(pA[32 * g:32 * g + 1, 0:512], E32[:, g:g + 1],
                                           RGS[:, par * 768:par * 768 + 512],
                                           start=False, stop=True, tile_position=(0, 32 * g))
                        ins.then_inc(SM, 1)
                        t.wait_ge(SV, sveb + 6)
                        t.drain()
                        for c in range(8):
                            g, half = c // 2, c % 2
                            ins = t.matmul(psTg[g][:, par * 2 + half:par * 2 + half + 1],
                                           H[32 * g:32 * g + 1, 128 * half:128 * half + 128],
                                           ONE1[32 * g:32 * g + 1, 0:1],
                                           start=True, stop=True, tile_position=(32 * g, 0))
                        t.drain()
                        ins.then_inc(STp, 1)

            @blk.sync
            def _(y):
                y.wait_ge(SER, P_gru)
                for j in range(2):
                    SDp = SD0 if j % 2 == 0 else SD1
                    y.dma_start(out=RGS[0:128:32, j * 768:(j + 1) * 768],
                                in_=xw_d[j:j + 1, :].rearrange("r (g q) -> (r g) q", g=4)).then_inc(SDp, 16)
                with y.Fori(0, NB) as i:
                    for par in range(2):
                        step = i * 2 + par
                        sveb = i * 14 + par * 7 + 1
                        y.wait_ge(SV, sveb + 3)
                        r = step + 2
                        SDp = SD0 if par == 0 else SD1
                        y.dma_start(out=RGS[0:128:32, par * 768:par * 768 + 768],
                                    in_=xw_d[bass.ts(r, 1), :].rearrange("r (g q) -> (r g) q", g=4)).then_inc(SDp, 16)
        gsc.close()
        SV_END = 1 + 7 * S
        SGP_END = 1 + S

        ssc = contextlib.ExitStack()
        EXT = ssc.enter_context(nc.sbuf_tensor("EXT", [128, 8 * S], BF16))
        WES = ssc.enter_context(nc.sbuf_tensor("WES", [128, 8 * D], BF16))
        PRD = ssc.enter_context(nc.sbuf_tensor("PRD", [128, 80 * 128], BF16))
        SSB = ssc.enter_context(nc.sbuf_tensor("SSB", [128, 80], F32))
        G2B = ssc.enter_context(nc.sbuf_tensor("G2B", [128, 64], F32))
        SCs = ssc.enter_context(nc.sbuf_tensor("SCs", [128, 64], F32))
        PENR = ssc.enter_context(nc.sbuf_tensor("PENR", [128, 128], F32))
        sPS = ssc.enter_context(nc.psum_tensor("sPS", [128, 512], F32))
        sPT = ssc.enter_context(nc.psum_tensor("sPT", [128, 128], F32))

        runs = []
        R = lambda k, f: runs.append((k, f))

        def f_bridge(v):
            v.wait_ge(SV, SV_END)
            v.wait_ge(SGp, SGP_END)
            v.memset(PENR[:, :], 1.0)
            v.drain()
            return v.nop()
        R("v", f_bridge)

        R("g", mk_gdma([(WES[:, :].rearrange("p (c j) -> p c j", c=8),
                        exprwT_in.rearrange("(c p) j -> p c j", p=128))]))
        for jc in range(8):
            for tsp in range(4):
                def fe_t(t, jc=jc, tsp=tsp):
                    t0 = tsp * 512
                    for c in range(8):
                        ins = t.matmul(sPS[:, :], WES[:, c * D + jc * 128:c * D + jc * 128 + 128],
                                       XNT[:, c * S + t0:c * S + t0 + 512],
                                       start=(c == 0), stop=(c == 7))
                    return ins
                R("t", fe_t)

                def fe_v(v, jc=jc, tsp=tsp):
                    t0 = tsp * 512
                    return v.tensor_copy(EXT[:, jc * S + t0:jc * S + t0 + 512], sPS[:, :])
                R("v", fe_v)

        for tt in range(NT):
            def f_prod(v, tt=tt):
                for e in range(8):
                    for f in range(8):
                        a = HST[:, e * TP1 + 1 + tt * 128:e * TP1 + 1 + (tt + 1) * 128]
                        b = HST[:, f * TP1 + 1 + tt * 128:f * TP1 + 1 + (tt + 1) * 128]
                        v.tensor_tensor(PRD[:, (e * 8 + f) * 128:(e * 8 + f) * 128 + 128],
                                        a, b, op=ALU.mult)
                for e in range(8):
                    a = HST[:, e * TP1 + 1 + tt * 128:e * TP1 + 1 + (tt + 1) * 128]
                    xb = EXT[:, e * S + tt * 128:e * S + (tt + 1) * 128]
                    v.tensor_tensor(PRD[:, (64 + e) * 128:(64 + e) * 128 + 128],
                                    a, xb, op=ALU.mult)
                    v.tensor_tensor(PRD[:, (72 + e) * 128:(72 + e) * 128 + 128],
                                    xb, xb, op=ALU.mult)
                v.drain()
                return v.nop()
            R("v", f_prod)

            def f_sums(t):
                for idx in range(80):
                    ins = t.matmul(sPT[:, idx:idx + 1],
                                   PRD[:, idx * 128:(idx + 1) * 128], ONEB[:, :],
                                   start=True, stop=True)
                return ins
            R("t", f_sums)

            def f_ev(v):
                return v.tensor_copy(SSB[:, :], sPT[:, 0:80])
            R("v", f_ev)

            def f_sq(s):
                s.activation(MISC[:, 8:16], SSB[:, 0:72:9], AF.Sqrt)
                s.drain()
                return s.activation(MISC[:, 16:24], SSB[:, 72:80], AF.Sqrt)
            R("s", f_sq)

            def f_elt(v, tt=tt):
                v.tensor_scalar_max(MISC[:, 8:16], MISC[:, 8:16], 1e-12)
                v.drain()
                v.reciprocal(MISC[:, 24:32], MISC[:, 8:16])
                v.drain()
                for f in range(8):
                    v.tensor_scalar(G2B[:, f:64:8], SSB[:, f:64:8],
                                    MISC[:, 24 + f:25 + f], None, op0=ALU.mult)
                v.drain()
                for e in range(8):
                    v.tensor_scalar(G2B[:, 8 * e:8 * e + 8], G2B[:, 8 * e:8 * e + 8],
                                    MISC[:, 24 + e:25 + e], None, op0=ALU.mult)
                v.drain()
                v.tensor_scalar(G2B[:, 0:64:9], G2B[:, 0:64:9], 1.0, None, op0=ALU.subtract)
                v.drain()
                v.tensor_tensor(G2B[:, :], G2B[:, :], G2B[:, :], op=ALU.mult)
                v.drain()
                for e in range(8):
                    v.reduce_sum(SCs[:, e:e + 1], G2B[:, 8 * e:8 * e + 8], axis=AX)
                v.drain()
                v.tensor_scalar_max(SCs[:, 8:16], SCs[:, 0:8], 1e-24)
                v.drain()
                v.reciprocal(SCs[:, 16:24], SCs[:, 8:16])
                v.drain()
                v.tensor_tensor(SCs[:, 24:32], SCs[:, 0:8], SCs[:, 16:24], op=ALU.mult)
                v.drain()
                v.reduce_sum(PEN[:, tt:tt + 1], SCs[:, 24:32], axis=AX)
                v.drain()
                v.tensor_tensor(SCs[:, 32:40], SSB[:, 64:72], MISC[:, 24:32], op=ALU.mult)
                v.drain()
                v.tensor_tensor(SCs[:, 40:48], MISC[:, 8:16], MISC[:, 24:32], op=ALU.mult)
                v.drain()
                v.tensor_scalar_max(SCs[:, 40:48], SCs[:, 40:48], 1e-8)
                v.drain()
                v.tensor_scalar_max(SCs[:, 48:56], MISC[:, 16:24], 1e-8)
                v.drain()
                v.tensor_tensor(SCs[:, 56:64], SCs[:, 40:48], SCs[:, 48:56], op=ALU.mult)
                v.drain()
                v.reciprocal(SCs[:, 40:48], SCs[:, 56:64])
                v.drain()
                v.tensor_tensor(SCs[:, 48:56], SCs[:, 32:40], SCs[:, 40:48], op=ALU.mult)
                v.drain()
                v.tensor_scalar(CW[:, tt * 8:tt * 8 + 8], SCs[:, 48:56], -1.0, 1.0,
                                op0=ALU.mult, op1=ALU.add)
                v.drain()
                return v.nop()
            R("v", f_elt)

        def f_pt(t):
            return t.matmul(sPT[0:1, 0:16], ONE1[:, :], PEN[:, :], start=True, stop=True)
        R("t", f_pt)

        def f_pv(v):
            v.tensor_copy(MISC[0:1, 0:16], sPT[0:1, 0:16])
            v.drain()
            v.reduce_sum(MISC[0:1, 16:17], MISC[0:1, 0:16], axis=AX)
            v.drain()
            v.tensor_scalar_mul(MISC[0:1, 16:17], MISC[0:1, 16:17], 1.0 / (8.0 * S))
            v.drain()
            return v.nop()
        R("v", f_pv)

        def f_pb(v):
            v.tensor_scalar(PENR[0:1, 0:128], PENR[0:1, 0:128],
                            MISC[0:1, 16:17], None, op0=ALU.mult)
            v.drain()
            return v.nop()
        R("v", f_pb)
        R("y", mk_sdma([(pen_in_d, PENR[0:1, 0:128])]))

        def f_cc(g):
            g.collective_compute("AllReduce", ALU.add,
                                 replica_groups=[list(range(8))],
                                 ins=[pen_in_d], outs=[pen_out_d]).then_inc(CCS, 1)
            g.wait_ge(CCS, 1)
            return g.nop()
        R("g", f_cc)
        R("y", mk_sdma([(MISC[:, 17:18], pen_out_d.rearrange("o p -> p o"))], slow=True))

        def f_fac(v):
            v.tensor_scalar(MISC[:, 18:19], MISC[:, 17:18], 1.0, None, op0=ALU.add)
            v.drain()
            return v.nop()
        R("v", f_fac)

        for tt in range(NT):
            def f_cw1(v, tt=tt):
                cd = CW[:, tt * 8:tt * 8 + 8]
                v.reduce_max(MISC[:, 20:21], cd, axis=AX)
                v.drain()
                v.tensor_scalar(SCs[:, 0:8], cd, MISC[:, 20:21], None, op0=ALU.is_equal)
                v.drain()
                v.tensor_scalar(SCs[:, 8:16], SCs[:, 0:8], 1e30, None, op0=ALU.mult)
                v.drain()
                v.tensor_tensor(SCs[:, 16:24], cd, SCs[:, 8:16], op=ALU.subtract)
                v.drain()
                v.reduce_max(MISC[:, 21:22], SCs[:, 16:24], axis=AX)
                v.drain()
                v.tensor_scalar(SCs[:, 24:32], cd, MISC[:, 21:22], None, op0=ALU.is_equal)
                v.drain()
                v.tensor_tensor(MISC[:, 22:23], MISC[:, 20:21], MISC[:, 21:22], op=ALU.subtract)
                v.drain()
                return v.nop()
            R("v", f_cw1)

            def f_cw2(s):
                return s.activation(MISC[:, 23:24], MISC[:, 22:23], AF.Sigmoid,
                                    scale=MISC[:, 18:19])
            R("s", f_cw2)

            def f_cw3(v, tt=tt):
                v.tensor_scalar(MISC[:, 24:25], MISC[:, 23:24], -1.0, 1.0,
                                op0=ALU.mult, op1=ALU.add)
                v.drain()
                v.tensor_scalar(SCs[:, 32:40], SCs[:, 0:8], MISC[:, 23:24], None, op0=ALU.mult)
                v.drain()
                v.tensor_scalar(SCs[:, 40:48], SCs[:, 24:32], MISC[:, 24:25], None, op0=ALU.mult)
                v.drain()
                v.tensor_tensor(CW[:, tt * 8:tt * 8 + 8], SCs[:, 32:40],
                                SCs[:, 40:48], op=ALU.add)
                v.drain()
                return v.nop()
            R("v", f_cw3)
        emit_block(runs)
        ssc.close()
        g4.close()

        msc = contextlib.ExitStack()
        Y = msc.enter_context(nc.sbuf_tensor("Yacc", [128, 8 * 1024], F32))
        AT = msc.enter_context(nc.sbuf_tensor("ATm", [128, 16 * 1024], BF16))
        GUW = msc.enter_context(nc.sbuf_tensor("GUW", [128, 2 * 2048], BF16))
        DWN = msc.enter_context(nc.sbuf_tensor("DWN", [128, 2 * 2048], BF16))
        XNC = msc.enter_context(nc.sbuf_tensor("XNC", [128, 8 * 1024], BF16))
        CWTR = msc.enter_context(nc.sbuf_tensor("CWTR", [128, S], BF16))
        CWBE = msc.enter_context(nc.sbuf_tensor("CWBE", [128, 1024], BF16))
        SEL8 = msc.enter_context(nc.sbuf_tensor("SEL8", [128, 8 * 128], BF16))
        SLB = msc.enter_context(nc.sbuf_tensor("SLB", [128, 2048], F32))
        OSB = msc.enter_context(nc.sbuf_tensor("OSB", [128, 1024], F32))
        OTB = msc.enter_context(nc.sbuf_tensor("OTB", [128, 1024], F32))
        XTb = msc.enter_context(nc.sbuf_tensor("XTb", [128, 1024], F32))
        CWB8 = msc.enter_context(nc.sbuf_tensor("CWB8", [128, 8], BF16))
        EYEF = msc.enter_context(nc.sbuf_tensor("EYEF", [128, 128], F32))
        psc = contextlib.ExitStack()
        tPS = psc.enter_context(nc.psum_tensor("tPS", [128, 128], BF16))

        runs = []
        R = lambda k, f: runs.append((k, f))
        consume_run = []
        chunk_dmas = []
        NCH = 2 * NE * 24

        R("y", mk_sdma([(EYEF[:, :], eyef_in), (SEL8[:, :], sel8_in)]))

        def f_cz(v):
            v.memset(CWTR[:, :], 0.0)
            v.drain()
            return v.nop()
        R("v", f_cz)
        for tt in range(NT):
            def f_ct1(v, tt=tt):
                v.tensor_copy(CWB8[:, :], CW[:, tt * 8:tt * 8 + 8])
                v.drain()
                return v.nop()
            R("v", f_ct1)

            def f_ct2(t):
                return t.matmul(tPS[0:8, 0:128], CWB8[:, :], EYEB[:, :],
                                start=True, stop=True, is_transpose=True)
            R("t", f_ct2)

            def f_ct3(v, tt=tt):
                return v.tensor_copy(CWTR[0:8, tt * 128:(tt + 1) * 128], tPS[0:8, 0:128])
            R("v", f_ct3)

        P_moe = emit_block(runs) + len(runs)
        psc.close()
        mPS8 = [msc.enter_context(nc.psum_tensor(f"mPS8_{p}", [128, 512], F32)) for p in range(8)]

        NST = es.enter_context(nc.semaphore("NST"))
        NSL = es.enter_context(nc.semaphore("NSL"))
        NAU = es.enter_context(nc.semaphore("NAU"))
        NTD = es.enter_context(nc.semaphore("NTD"))
        NDV = es.enter_context(nc.semaphore("NDV"))
        NCB = es.enter_context(nc.semaphore("NCB"))
        NXN = es.enter_context(nc.semaphore("NXN"))
        NEP = es.enter_context(nc.semaphore("NEP"))

        sched = []
        for mb in range(2):
            for e in range(NE):
                sched.append((mb, e))
        chunk_marks = []
        gi = 0
        di = 0
        for mb, e in sched:
            for fc in range(16):
                chunk_marks.append(("G", gi + 1))
                chunk_dmas.append(8)
                gi += 1
            for hc in range(8):
                chunk_marks.append(("D", di + 1))
                chunk_dmas.append(4)
                di += 1
        assert len(chunk_marks) == NCH
        wd_cum.extend(np.cumsum(chunk_dmas).tolist())
        EP_STEPS = 7
        EP_MB = 8 * EP_STEPS

        with nc.Block() as blk:
            @blk.tensor
            def _(t):
                t.wait_ge(SER, P_moe)
                gi = 0
                di = 0
                ci = 0
                xn = 0
                for mb, e in sched:
                    dn_before = di
                    if e < 8:
                        if gi >= 2:
                            t.wait_ge(NAU, gi - 1)
                        t.wait_ge(NDV, dn_before)
                        if mb == 1 and e == 0:
                            t.wait_ge(NEP, EP_MB)
                        for sb in range(2):
                            ins = t.matmul(mPS8[sb][:, :], SEL8[:, e * 128:(e + 1) * 128],
                                           CWTR[:, mb * 1024 + sb * 512:mb * 1024 + sb * 512 + 512],
                                           start=True, stop=True)
                        ins.then_inc(NCB, 1)
                        xn += 1
                    for fc in range(16):
                        par = fc % 2
                        t.wait_ge(WD, 16 * wd_cum[min(ci + 1, NCH - 1)])
                        if gi >= 2:
                            t.wait_ge(NAU, gi - 1)
                        if fc == 0:
                            t.wait_ge(NDV, dn_before)
                            if e < 8:
                                t.wait_ge(NXN, xn)
                            if mb == 1 and e == 0:
                                t.wait_ge(NEP, EP_MB)
                        for gu in range(2):
                            for half in range(2):
                                ps = mPS8[4 * par + gu * 2 + half]
                                for c in range(8):
                                    if gu == 0 or e == 8:
                                        rhs = XNT[:, c * S + mb * 1024 + half * 512:
                                                  c * S + mb * 1024 + half * 512 + 512]
                                    else:
                                        rhs = XNC[:, c * 1024 + half * 512:c * 1024 + half * 512 + 512]
                                    ins = t.matmul(
                                        ps[:, :],
                                        GUW[:, (par * 2 + gu) * 1024 + c * 128:
                                             (par * 2 + gu) * 1024 + c * 128 + 128],
                                        rhs, start=(c == 0), stop=(c == 7))
                        ins.then_inc(NST, 1)
                        gi += 1
                        ci += 1
                    aulast = gi
                    for hc in range(8):
                        q = hc % 2
                        t.wait_ge(WD, 16 * wd_cum[min(ci + 1, NCH - 1)])
                        if hc == 0:
                            t.wait_ge(NAU, aulast)
                        if di >= 2:
                            t.wait_ge(NDV, di - 1)
                        for half in range(2):
                            ps = mPS8[2 * q + half]
                            for fcc in range(16):
                                ins = t.matmul(ps[:, :],
                                               DWN[:, q * 2048 + fcc * 128:q * 2048 + fcc * 128 + 128],
                                               AT[:, fcc * 1024 + half * 512:fcc * 1024 + half * 512 + 512],
                                               start=(fcc == 0), stop=(fcc == 15))
                        ins.then_inc(NTD, 1)
                        di += 1
                        ci += 1
                    if e == NE - 1:
                        for lt in range(8):
                            base_i = mb * EP_MB + lt * EP_STEPS
                            for chalf in range(2):
                                t.wait_ge(NEP, base_i + 2 + chalf * 2)
                                for i in range(4):
                                    c = chalf * 4 + i
                                    ins = t.matmul(mPS8[i][:, 0:128],
                                                   OSB[:, c * 128:(c + 1) * 128],
                                                   EYEF[:, :], start=True, stop=True)
                                ins.then_inc(NEP, 1)

            @blk.scalar
            def _(s):
                s.wait_ge(SER, P_moe)
                for gi in range(2 * NE * 16):
                    par = gi % 2
                    s.wait_ge(NST, gi + 1)
                    if gi >= 2:
                        s.wait_ge(NAU, gi - 1)
                    s.activation(SLB[:, par * 1024:par * 1024 + 512], mPS8[4 * par][:, :], AF.Silu)
                    s.activation(SLB[:, par * 1024 + 512:par * 1024 + 1024],
                                 mPS8[4 * par + 1][:, :], AF.Silu).then_inc(NSL, 1)

            @blk.vector
            def _(v):
                v.wait_ge(SER, P_moe)
                gi = 0
                di = 0
                cb = 0
                for mb, e in sched:
                    if e < 8:
                        cb += 1
                        v.wait_ge(NCB, cb)
                        v.tensor_copy(CWBE[:, 0:512], mPS8[0][:, :])
                        v.tensor_copy(CWBE[:, 512:1024], mPS8[1][:, :])
                        v.drain()
                        for c in range(8):
                            ins = v.tensor_tensor(XNC[:, c * 1024:(c + 1) * 1024],
                                                  XNT[:, c * S + mb * 1024:c * S + mb * 1024 + 1024],
                                                  CWBE[:, :], op=ALU.mult)
                        ins.then_inc(NXN, 1)
                    for fc in range(16):
                        par = fc % 2
                        v.wait_ge(NSL, gi + 1)
                        for half in range(2):
                            ins = v.tensor_tensor(
                                AT[:, fc * 1024 + half * 512:fc * 1024 + half * 512 + 512],
                                SLB[:, par * 1024 + half * 512:par * 1024 + half * 512 + 512],
                                mPS8[4 * par + 2 + half][:, :], op=ALU.mult)
                        ins.then_inc(NAU, 1)
                        gi += 1
                    v.drain()
                    for hc in range(8):
                        q = hc % 2
                        v.wait_ge(NTD, di + 1)
                        for half in range(2):
                            dst = Y[:, hc * 1024 + half * 512:hc * 1024 + half * 512 + 512]
                            if e == 0:
                                ins = v.tensor_copy(dst, mPS8[2 * q + half][:, :])
                            else:
                                ins = v.tensor_tensor(dst, dst, mPS8[2 * q + half][:, :], op=ALU.add)
                        ins.then_inc(NDV, 1)
                        di += 1
                    if e == NE - 1:
                        for lt in range(8):
                            tt = mb * 8 + lt
                            base_i = mb * EP_MB + lt * EP_STEPS
                            v.wait_ge(NEP, base_i + 1)
                            for c in range(8):
                                v.tensor_copy(OSB[:, c * 128:(c + 1) * 128],
                                              ATT[:, c * S + tt * 128:c * S + (tt + 1) * 128])
                            v.drain()
                            for c in range(8):
                                v.tensor_tensor(OSB[:, c * 128:(c + 1) * 128],
                                                OSB[:, c * 128:(c + 1) * 128],
                                                Y[:, c * 1024 + lt * 128:c * 1024 + (lt + 1) * 128],
                                                op=ALU.add)
                            v.drain()
                            v.nop().then_inc(NEP, 1)
                            for chalf in range(2):
                                v.wait_ge(NEP, base_i + 3 + chalf * 2)
                                for i in range(4):
                                    c = chalf * 4 + i
                                    ins = v.tensor_tensor(OTB[:, c * 128:(c + 1) * 128],
                                                          mPS8[i][:, 0:128],
                                                          XTb[:, c * 128:(c + 1) * 128], op=ALU.add)
                                ins.then_inc(NEP, 1)

            @blk.sync
            def _(y):
                y.wait_ge(SER, P_moe)
                for mb in range(2):
                    for lt in range(8):
                        tt = mb * 8 + lt
                        base_i = mb * EP_MB + lt * EP_STEPS
                        y.wait_ge(NEP, base_i)
                        y.dma_start(out=XTb[:, :],
                                    in_=x_in[tt * 128:(tt + 1) * 128, :]).then_inc(LS, 16)
                        ls_cnt[0] += 16
                        y.wait_ge(LS, ls_cnt[0])
                        y.nop().then_inc(NEP, 1)
                        y.wait_ge(NEP, base_i + 6)
                        y.dma_start(out=out_d[tt * 128:(tt + 1) * 128, :],
                                    in_=OTB[:, :]).then_inc(LS, 16)
                        ls_cnt[0] += 16
                        y.wait_ge(LS, ls_cnt[0])
                        y.nop().then_inc(NEP, 1)

            @blk.gpsimd
            def _(g):
                g.wait_ge(SER, P_moe)
                g.wait_ge(PCD, 16 * 3 * NE)
                ci = 0
                for mb, e in sched:
                    for fc in range(16):
                        par = fc % 2
                        if ci >= 2:
                            kind, val = chunk_marks[ci - 2]
                            g.wait_ge(NST if kind == "G" else NTD, val)
                        for pr in range(4):
                            g.dma_start(out=GUW[pr * 32:pr * 32 + 32, par * 2048:par * 2048 + 1024],
                                        in_=gB_d[e * 16 + fc][pr * 32:pr * 32 + 32]).then_inc(WD, 16)
                            g.dma_start(out=GUW[pr * 32:pr * 32 + 32, par * 2048 + 1024:par * 2048 + 2048],
                                        in_=uB_d[e * 16 + fc][pr * 32:pr * 32 + 32]).then_inc(WD, 16)
                        ci += 1
                    for hc in range(8):
                        q = hc % 2
                        if ci >= 2:
                            kind, val = chunk_marks[ci - 2]
                            g.wait_ge(NST if kind == "G" else NTD, val)
                        for pr in range(4):
                            g.dma_start(out=DWN[pr * 32:pr * 32 + 32, q * 2048:q * 2048 + 2048],
                                        in_=dB_d[e * 8 + hc][pr * 32:pr * 32 + 32]).then_inc(WD, 16)
                        ci += 1
        msc.close()
    return nc


PERM_W = _perm_w()
PERM_XW = _perm_xw()


def kernel(**inputs):
    xfull = np.asarray(inputs["L0_attn_x"])
    f32 = np.float32
    asc = np.ascontiguousarray
    wqT = asc(np.asarray(inputs["L0_attn_Wq"]).T.astype(f32))
    wkT = asc(np.asarray(inputs["L0_attn_Wk"]).T.astype(f32))
    wvT = asc(np.asarray(inputs["L0_attn_Wv"]).T.astype(f32))
    woT = asc(np.asarray(inputs["L0_attn_Wo"]).T.astype(f32))
    bq = np.asarray(inputs["L0_attn_bq"]).astype(f32)
    bk = np.asarray(inputs["L0_attn_bk"]).astype(f32)
    bv = np.asarray(inputs["L0_attn_bv"]).astype(f32)
    bo = np.asarray(inputs["L0_attn_bo"]).astype(f32)
    brow = asc(np.stack([bq, bk, bv, bo]))
    bo_f = asc(bo.reshape(8, 128).T)
    whhT = asc(np.asarray(inputs["L0_ffn_gru_w_hh"]).T.astype(f32)[:, PERM_W])
    wihT = asc(np.asarray(inputs["L0_ffn_gru_w_ih"]).T.astype(f32)[:, PERM_XW])
    ctxwT = asc(np.asarray(inputs["L0_ffn_ctx_w"]).T.astype(f32))
    ctxbf = asc(np.asarray(inputs["L0_ffn_ctx_b"]).astype(f32).reshape(8, 128).T)
    exprwT = asc(np.asarray(inputs["L0_ffn_expr_w"]).T.astype(f32))
    g9 = np.concatenate([np.asarray(inputs["L0_ffn_gate_w"]),
                         np.asarray(inputs["L0_ffn_shared_gate_w"])[None]],
                        axis=0).transpose(0, 2, 1).astype(f32)
    u9 = np.concatenate([np.asarray(inputs["L0_ffn_up_w"]),
                         np.asarray(inputs["L0_ffn_shared_up_w"])[None]],
                        axis=0).transpose(0, 2, 1).astype(f32)
    d9 = np.concatenate([np.asarray(inputs["L0_ffn_down_w"]),
                         np.asarray(inputs["L0_ffn_shared_down_w"])[None]],
                        axis=0).transpose(0, 2, 1).astype(f32)
    gL = asc(g9.reshape(NE, 8, 128, 16, 128).transpose(0, 3, 2, 1, 4).reshape(NE * 16, 128, 1024))
    uL = asc(u9.reshape(NE, 8, 128, 16, 128).transpose(0, 3, 2, 1, 4).reshape(NE * 16, 128, 1024))
    dL = asc(d9.reshape(NE, 16, 128, 8, 128).transpose(0, 3, 2, 1, 4).reshape(NE * 8, 128, 2048))

    inv = 1.0 / (10000.0 ** (np.arange(0, 64, 2, dtype=np.float64) / 64.0))
    ang = np.arange(S, dtype=np.float64)[:, None] * inv[None, :]
    cosE = asc(np.tile(np.cos(ang).astype(f32), (1, 8)))
    sinE = asc(np.tile(np.sin(ang).astype(f32), (1, 8)))
    eyeb = asc(np.eye(128, dtype=f32).astype(BF))
    eyef = asc(np.eye(128, dtype=f32))
    ku = np.arange(128)
    umask = asc(np.where(ku[:, None] > ku[None, :], np.float32(-1e9),
                         np.float32(0.0)).astype(BF))
    sel8 = np.zeros((128, 8 * 128), dtype=np.float32)
    for e in range(8):
        sel8[e, e * 128:(e + 1) * 128] = 1.0
    sel8 = asc(sel8.astype(BF))

    nc = build()
    in_maps = []
    for i in range(8):
        in_maps.append({
            "x": asc(xfull[i].astype(f32)),
            "wqT": wqT, "wkT": wkT, "wvT": wvT, "woT": woT,
            "brow": brow, "bo_f": bo_f,
            "whhT": whhT, "wihT": wihT,
            "ctxwT": ctxwT, "ctxbf": ctxbf,
            "exprwT": exprwT,
            "g9L": gL, "u9L": uL, "d9L": dL,
            "cosE": cosE, "sinE": sinE,
            "eyeb": eyeb, "eyef": eyef, "umask": umask, "sel8": sel8,
        })
    res = run_bass_kernel_spmd(nc, in_maps, core_ids=list(range(8)))
    out = np.stack([np.asarray(r["y"]) for r in res.results], axis=0)
    return out


if __name__ == "__main__":
    nc = build()
    print("built ok")


# version 06
# speedup vs baseline: 1.0280x; 1.0280x over previous
import contextlib
import numpy as np
import ml_dtypes
import concourse.bass as bass
import concourse.mybir as mybir
from concourse.bass_utils import run_bass_kernel_spmd

F32 = mybir.dt.float32
BF16 = mybir.dt.bfloat16
AF = mybir.ActivationFunctionType
ALU = mybir.AluOpType
AX = mybir.AxisListType.X
BF = ml_dtypes.bfloat16

S, D = 2048, 1024
NT = S // 128
G3 = 3072
TP1 = S + 1
NB = S // 2
FF = 2048
NE = 9


def _perm_w():
    idx = []
    for g in range(4):
        idx.extend(range(256 * g, 256 * (g + 1)))
        idx.extend(range(1024 + 256 * g, 1024 + 256 * (g + 1)))
    idx.extend(range(2048, 3072))
    return np.array(idx)


def _perm_xw():
    idx = []
    for g in range(4):
        idx.extend(range(256 * g, 256 * (g + 1)))
        idx.extend(range(1024 + 256 * g, 1024 + 256 * (g + 1)))
        idx.extend(range(2048 + 256 * g, 2048 + 256 * (g + 1)))
    return np.array(idx)


def build():
    nc = bass.Bass("TRN2", target_bir_lowering=False, debug=False)
    dt = nc.dram_tensor
    x_in = dt("x", [S, D], F32, kind="ExternalInput").ap()
    wqT_in = dt("wqT", [D, D], F32, kind="ExternalInput").ap()
    wkT_in = dt("wkT", [D, D], F32, kind="ExternalInput").ap()
    wvT_in = dt("wvT", [D, D], F32, kind="ExternalInput").ap()
    woT_in = dt("woT", [D, D], F32, kind="ExternalInput").ap()
    brow_in = dt("brow", [4, D], F32, kind="ExternalInput").ap()
    bo_f_in = dt("bo_f", [128, 8], F32, kind="ExternalInput").ap()
    whhT_in = dt("whhT", [D, G3], F32, kind="ExternalInput").ap()
    wihT_in = dt("wihT", [D, G3], F32, kind="ExternalInput").ap()
    ctxwT_in = dt("ctxwT", [D, D], F32, kind="ExternalInput").ap()
    ctxbf_in = dt("ctxbf", [128, 8], F32, kind="ExternalInput").ap()
    exprwT_in = dt("exprwT", [D, D], F32, kind="ExternalInput").ap()
    gL_in = dt("g9L", [NE * 16, 128, 1024], F32, kind="ExternalInput").ap()
    uL_in = dt("u9L", [NE * 16, 128, 1024], F32, kind="ExternalInput").ap()
    dL_in = dt("d9L", [NE * 8, 128, 2048], F32, kind="ExternalInput").ap()
    cosE_in = dt("cosE", [S, 256], F32, kind="ExternalInput").ap()
    sinE_in = dt("sinE", [S, 256], F32, kind="ExternalInput").ap()
    eyeb_in = dt("eyeb", [128, 128], BF16, kind="ExternalInput").ap()
    eyef_in = dt("eyef", [128, 128], F32, kind="ExternalInput").ap()
    umask_in = dt("umask", [128, 128], BF16, kind="ExternalInput").ap()
    sel8_in = dt("sel8", [128, 8 * 128], BF16, kind="ExternalInput").ap()
    out_d = dt("y", [S, D], F32, kind="ExternalOutput").ap()
    xw_d = dt("xw_scr", [S + 2, G3], BF16, kind="Internal").ap()
    gB_d = dt("gB", [NE * 16, 128, 1024], BF16, kind="Internal").ap()
    uB_d = dt("uB", [NE * 16, 128, 1024], BF16, kind="Internal").ap()
    dB_d = dt("dB", [NE * 8, 128, 2048], BF16, kind="Internal").ap()
    h0tmp_d = dt("h0tmp", [128, 8], F32, kind="Internal").ap()
    pen_in_d = dt("pen_in", [1, 128], F32, kind="Internal").ap()
    pen_out_d = dt("pen_out", [1, 128], F32, kind="Internal").ap()

    es = contextlib.ExitStack()
    with es:
        XNT = es.enter_context(nc.sbuf_tensor("XNT", [128, 8 * S], BF16))
        ATT = es.enter_context(nc.sbuf_tensor("ATT", [128, 8 * S], BF16))
        EYEB = es.enter_context(nc.sbuf_tensor("EYEB", [128, 128], BF16))
        UMASK = es.enter_context(nc.sbuf_tensor("UMASK", [128, 128], BF16))
        SELRT = es.enter_context(nc.sbuf_tensor("SELRT", [128, 128], BF16))
        SEL64 = es.enter_context(nc.sbuf_tensor("SEL64", [128, 128], BF16))
        E32 = es.enter_context(nc.sbuf_tensor("E32c", [128, 4], BF16))
        ONE1 = es.enter_context(nc.sbuf_tensor("ONE1", [128, 1], F32))
        ONEB = es.enter_context(nc.sbuf_tensor("ONEB", [128, 1], BF16))
        MISC = es.enter_context(nc.sbuf_tensor("MISC", [128, 32], F32))
        CW = es.enter_context(nc.sbuf_tensor("CWtm", [128, NT * 8], F32))
        PEN = es.enter_context(nc.sbuf_tensor("PENt", [128, NT], F32))
        H0F = es.enter_context(nc.sbuf_tensor("H0F", [128, 8], F32))
        H0S = es.enter_context(nc.sbuf_tensor("H0S", [128, 256], F32))
        BOF = es.enter_context(nc.sbuf_tensor("BOF", [128, 8], F32))
        SER = es.enter_context(nc.semaphore("SER"))
        LD = es.enter_context(nc.semaphore("LDd"))
        LS = es.enter_context(nc.semaphore("LSd"))
        SM = es.enter_context(nc.semaphore("SMg"))
        STp = es.enter_context(nc.semaphore("STg"))
        SA = es.enter_context(nc.semaphore("SAg"))
        SV = es.enter_context(nc.semaphore("SVg"))
        SD0 = es.enter_context(nc.semaphore("SD0"))
        SD1 = es.enter_context(nc.semaphore("SD1"))
        SGp = es.enter_context(nc.semaphore("SGg"))
        WD = es.enter_context(nc.semaphore("WDd"))
        CCS = es.enter_context(nc.semaphore("CCS"))
        PCD = es.enter_context(nc.semaphore("PCD"))

        ld_cnt = [0]
        ls_cnt = [0]
        ser_base = [0]
        wd_cum = []

        def emit_block(runs, extra_gpsimd=None):
            base = ser_base[0]
            with nc.Block() as blk:
                for ekey, attr in [("v", "vector"), ("s", "scalar"), ("t", "tensor"),
                                   ("g", "gpsimd"), ("y", "sync")]:
                    if ekey == "g" and extra_gpsimd is not None:
                        blk.gpsimd(lambda g: extra_gpsimd(g, base))
                        continue

                    def cb(eng, ekey=ekey):
                        for i, (k, fn) in enumerate(runs):
                            if k != ekey:
                                continue
                            eng.wait_ge(SER, base + i)
                            fn(eng).then_inc(SER, 1)
                    getattr(blk, attr)(cb)
            ser_base[0] = base + len(runs)
            return base

        def mk_gdma(pairs, slow=False):
            def f(g):
                for dst, src in pairs:
                    g.dma_start(out=dst, in_=src,
                                allow_slow_non_contiguous=slow).then_inc(LD, 16)
                    ld_cnt[0] += 16
                g.wait_ge(LD, ld_cnt[0])
                return g.nop()
            return f

        def mk_sdma(pairs, slow=False):
            def f(y):
                for dst, src in pairs:
                    y.dma_start(out=dst, in_=src,
                                allow_slow_non_contiguous=slow).then_inc(LS, 16)
                    ls_cnt[0] += 16
                y.wait_ge(LS, ls_cnt[0])
                return y.nop()
            return f

        pre = contextlib.ExitStack()
        SC1 = pre.enter_context(nc.sbuf_tensor("SC1", [128, D], F32))
        SC3 = pre.enter_context(nc.sbuf_tensor("SC3", [128, 256], F32))
        SC4 = pre.enter_context(nc.sbuf_tensor("SC4", [128, 256], F32))
        SCB = pre.enter_context(nc.sbuf_tensor("SCBa", [128, 1024], BF16))
        STT = pre.enter_context(nc.sbuf_tensor("STT", [128, 8], F32))
        WSTG = pre.enter_context(nc.sbuf_tensor("WSTG", [128, 8 * D], BF16))
        QT = pre.enter_context(nc.sbuf_tensor("QTf", [128, 4 * S], BF16))
        KT = pre.enter_context(nc.sbuf_tensor("KTf", [128, 4 * S], BF16))
        VA = pre.enter_context(nc.sbuf_tensor("VAf", [128, NT * 16 * 65], BF16))
        CCT = pre.enter_context(nc.sbuf_tensor("CCT", [128, 8 * S], BF16))
        PTB = pre.enter_context(nc.sbuf_tensor("PTBb", [128, 1024], BF16))
        CTRG = pre.enter_context(nc.sbuf_tensor("CTRG", [128, 512], F32))
        SPS = pre.enter_context(nc.psum_tensor("SPS", [128, 1024], F32))
        TPB = pre.enter_context(nc.psum_tensor("TPB", [128, 1024], BF16))
        AVP = pre.enter_context(nc.psum_tensor("AVP", [128, 2048], F32))

        runs = []
        R = lambda k, f: runs.append((k, f))

        R("g", mk_gdma([(EYEB[:, :], eyeb_in), (UMASK[:, :], umask_in),
                        (BOF[:, :], bo_f_in)]))

        def f_init(v):
            v.memset(SELRT[:, :], 0.0)
            v.memset(SEL64[:, :], 0.0)
            v.memset(E32[:, :], 0.0)
            v.memset(PTB[:, :], 0.0)
            v.drain()
            v.memset(SELRT[0:1, :], 1.0)
            v.memset(SEL64[64:65, :], 1.0)
            for g in range(4):
                v.memset(E32[32 * g:32 * g + 1, g:g + 1], 1.0)
            v.memset(ONE1[:, :], 1.0)
            v.memset(ONEB[:, :], 1.0)
            v.memset(H0S[:, :], 0.0)
            v.memset(VA[:, :], 1.0)
            v.drain()
            return v.nop()
        R("v", f_init)

        for tt in range(NT):
            R("g", mk_gdma([(SC1[:, :], x_in[tt * 128:(tt + 1) * 128, :])]))

            def f1(v):
                v.reduce_sum(STT[:, 0:1], SC1[:, :], axis=AX)
                v.drain()
                v.tensor_scalar_mul(STT[:, 1:2], STT[:, 0:1], 1.0 / D)
                v.drain()
                v.tensor_scalar(SC1[:, :], SC1[:, :], STT[:, 1:2], None, op0=ALU.subtract)
                v.drain()
                return v.nop()
            R("v", f1)

            def f2(s):
                return s.activation(SCB[:, 0:D], SC1[:, :], AF.Square, accum_out=STT[:, 2:3])
            R("s", f2)

            def f2b(v):
                v.tensor_scalar(STT[:, 5:6], STT[:, 2:3], 1.0 / D, 1e-5,
                                op0=ALU.mult, op1=ALU.add)
                v.drain()
                return v.nop()
            R("v", f2b)

            def f2c(s):
                return s.activation(STT[:, 3:4], STT[:, 5:6], AF.Sqrt)
            R("s", f2c)

            def f3(v):
                v.reciprocal(STT[:, 4:5], STT[:, 3:4])
                v.drain()
                v.tensor_scalar(SC1[:, :], SC1[:, :], STT[:, 4:5], None, op0=ALU.mult)
                v.drain()
                v.tensor_copy(SCB[:, 0:D], SC1[:, :])
                v.drain()
                return v.nop()
            R("v", f3)

            def f4(t):
                for i in range(8):
                    ins = t.matmul(TPB[:, i * 128:(i + 1) * 128],
                                   SCB[:, i * 128:(i + 1) * 128], EYEB[:, :],
                                   start=True, stop=True, is_transpose=True)
                return ins
            R("t", f4)

            def f5(v, tt=tt):
                for i in range(8):
                    ins = v.tensor_copy(XNT[:, i * S + tt * 128:i * S + (tt + 1) * 128],
                                        TPB[:, i * 128:(i + 1) * 128])
                return ins
            R("v", f5)

        R("g", mk_gdma([(WSTG[:, :].rearrange("p (c j) -> p c j", c=8),
                        wvT_in.rearrange("(c p) j -> p c j", p=128)),
                       (PTB[0:1, 0:D], brow_in[2:3, :])]))
        for tt in range(NT):
            def fv_t(t, tt=tt):
                for jh in range(2):
                    for c in range(8):
                        t.matmul(SPS[:, jh * 512:jh * 512 + 512],
                                 XNT[:, c * S + tt * 128:c * S + (tt + 1) * 128],
                                 WSTG[:, c * D + jh * 512:c * D + jh * 512 + 512],
                                 start=(c == 0), stop=False)
                    ins = t.matmul(SPS[:, jh * 512:jh * 512 + 512], SELRT[:, :],
                                   PTB[:, jh * 512:jh * 512 + 512],
                                   start=False, stop=True)
                return ins
            R("t", fv_t)

            def fv_v(v, tt=tt):
                for jh in range(2):
                    base = (tt * 16 + jh * 8) * 65
                    ins = v.tensor_copy(
                        VA[:, base:base + 8 * 65].rearrange("p (h e) -> p h e", h=8)[:, :, 0:64],
                        SPS[:, jh * 512:jh * 512 + 512].rearrange("p (h e) -> p h e", h=8))
                return ins
            R("v", fv_v)

        for hg in range(2):
            R("g", mk_gdma([
                (WSTG[:, 0:4096].rearrange("p (c j) -> p c j", c=8),
                 wqT_in[:, hg * 512:(hg + 1) * 512].rearrange("(c p) j -> p c j", p=128)),
                (WSTG[:, 4096:8192].rearrange("p (c j) -> p c j", c=8),
                 wkT_in[:, hg * 512:(hg + 1) * 512].rearrange("(c p) j -> p c j", p=128)),
                (PTB[0:1, 0:512], brow_in[0:1, hg * 512:(hg + 1) * 512]),
                (PTB[0:1, 512:1024], brow_in[1:2, hg * 512:(hg + 1) * 512]),
            ]))
            for tt in range(NT):
                R("y", mk_sdma([(CTRG[:, 0:256], cosE_in[tt * 128:(tt + 1) * 128, :]),
                                (CTRG[:, 256:512], sinE_in[tt * 128:(tt + 1) * 128, :])]))

                def fp_t(t, tt=tt):
                    for qk in range(2):
                        for c in range(8):
                            t.matmul(SPS[:, qk * 512:qk * 512 + 512],
                                     XNT[:, c * S + tt * 128:c * S + (tt + 1) * 128],
                                     WSTG[:, qk * 4096 + c * 512:qk * 4096 + c * 512 + 512],
                                     start=(c == 0), stop=False)
                        ins = t.matmul(SPS[:, qk * 512:qk * 512 + 512], SELRT[:, :],
                                       PTB[:, qk * 512:qk * 512 + 512],
                                       start=False, stop=True)
                    return ins
                R("t", fp_t)

                for qk in range(2):
                    scl = 0.125 if qk == 0 else 1.0

                    def fp_rope(v, qk=qk, scl=scl):
                        v.tensor_copy(SC1[:, 0:512], SPS[:, qk * 512:qk * 512 + 512])
                        v.drain()
                        QE = SC1[:, 0:512].rearrange("p (h d) -> p h d", h=8)[:, :, 0:64:2]
                        QO = SC1[:, 0:512].rearrange("p (h d) -> p h d", h=8)[:, :, 1:64:2]
                        CO = CTRG[:, 0:256].rearrange("p (h i) -> p h i", h=8)
                        SI = CTRG[:, 256:512].rearrange("p (h i) -> p h i", h=8)
                        S3 = SC3[:, :].rearrange("p (h i) -> p h i", h=8)
                        S4 = SC4[:, :].rearrange("p (h i) -> p h i", h=8)
                        RE = SC1[:, 512:1024].rearrange("p (h d) -> p h d", h=8)[:, :, 0:64:2]
                        RO = SC1[:, 512:1024].rearrange("p (h d) -> p h d", h=8)[:, :, 1:64:2]
                        v.tensor_tensor(S3, QE, CO, op=ALU.mult)
                        v.drain()
                        v.tensor_tensor(S4, QO, SI, op=ALU.mult)
                        v.drain()
                        v.tensor_tensor(RE, S3, S4, op=ALU.subtract)
                        v.drain()
                        v.tensor_tensor(S3, QE, SI, op=ALU.mult)
                        v.drain()
                        v.tensor_tensor(S4, QO, CO, op=ALU.mult)
                        v.drain()
                        v.tensor_tensor(RO, S3, S4, op=ALU.add)
                        v.drain()
                        v.tensor_scalar_mul(SCB[:, 0:512], SC1[:, 512:1024], scl)
                        v.drain()
                        return v.nop()
                    R("v", fp_rope)

                    def fp_tr(t, qk=qk):
                        for jj in range(4):
                            o0 = qk * 512 + jj * 128
                            ins = t.matmul(TPB[:, o0:o0 + 128],
                                           SCB[:, jj * 128:(jj + 1) * 128], EYEB[:, :],
                                           start=True, stop=True, is_transpose=True)
                        return ins
                    R("t", fp_tr)

                    def fp_ev(v, tt=tt, qk=qk):
                        DSTQK = QT if qk == 0 else KT
                        for jj in range(4):
                            o0 = qk * 512 + jj * 128
                            ins = v.tensor_copy(DSTQK[:, jj * S + tt * 128:jj * S + (tt + 1) * 128],
                                                TPB[:, o0:o0 + 128])
                        return ins
                    R("v", fp_ev)

            for hl in range(8):
                h = hg * 8 + hl
                p0 = (hl % 2) * 64
                cj = hl // 2
                ccj = hg * 4 + cj
                for kt in range(NT):
                    q0 = kt * 128
                    for b2 in range(q0 // 1024, 2):
                        lo = max(q0, b2 * 1024)
                        hi = (b2 + 1) * 1024
                        rel = b2 * 1024

                        def fs_t(t, p0=p0, cj=cj, kt=kt, q0=q0, lo=lo, hi=hi, rel=rel):
                            kl = KT[p0:p0 + 64, cj * S + kt * 128:cj * S + (kt + 1) * 128]
                            c0 = lo
                            ins = None
                            if lo == q0:
                                t.matmul(SPS[:, q0 - rel:q0 - rel + 128], kl,
                                         QT[p0:p0 + 64, cj * S + q0:cj * S + q0 + 128],
                                         start=True, stop=False)
                                ins = t.matmul(SPS[:, q0 - rel:q0 - rel + 128],
                                               EYEB[:, :], UMASK[:, :],
                                               start=False, stop=True)
                                c0 = q0 + 128
                            while c0 < hi:
                                c1 = min(((c0 - rel) // 512 + 1) * 512 + rel, hi)
                                ins = t.matmul(SPS[:, c0 - rel:c1 - rel], kl,
                                               QT[p0:p0 + 64, cj * S + c0:cj * S + c1],
                                               start=True, stop=True)
                                c0 = c1
                            return ins
                        R("t", fs_t)

                        if lo > b2 * 1024:
                            def fs_pad(v, lo=lo, rel=rel):
                                v.memset(PTB[:, 0:lo - rel], 0.0)
                                v.drain()
                                return v.nop()
                            R("v", fs_pad)

                        def fs_exp(s, lo=lo, hi=hi, rel=rel):
                            return s.activation(PTB[:, lo - rel:1024],
                                                SPS[:, lo - rel:hi - rel], AF.Exp)
                        R("s", fs_exp)

                        def fs_av(t, h=h, kt=kt, b2=b2):
                            for sb in range(2):
                                qs = b2 * 1024 + sb * 512
                                ins = t.matmul(AVP[0:65, qs:qs + 512],
                                               VA[:, (kt * 16 + h) * 65:(kt * 16 + h) * 65 + 65],
                                               PTB[:, sb * 512:sb * 512 + 512],
                                               start=(kt == 0), stop=(kt == qs // 128 + 3))
                            return ins
                        R("t", fs_av)

                for b2 in range(2):
                    def fn_rec(v, b2=b2):
                        v.reciprocal(SC1[64:65, 0:1024], AVP[64:65, b2 * 1024:b2 * 1024 + 1024])
                        v.drain()
                        v.tensor_copy(PTB[64:65, 0:1024], SC1[64:65, 0:1024])
                        v.drain()
                        return v.nop()
                    R("v", fn_rec)

                    def fn_bc(t):
                        for sb in range(2):
                            ins = t.matmul(SPS[:, sb * 512:(sb + 1) * 512], SEL64[:, :],
                                           PTB[:, sb * 512:(sb + 1) * 512], start=True, stop=True)
                        return ins
                    R("t", fn_bc)

                    def fn_out(v, p0=p0, ccj=ccj, b2=b2):
                        v.tensor_copy(SC1[:, 0:1024], SPS[:, 0:1024])
                        v.drain()
                        if p0 == 0:
                            v.tensor_tensor(CCT[0:64, ccj * S + b2 * 1024:ccj * S + b2 * 1024 + 1024],
                                            AVP[0:64, b2 * 1024:b2 * 1024 + 1024],
                                            SC1[0:64, 0:1024], op=ALU.mult)
                        else:
                            v.tensor_tensor(SCB[0:64, 0:1024],
                                            AVP[0:64, b2 * 1024:b2 * 1024 + 1024],
                                            SC1[0:64, 0:1024], op=ALU.mult)
                        v.drain()
                        return v.nop()
                    R("v", fn_out)
                    if p0 != 0:
                        R("y", mk_sdma([(CCT[64:128, ccj * S + b2 * 1024:ccj * S + b2 * 1024 + 1024],
                                         SCB[0:64, 0:1024])]))

        R("g", mk_gdma([(WSTG[:, :].rearrange("p (c j) -> p c j", c=8),
                        woT_in.rearrange("(c p) j -> p c j", p=128))]))
        for jc in range(8):
            for tsp in range(2):
                def fo_t(t, jc=jc, tsp=tsp):
                    for half in range(2):
                        t0 = (tsp * 2 + half) * 512
                        for c in range(8):
                            ins = t.matmul(SPS[:, half * 512:half * 512 + 512],
                                           WSTG[:, c * D + jc * 128:c * D + jc * 128 + 128],
                                           CCT[:, c * S + t0:c * S + t0 + 512],
                                           start=(c == 0), stop=(c == 7))
                    return ins
                R("t", fo_t)

                def fo_v(v, jc=jc, tsp=tsp):
                    for half in range(2):
                        t0 = (tsp * 2 + half) * 512
                        ins = v.tensor_scalar(ATT[:, jc * S + t0:jc * S + t0 + 512],
                                              SPS[:, half * 512:half * 512 + 512],
                                              BOF[:, jc:jc + 1], None, op0=ALU.add)
                    return ins
                R("v", fo_v)

        emit_block(runs)
        pre.close()

        g4 = contextlib.ExitStack()
        HST = g4.enter_context(nc.sbuf_tensor("HST", [128, 8 * TP1], BF16))

        xsc = contextlib.ExitStack()
        WIH = xsc.enter_context(nc.sbuf_tensor("WIH", [128, 8 * G3], BF16))
        XSC = xsc.enter_context(nc.sbuf_tensor("XSC", [128, G3], BF16))
        CXB = xsc.enter_context(nc.sbuf_tensor("CXBx", [128, 8], F32))
        xPSa = xsc.enter_context(nc.psum_tensor("xPSa", [128, 512], F32))
        xPSb = xsc.enter_context(nc.psum_tensor("xPSb", [128, 512], F32))

        runs = []
        R = lambda k, f: runs.append((k, f))
        R("g", mk_gdma([(WIH[:, :].rearrange("p (c j) -> p c j", c=8),
                        wihT_in.rearrange("(c p) j -> p c j", p=128)),
                       (CXB[:, :], ctxbf_in)]))
        for tt in range(NT):
            for spp in range(3):
                def fx_t(t, tt=tt, spp=spp):
                    for half in range(2):
                        sl = spp * 2 + half
                        ps = xPSa if half == 0 else xPSb
                        for c in range(8):
                            ins = t.matmul(ps[:, :],
                                           XNT[:, c * S + tt * 128:c * S + (tt + 1) * 128],
                                           WIH[:, c * G3 + sl * 512:c * G3 + sl * 512 + 512],
                                           start=(c == 0), stop=(c == 7))
                    return ins
                R("t", fx_t)

                def fx_v(v, spp=spp):
                    for half in range(2):
                        sl = spp * 2 + half
                        ps = xPSa if half == 0 else xPSb
                        ins = v.tensor_copy(XSC[:, sl * 512:sl * 512 + 512], ps[:, :])
                    return ins
                R("v", fx_v)
            R("y", mk_sdma([(xw_d[tt * 128:(tt + 1) * 128, :], XSC[:, :])]))

        def fz(v):
            v.memset(XSC[:, :], 0.0)
            v.drain()
            return v.nop()
        R("v", fz)
        R("y", mk_sdma([(xw_d[S:S + 1, :], XSC[0:1, :]),
                        (xw_d[S + 1:S + 2, :], XSC[1:2, :])]))

        R("g", mk_gdma([(WIH[:, 0:8 * D].rearrange("p (c j) -> p c j", c=8),
                        ctxwT_in.rearrange("(c p) j -> p c j", p=128))]))

        def fh_v1(v):
            for c in range(8):
                v.reduce_sum(MISC[:, c:c + 1], XNT[:, c * S:(c + 1) * S], axis=AX)
                v.drain()
            v.tensor_scalar_mul(MISC[:, 0:8], MISC[:, 0:8], 1.0 / S)
            v.drain()
            v.tensor_copy(XSC[:, 0:8], MISC[:, 0:8])
            v.drain()
            return v.nop()
        R("v", fh_v1)

        def fh_t(t):
            for j in range(8):
                for k in range(8):
                    ins = t.matmul(xPSa[:, j:j + 1],
                                   WIH[:, k * D + j * 128:k * D + j * 128 + 128],
                                   XSC[:, k:k + 1], start=(k == 0), stop=(k == 7))
            return ins
        R("t", fh_t)

        def fh_v2(v):
            return v.tensor_tensor(H0F[:, :], xPSa[:, 0:8], CXB[:, :], op=ALU.add)
        R("v", fh_v2)
        R("y", mk_sdma([(h0tmp_d, H0F[:, :])]))
        R("g", mk_gdma([(H0S[0:128:32, :].rearrange("g (h f) -> g h f", h=2),
                         h0tmp_d.rearrange("p (g h) -> g h p", g=4))], slow=True))

        def f_pcast(g):
            for e in range(NE):
                g.dma_start(out=gB_d[e * 16:(e + 1) * 16],
                            in_=gL_in[e * 16:(e + 1) * 16]).then_inc(PCD, 16)
                g.dma_start(out=uB_d[e * 16:(e + 1) * 16],
                            in_=uL_in[e * 16:(e + 1) * 16]).then_inc(PCD, 16)
                g.dma_start(out=dB_d[e * 8:(e + 1) * 8],
                            in_=dL_in[e * 8:(e + 1) * 8]).then_inc(PCD, 16)
            return g.nop()
        R("g", f_pcast)
        emit_block(runs)
        xsc.close()
        P_gru = ser_base[0]

        gsc = contextlib.ExitStack()
        WT = gsc.enter_context(nc.sbuf_tensor("WTg", [128, 8 * G3], BF16))
        HTR = gsc.enter_context(nc.sbuf_tensor("HTR", [128, 16], BF16))
        SGb = gsc.enter_context(nc.sbuf_tensor("SGb", [128, 1024], F32))
        NG1 = gsc.enter_context(nc.sbuf_tensor("NG1", [128, 512], F32))
        T1 = gsc.enter_context(nc.sbuf_tensor("T1g", [128, 512], F32))
        T2 = gsc.enter_context(nc.sbuf_tensor("T2g", [128, 512], F32))
        NN1 = gsc.enter_context(nc.sbuf_tensor("NN1", [128, 512], F32))
        ZH = gsc.enter_context(nc.sbuf_tensor("ZHg", [128, 512], F32))
        NT2 = gsc.enter_context(nc.sbuf_tensor("NT2", [128, 512], F32))
        SG2 = gsc.enter_context(nc.sbuf_tensor("SG2", [128, 512], F32))
        RGS = gsc.enter_context(nc.sbuf_tensor("RGS", [128, 2 * 768], BF16))
        psA = [gsc.enter_context(nc.psum_tensor(f"psA{p}", [128, 512], F32)) for p in range(2)]
        psB = [gsc.enter_context(nc.psum_tensor(f"psB{p}", [128, 256], F32)) for p in range(2)]
        psTg = [gsc.enter_context(nc.psum_tensor(f"psTg{g}", [128, 4], F32)) for g in range(4)]
        H = H0S

        with nc.Block() as blk:
            @blk.gpsimd
            def _(g):
                g.wait_ge(SER, P_gru)
                g.dma_start(out=WT[:, :].rearrange("p (c j) -> p c j", c=8),
                            in_=whhT_in.rearrange("(c p) j -> p c j", p=128)).then_inc(LD, 16)
                ld_cnt[0] += 16
                g.wait_ge(LD, ld_cnt[0])
                g.nop().then_inc(SGp, 1)
                with g.Fori(0, NB) as i:
                    for par in range(2):
                        step = i * 2 + par
                        sveb = i * 14 + par * 7 + 1
                        g.wait_ge(SV, sveb + 7)
                        col = step + 1
                        g.tensor_copy(HST[:, :].rearrange("p (c t) -> p c t", c=8)[:, :, bass.ts(col, 1)],
                                      HTR[:, par * 8:par * 8 + 8].rearrange("p (c o) -> p c o", o=1)).then_inc(SGp, 1)

            @blk.vector
            def _(v):
                v.wait_ge(SER, P_gru)
                for sb in [SGb, NG1, NT2, SG2, T2, T1, NN1, ZH]:
                    v.memset(sb[:, :], 0.0)
                v.memset(RGS[:, :], 0.0)
                for ps in psA + psB + psTg:
                    v.memset(ps[:, :], 0.0)
                v.drain()
                v.tensor_copy(HST[:, :].rearrange("p (c t) -> p c t", c=8)[:, :, 0:1],
                              H0F[:, :].rearrange("p (c o) -> p c o", o=1))
                v.tensor_copy(HTR[:, 8:16], H0F[:, :])
                v.drain()
                v.nop().then_inc(SV, 1)
                with v.Fori(0, NB) as i:
                    for par in range(2):
                        sveb = i * 14 + par * 7 + 1
                        sab = i * 6 + par * 3 + 6
                        pB = psB[par]
                        q = par * 256
                        q2 = par * 512
                        v.wait_ge(SA, sab + 1)
                        v.tensor_tensor(T1[0:97, q:q + 256], SGb[0:97, q2:q2 + 256],
                                        pB[0:97, :], op=ALU.mult).then_inc(SV, 1)
                        v.tensor_tensor(ZH[0:97, q:q + 256], SGb[0:97, q2 + 256:q2 + 512],
                                        H[0:97, :], op=ALU.mult).then_inc(SV, 1)
                        v.wait_ge(SV, sveb + 1)
                        v.tensor_tensor(T2[0:97, q:q + 256], T1[0:97, q:q + 256],
                                        RGS[0:97, par * 768 + 512:par * 768 + 768], op=ALU.add).then_inc(SV, 1)
                        v.wait_ge(SA, sab + 3)
                        v.tensor_scalar(NT2[0:97, q:q + 256], SG2[0:97, q:q + 256], 2.0, 1.0,
                                        op0=ALU.mult, op1=ALU.subtract).then_inc(SV, 1)
                        v.wait_ge(SV, sveb + 4)
                        v.tensor_tensor(NN1[0:97, q:q + 256], NT2[0:97, q:q + 256],
                                        NG1[0:97, q:q + 256], op=ALU.mult).then_inc(SV, 1)
                        v.wait_ge(SV, sveb + 5)
                        v.tensor_tensor(H[0:97, :], NN1[0:97, q:q + 256],
                                        ZH[0:97, q:q + 256], op=ALU.add).then_inc(SV, 1)
                        v.wait_ge(STp, i * 2 + par + 1)
                        for g4i in range(3):
                            v.tensor_copy(HTR[:, par * 8 + 2 * g4i:par * 8 + 2 * g4i + 2],
                                          psTg[g4i][:, par * 2:par * 2 + 2])
                        v.tensor_copy(HTR[:, par * 8 + 6:par * 8 + 8],
                                      psTg[3][:, par * 2:par * 2 + 2]).then_inc(SV, 1)

            @blk.scalar
            def _(s):
                s.wait_ge(SER, P_gru)
                s.sem_inc(SA, 6)
                with s.Fori(0, NB) as i:
                    for par in range(2):
                        sveb = i * 14 + par * 7 + 1
                        q = par * 256
                        q2 = par * 512
                        s.wait_ge(SM, i * 2 + par + 1)
                        s.activation(SGb[0:97, q2:q2 + 512], psA[par][0:97, :],
                                     AF.Sigmoid).then_inc(SA, 1)
                        s.activation(NG1[0:97, q:q + 256], psA[par][0:97, 256:512],
                                     AF.Sigmoid, scale=-1.0).then_inc(SA, 1)
                        s.wait_ge(SV, sveb + 3)
                        s.activation(SG2[0:97, q:q + 256], T2[0:97, q:q + 256],
                                     AF.Sigmoid, scale=2.0).then_inc(SA, 1)

            @blk.tensor
            def _(t):
                t.wait_ge(SER, P_gru)
                t.wait_ge(SGp, 1)
                with t.Fori(0, NB) as i:
                    for par in range(2):
                        sveb = i * 14 + par * 7 + 1
                        pA, pB = psA[par], psB[par]
                        opar = 1 - par
                        t.wait_ge(SV, sveb)
                        t.wait_ge(SA, i * 6 + par * 3 + 2)
                        for c in range(8):
                            lh = HTR[:, opar * 8 + c:opar * 8 + c + 1]
                            base = c * G3
                            for g in range(4):
                                t.matmul(pA[32 * g:32 * g + 1, 0:512], lh,
                                         WT[:, base + 512 * g:base + 512 * g + 512],
                                         start=(c == 0), stop=False, tile_position=(0, 32 * g))
                                t.matmul(pB[32 * g:32 * g + 1, 0:256], lh,
                                         WT[:, base + 2048 + 256 * g:base + 2048 + 256 * (g + 1)],
                                         start=(c == 0), stop=(c == 7), tile_position=(0, 32 * g))
                        SDp = SD0 if par == 0 else SD1
                        t.wait_ge(SDp, i * 16 + 16)
                        for g in range(4):
                            ins = t.matmul(pA[32 * g:32 * g + 1, 0:512], E32[:, g:g + 1],
                                           RGS[:, par * 768:par * 768 + 512],
                                           start=False, stop=True, tile_position=(0, 32 * g))
                        ins.then_inc(SM, 1)
                        t.wait_ge(SV, sveb + 6)
                        t.drain()
                        for c in range(8):
                            g, half = c // 2, c % 2
                            ins = t.matmul(psTg[g][:, par * 2 + half:par * 2 + half + 1],
                                           H[32 * g:32 * g + 1, 128 * half:128 * half + 128],
                                           ONE1[32 * g:32 * g + 1, 0:1],
                                           start=True, stop=True, tile_position=(32 * g, 0))
                        t.drain()
                        ins.then_inc(STp, 1)

            @blk.sync
            def _(y):
                y.wait_ge(SER, P_gru)
                for j in range(2):
                    SDp = SD0 if j % 2 == 0 else SD1
                    y.dma_start(out=RGS[0:128:32, j * 768:(j + 1) * 768],
                                in_=xw_d[j:j + 1, :].rearrange("r (g q) -> (r g) q", g=4)).then_inc(SDp, 16)
                with y.Fori(0, NB) as i:
                    for par in range(2):
                        step = i * 2 + par
                        sveb = i * 14 + par * 7 + 1
                        y.wait_ge(SV, sveb + 3)
                        r = step + 2
                        SDp = SD0 if par == 0 else SD1
                        y.dma_start(out=RGS[0:128:32, par * 768:par * 768 + 768],
                                    in_=xw_d[bass.ts(r, 1), :].rearrange("r (g q) -> (r g) q", g=4)).then_inc(SDp, 16)
        gsc.close()
        SV_END = 1 + 7 * S
        SGP_END = 1 + S

        ssc = contextlib.ExitStack()
        EXT = ssc.enter_context(nc.sbuf_tensor("EXT", [128, 8 * S], BF16))
        WES = ssc.enter_context(nc.sbuf_tensor("WES", [128, 8 * D], BF16))
        PRD = ssc.enter_context(nc.sbuf_tensor("PRD", [128, 80 * 128], BF16))
        SSB = ssc.enter_context(nc.sbuf_tensor("SSB", [128, 80], F32))
        G2B = ssc.enter_context(nc.sbuf_tensor("G2B", [128, 64], F32))
        SCs = ssc.enter_context(nc.sbuf_tensor("SCs", [128, 64], F32))
        PENR = ssc.enter_context(nc.sbuf_tensor("PENR", [128, 128], F32))
        sPS = ssc.enter_context(nc.psum_tensor("sPS", [128, 512], F32))
        sPT = ssc.enter_context(nc.psum_tensor("sPT", [128, 128], F32))

        runs = []
        R = lambda k, f: runs.append((k, f))

        def f_bridge(v):
            v.wait_ge(SV, SV_END)
            v.wait_ge(SGp, SGP_END)
            v.memset(PENR[:, :], 1.0)
            v.drain()
            return v.nop()
        R("v", f_bridge)

        R("g", mk_gdma([(WES[:, :].rearrange("p (c j) -> p c j", c=8),
                        exprwT_in.rearrange("(c p) j -> p c j", p=128))]))
        for jc in range(8):
            for tsp in range(4):
                def fe_t(t, jc=jc, tsp=tsp):
                    t0 = tsp * 512
                    for c in range(8):
                        ins = t.matmul(sPS[:, :], WES[:, c * D + jc * 128:c * D + jc * 128 + 128],
                                       XNT[:, c * S + t0:c * S + t0 + 512],
                                       start=(c == 0), stop=(c == 7))
                    return ins
                R("t", fe_t)

                def fe_v(v, jc=jc, tsp=tsp):
                    t0 = tsp * 512
                    return v.tensor_copy(EXT[:, jc * S + t0:jc * S + t0 + 512], sPS[:, :])
                R("v", fe_v)

        for tt in range(NT):
            def f_prod(v, tt=tt):
                for e in range(8):
                    for f in range(8):
                        a = HST[:, e * TP1 + 1 + tt * 128:e * TP1 + 1 + (tt + 1) * 128]
                        b = HST[:, f * TP1 + 1 + tt * 128:f * TP1 + 1 + (tt + 1) * 128]
                        v.tensor_tensor(PRD[:, (e * 8 + f) * 128:(e * 8 + f) * 128 + 128],
                                        a, b, op=ALU.mult)
                for e in range(8):
                    a = HST[:, e * TP1 + 1 + tt * 128:e * TP1 + 1 + (tt + 1) * 128]
                    xb = EXT[:, e * S + tt * 128:e * S + (tt + 1) * 128]
                    v.tensor_tensor(PRD[:, (64 + e) * 128:(64 + e) * 128 + 128],
                                    a, xb, op=ALU.mult)
                    v.tensor_tensor(PRD[:, (72 + e) * 128:(72 + e) * 128 + 128],
                                    xb, xb, op=ALU.mult)
                v.drain()
                return v.nop()
            R("v", f_prod)

            def f_sums(t):
                for idx in range(80):
                    ins = t.matmul(sPT[:, idx:idx + 1],
                                   PRD[:, idx * 128:(idx + 1) * 128], ONEB[:, :],
                                   start=True, stop=True)
                return ins
            R("t", f_sums)

            def f_ev(v):
                return v.tensor_copy(SSB[:, :], sPT[:, 0:80])
            R("v", f_ev)

            def f_sq(s):
                s.activation(MISC[:, 8:16], SSB[:, 0:72:9], AF.Sqrt)
                s.drain()
                return s.activation(MISC[:, 16:24], SSB[:, 72:80], AF.Sqrt)
            R("s", f_sq)

            def f_elt(v, tt=tt):
                v.tensor_scalar_max(MISC[:, 8:16], MISC[:, 8:16], 1e-12)
                v.drain()
                v.reciprocal(MISC[:, 24:32], MISC[:, 8:16])
                v.drain()
                for f in range(8):
                    v.tensor_scalar(G2B[:, f:64:8], SSB[:, f:64:8],
                                    MISC[:, 24 + f:25 + f], None, op0=ALU.mult)
                v.drain()
                for e in range(8):
                    v.tensor_scalar(G2B[:, 8 * e:8 * e + 8], G2B[:, 8 * e:8 * e + 8],
                                    MISC[:, 24 + e:25 + e], None, op0=ALU.mult)
                v.drain()
                v.tensor_scalar(G2B[:, 0:64:9], G2B[:, 0:64:9], 1.0, None, op0=ALU.subtract)
                v.drain()
                v.tensor_tensor(G2B[:, :], G2B[:, :], G2B[:, :], op=ALU.mult)
                v.drain()
                for e in range(8):
                    v.reduce_sum(SCs[:, e:e + 1], G2B[:, 8 * e:8 * e + 8], axis=AX)
                v.drain()
                v.tensor_scalar_max(SCs[:, 8:16], SCs[:, 0:8], 1e-24)
                v.drain()
                v.reciprocal(SCs[:, 16:24], SCs[:, 8:16])
                v.drain()
                v.tensor_tensor(SCs[:, 24:32], SCs[:, 0:8], SCs[:, 16:24], op=ALU.mult)
                v.drain()
                v.reduce_sum(PEN[:, tt:tt + 1], SCs[:, 24:32], axis=AX)
                v.drain()
                v.tensor_tensor(SCs[:, 32:40], SSB[:, 64:72], MISC[:, 24:32], op=ALU.mult)
                v.drain()
                v.tensor_tensor(SCs[:, 40:48], MISC[:, 8:16], MISC[:, 24:32], op=ALU.mult)
                v.drain()
                v.tensor_scalar_max(SCs[:, 40:48], SCs[:, 40:48], 1e-8)
                v.drain()
                v.tensor_scalar_max(SCs[:, 48:56], MISC[:, 16:24], 1e-8)
                v.drain()
                v.tensor_tensor(SCs[:, 56:64], SCs[:, 40:48], SCs[:, 48:56], op=ALU.mult)
                v.drain()
                v.reciprocal(SCs[:, 40:48], SCs[:, 56:64])
                v.drain()
                v.tensor_tensor(SCs[:, 48:56], SCs[:, 32:40], SCs[:, 40:48], op=ALU.mult)
                v.drain()
                v.tensor_scalar(CW[:, tt * 8:tt * 8 + 8], SCs[:, 48:56], -1.0, 1.0,
                                op0=ALU.mult, op1=ALU.add)
                v.drain()
                return v.nop()
            R("v", f_elt)

        def f_pt(t):
            return t.matmul(sPT[0:1, 0:16], ONE1[:, :], PEN[:, :], start=True, stop=True)
        R("t", f_pt)

        def f_pv(v):
            v.tensor_copy(MISC[0:1, 0:16], sPT[0:1, 0:16])
            v.drain()
            v.reduce_sum(MISC[0:1, 16:17], MISC[0:1, 0:16], axis=AX)
            v.drain()
            v.tensor_scalar_mul(MISC[0:1, 16:17], MISC[0:1, 16:17], 1.0 / (8.0 * S))
            v.drain()
            return v.nop()
        R("v", f_pv)

        def f_pb(v):
            v.tensor_scalar(PENR[0:1, 0:128], PENR[0:1, 0:128],
                            MISC[0:1, 16:17], None, op0=ALU.mult)
            v.drain()
            return v.nop()
        R("v", f_pb)
        R("y", mk_sdma([(pen_in_d, PENR[0:1, 0:128])]))

        def f_cc(g):
            g.collective_compute("AllReduce", ALU.add,
                                 replica_groups=[list(range(8))],
                                 ins=[pen_in_d], outs=[pen_out_d]).then_inc(CCS, 1)
            g.wait_ge(CCS, 1)
            return g.nop()
        R("g", f_cc)
        R("y", mk_sdma([(MISC[:, 17:18], pen_out_d.rearrange("o p -> p o"))], slow=True))

        def f_fac(v):
            v.tensor_scalar(MISC[:, 18:19], MISC[:, 17:18], 1.0, None, op0=ALU.add)
            v.drain()
            return v.nop()
        R("v", f_fac)

        for tt in range(NT):
            def f_cw1(v, tt=tt):
                cd = CW[:, tt * 8:tt * 8 + 8]
                v.reduce_max(MISC[:, 20:21], cd, axis=AX)
                v.drain()
                v.tensor_scalar(SCs[:, 0:8], cd, MISC[:, 20:21], None, op0=ALU.is_equal)
                v.drain()
                v.tensor_scalar(SCs[:, 8:16], SCs[:, 0:8], 1e30, None, op0=ALU.mult)
                v.drain()
                v.tensor_tensor(SCs[:, 16:24], cd, SCs[:, 8:16], op=ALU.subtract)
                v.drain()
                v.reduce_max(MISC[:, 21:22], SCs[:, 16:24], axis=AX)
                v.drain()
                v.tensor_scalar(SCs[:, 24:32], cd, MISC[:, 21:22], None, op0=ALU.is_equal)
                v.drain()
                v.tensor_tensor(MISC[:, 22:23], MISC[:, 20:21], MISC[:, 21:22], op=ALU.subtract)
                v.drain()
                return v.nop()
            R("v", f_cw1)

            def f_cw2(s):
                return s.activation(MISC[:, 23:24], MISC[:, 22:23], AF.Sigmoid,
                                    scale=MISC[:, 18:19])
            R("s", f_cw2)

            def f_cw3(v, tt=tt):
                v.tensor_scalar(MISC[:, 24:25], MISC[:, 23:24], -1.0, 1.0,
                                op0=ALU.mult, op1=ALU.add)
                v.drain()
                v.tensor_scalar(SCs[:, 32:40], SCs[:, 0:8], MISC[:, 23:24], None, op0=ALU.mult)
                v.drain()
                v.tensor_scalar(SCs[:, 40:48], SCs[:, 24:32], MISC[:, 24:25], None, op0=ALU.mult)
                v.drain()
                v.tensor_tensor(CW[:, tt * 8:tt * 8 + 8], SCs[:, 32:40],
                                SCs[:, 40:48], op=ALU.add)
                v.drain()
                return v.nop()
            R("v", f_cw3)
        emit_block(runs)
        ssc.close()
        g4.close()

        msc = contextlib.ExitStack()
        Y = msc.enter_context(nc.sbuf_tensor("Yacc", [128, 8 * 1024], F32))
        AT = msc.enter_context(nc.sbuf_tensor("ATm", [128, 16 * 1024], BF16))
        GUW = msc.enter_context(nc.sbuf_tensor("GUW", [128, 4 * 2048], BF16))
        DWN = msc.enter_context(nc.sbuf_tensor("DWN", [128, 4 * 2048], BF16))
        CWTR = msc.enter_context(nc.sbuf_tensor("CWTR", [128, S], BF16))
        CWBE = msc.enter_context(nc.sbuf_tensor("CWBE", [128, 1024], F32))
        SEL8 = msc.enter_context(nc.sbuf_tensor("SEL8", [128, 8 * 128], BF16))
        U12 = msc.enter_context(nc.sbuf_tensor("U12", [128, 3072], F32))
        CWB8 = msc.enter_context(nc.sbuf_tensor("CWB8", [128, 8], BF16))
        EYEF = msc.enter_context(nc.sbuf_tensor("EYEF", [128, 128], F32))
        psc = contextlib.ExitStack()
        tPS = psc.enter_context(nc.psum_tensor("tPS", [128, 128], BF16))

        runs = []
        R = lambda k, f: runs.append((k, f))
        consume_run = []
        chunk_dmas = []
        NCH = 2 * NE * 24

        R("y", mk_sdma([(EYEF[:, :], eyef_in), (SEL8[:, :], sel8_in)]))

        def f_cz(v):
            v.memset(CWTR[:, :], 0.0)
            v.drain()
            return v.nop()
        R("v", f_cz)
        for tt in range(NT):
            def f_ct1(v, tt=tt):
                v.tensor_copy(CWB8[:, :], CW[:, tt * 8:tt * 8 + 8])
                v.drain()
                return v.nop()
            R("v", f_ct1)

            def f_ct2(t):
                return t.matmul(tPS[0:8, 0:128], CWB8[:, :], EYEB[:, :],
                                start=True, stop=True, is_transpose=True)
            R("t", f_ct2)

            def f_ct3(v, tt=tt):
                return v.tensor_copy(CWTR[0:8, tt * 128:(tt + 1) * 128], tPS[0:8, 0:128])
            R("v", f_ct3)

        P_moe = emit_block(runs) + len(runs)
        psc.close()
        mPS8 = [msc.enter_context(nc.psum_tensor(f"mPS8_{p}", [128, 512], F32)) for p in range(8)]

        NST = es.enter_context(nc.semaphore("NST"))
        NSL = es.enter_context(nc.semaphore("NSL"))
        NAU = es.enter_context(nc.semaphore("NAU"))
        NTD = es.enter_context(nc.semaphore("NTD"))
        NDV = es.enter_context(nc.semaphore("NDV"))
        NCB = es.enter_context(nc.semaphore("NCB"))
        NXN = es.enter_context(nc.semaphore("NXN"))
        NEP = es.enter_context(nc.semaphore("NEP"))

        sched = []
        for mb in range(2):
            for e in range(NE):
                sched.append((mb, e))
        chunk_marks = []
        gi = 0
        di = 0
        for mb, e in sched:
            for fc in range(16):
                chunk_marks.append(("G", gi + 1))
                chunk_dmas.append(2)
                gi += 1
            for hc in range(8):
                chunk_marks.append(("D", di + 1))
                chunk_dmas.append(1)
                di += 1
        assert len(chunk_marks) == NCH
        wd_cum.extend(np.cumsum(chunk_dmas).tolist())
        EP_STEPS = 7
        EP_MB = 8 * EP_STEPS

        with nc.Block() as blk:
            @blk.tensor
            def _(t):
                t.wait_ge(SER, P_moe)
                gi = 0
                di = 0
                ci = 0
                xn = 0
                for mb, e in sched:
                    dn_before = di
                    if e < 8:
                        if gi >= 2:
                            t.wait_ge(NAU, gi - 1)
                        t.wait_ge(NDV, dn_before)
                        if mb == 1 and e == 0:
                            t.wait_ge(NEP, EP_MB)
                        for sb in range(2):
                            ins = t.matmul(mPS8[sb][:, :], SEL8[:, e * 128:(e + 1) * 128],
                                           CWTR[:, mb * 1024 + sb * 512:mb * 1024 + sb * 512 + 512],
                                           start=True, stop=True)
                        ins.then_inc(NCB, 1)
                        xn += 1
                    for fc in range(16):
                        par = fc % 2
                        s4 = fc % 4
                        t.wait_ge(WD, 16 * wd_cum[min(ci + 3, NCH - 1)])
                        if gi >= 2:
                            t.wait_ge(NAU, gi - 1)
                        if fc == 0:
                            t.wait_ge(NDV, dn_before)
                            if e < 8:
                                t.wait_ge(NXN, xn)
                            if mb == 1 and e == 0:
                                t.wait_ge(NEP, EP_MB)
                        for gu in range(2):
                            for half in range(2):
                                ps = mPS8[4 * par + gu * 2 + half]
                                for c in range(8):
                                    rhs = XNT[:, c * S + mb * 1024 + half * 512:
                                              c * S + mb * 1024 + half * 512 + 512]
                                    ins = t.matmul(
                                        ps[:, :],
                                        GUW[:, s4 * 2048 + gu * 1024 + c * 128:
                                             s4 * 2048 + gu * 1024 + c * 128 + 128],
                                        rhs, start=(c == 0), stop=(c == 7))
                        ins.then_inc(NST, 1)
                        gi += 1
                        ci += 1
                    aulast = gi
                    for hc in range(8):
                        q = hc % 2
                        q4 = hc % 4
                        t.wait_ge(WD, 16 * wd_cum[min(ci + 3, NCH - 1)])
                        if hc == 0:
                            t.wait_ge(NAU, aulast)
                        if di >= 2:
                            t.wait_ge(NDV, di - 1)
                        for half in range(2):
                            ps = mPS8[2 * q + half]
                            for fcc in range(16):
                                ins = t.matmul(ps[:, :],
                                               DWN[:, q4 * 2048 + fcc * 128:q4 * 2048 + fcc * 128 + 128],
                                               AT[:, fcc * 1024 + half * 512:fcc * 1024 + half * 512 + 512],
                                               start=(fcc == 0), stop=(fcc == 15))
                        ins.then_inc(NTD, 1)
                        di += 1
                        ci += 1
                    if e == NE - 1:
                        for lt in range(8):
                            base_i = mb * EP_MB + lt * EP_STEPS
                            for chalf in range(2):
                                t.wait_ge(NEP, base_i + 2 + chalf * 2)
                                for i in range(4):
                                    c = chalf * 4 + i
                                    ins = t.matmul(mPS8[i][:, 0:128],
                                                   U12[:, c * 128:(c + 1) * 128],
                                                   EYEF[:, :], start=True, stop=True)
                                ins.then_inc(NEP, 1)

            @blk.scalar
            def _(s):
                s.wait_ge(SER, P_moe)
                for gi in range(2 * NE * 16):
                    par = gi % 2
                    s.wait_ge(NST, gi + 1)
                    if gi >= 2:
                        s.wait_ge(NAU, gi - 1)
                    s.activation(U12[:, par * 1024:par * 1024 + 512], mPS8[4 * par][:, :], AF.Silu)
                    s.activation(U12[:, par * 1024 + 512:par * 1024 + 1024],
                                 mPS8[4 * par + 1][:, :], AF.Silu).then_inc(NSL, 1)

            @blk.vector
            def _(v):
                v.wait_ge(SER, P_moe)
                gi = 0
                di = 0
                cb = 0
                for mb, e in sched:
                    if e < 8:
                        cb += 1
                        v.wait_ge(NCB, cb)
                        v.tensor_copy(CWBE[:, 0:512], mPS8[0][:, :])
                        ins = v.tensor_copy(CWBE[:, 512:1024], mPS8[1][:, :])
                        ins.then_inc(NXN, 1)
                    for fc in range(16):
                        par = fc % 2
                        v.wait_ge(NSL, gi + 1)
                        if e < 8:
                            for half in range(2):
                                v.tensor_tensor(
                                    U12[:, 2048 + half * 512:2048 + half * 512 + 512],
                                    U12[:, par * 1024 + half * 512:par * 1024 + half * 512 + 512],
                                    mPS8[4 * par + 2 + half][:, :], op=ALU.mult)
                            v.drain()
                            for half in range(2):
                                ins = v.tensor_tensor(
                                    AT[:, fc * 1024 + half * 512:fc * 1024 + half * 512 + 512],
                                    U12[:, 2048 + half * 512:2048 + half * 512 + 512],
                                    CWBE[:, half * 512:half * 512 + 512], op=ALU.mult)
                        else:
                            for half in range(2):
                                ins = v.tensor_tensor(
                                    AT[:, fc * 1024 + half * 512:fc * 1024 + half * 512 + 512],
                                    U12[:, par * 1024 + half * 512:par * 1024 + half * 512 + 512],
                                    mPS8[4 * par + 2 + half][:, :], op=ALU.mult)
                        ins.then_inc(NAU, 1)
                        gi += 1
                    v.drain()
                    for hc in range(8):
                        q = hc % 2
                        v.wait_ge(NTD, di + 1)
                        for half in range(2):
                            dst = Y[:, hc * 1024 + half * 512:hc * 1024 + half * 512 + 512]
                            if e == 0:
                                ins = v.tensor_copy(dst, mPS8[2 * q + half][:, :])
                            else:
                                ins = v.tensor_tensor(dst, dst, mPS8[2 * q + half][:, :], op=ALU.add)
                        ins.then_inc(NDV, 1)
                        di += 1
                    if e == NE - 1:
                        for lt in range(8):
                            tt = mb * 8 + lt
                            base_i = mb * EP_MB + lt * EP_STEPS
                            v.wait_ge(NEP, base_i + 1)
                            for c in range(8):
                                v.tensor_copy(U12[:, c * 128:(c + 1) * 128],
                                              ATT[:, c * S + tt * 128:c * S + (tt + 1) * 128])
                            v.drain()
                            for c in range(8):
                                v.tensor_tensor(U12[:, c * 128:(c + 1) * 128],
                                                U12[:, c * 128:(c + 1) * 128],
                                                Y[:, c * 1024 + lt * 128:c * 1024 + (lt + 1) * 128],
                                                op=ALU.add)
                            v.drain()
                            v.nop().then_inc(NEP, 1)
                            for chalf in range(2):
                                v.wait_ge(NEP, base_i + 3 + chalf * 2)
                                for i in range(4):
                                    c = chalf * 4 + i
                                    ins = v.tensor_tensor(U12[:, 1024 + c * 128:1024 + (c + 1) * 128],
                                                          mPS8[i][:, 0:128],
                                                          U12[:, 2048 + c * 128:2048 + (c + 1) * 128],
                                                          op=ALU.add)
                                ins.then_inc(NEP, 1)

            @blk.sync
            def _(y):
                y.wait_ge(SER, P_moe)
                for mb in range(2):
                    for lt in range(8):
                        tt = mb * 8 + lt
                        base_i = mb * EP_MB + lt * EP_STEPS
                        y.wait_ge(NEP, base_i)
                        if lt == 0:
                            y.wait_ge(NDV, 72 * (mb + 1))
                        y.dma_start(out=U12[:, 2048:3072],
                                    in_=x_in[tt * 128:(tt + 1) * 128, :]).then_inc(LS, 16)
                        ls_cnt[0] += 16
                        y.wait_ge(LS, ls_cnt[0])
                        y.nop().then_inc(NEP, 1)
                        y.wait_ge(NEP, base_i + 6)
                        y.dma_start(out=out_d[tt * 128:(tt + 1) * 128, :],
                                    in_=U12[:, 1024:2048]).then_inc(LS, 16)
                        ls_cnt[0] += 16
                        y.wait_ge(LS, ls_cnt[0])
                        y.nop().then_inc(NEP, 1)

            @blk.gpsimd
            def _(g):
                g.wait_ge(SER, P_moe)
                g.wait_ge(PCD, 16 * 3 * NE)
                ci = 0
                for mb, e in sched:
                    for fc in range(16):
                        s4 = fc % 4
                        if ci >= 4:
                            kind, val = chunk_marks[ci - 4]
                            g.wait_ge(NST if kind == "G" else NTD, val)
                        g.dma_start(out=GUW[:, s4 * 2048:s4 * 2048 + 1024],
                                    in_=gB_d[e * 16 + fc]).then_inc(WD, 16)
                        g.dma_start(out=GUW[:, s4 * 2048 + 1024:s4 * 2048 + 2048],
                                    in_=uB_d[e * 16 + fc]).then_inc(WD, 16)
                        ci += 1
                    for hc in range(8):
                        q4 = hc % 4
                        if ci >= 4:
                            kind, val = chunk_marks[ci - 4]
                            g.wait_ge(NST if kind == "G" else NTD, val)
                        g.dma_start(out=DWN[:, q4 * 2048:q4 * 2048 + 2048],
                                    in_=dB_d[e * 8 + hc]).then_inc(WD, 16)
                        ci += 1
        msc.close()
    return nc


PERM_W = _perm_w()
PERM_XW = _perm_xw()


def kernel(**inputs):
    xfull = np.asarray(inputs["L0_attn_x"])
    f32 = np.float32
    asc = np.ascontiguousarray
    wqT = asc(np.asarray(inputs["L0_attn_Wq"]).T.astype(f32))
    wkT = asc(np.asarray(inputs["L0_attn_Wk"]).T.astype(f32))
    wvT = asc(np.asarray(inputs["L0_attn_Wv"]).T.astype(f32))
    woT = asc(np.asarray(inputs["L0_attn_Wo"]).T.astype(f32))
    bq = np.asarray(inputs["L0_attn_bq"]).astype(f32)
    bk = np.asarray(inputs["L0_attn_bk"]).astype(f32)
    bv = np.asarray(inputs["L0_attn_bv"]).astype(f32)
    bo = np.asarray(inputs["L0_attn_bo"]).astype(f32)
    brow = asc(np.stack([bq, bk, bv, bo]))
    bo_f = asc(bo.reshape(8, 128).T)
    whhT = asc(np.asarray(inputs["L0_ffn_gru_w_hh"]).T.astype(f32)[:, PERM_W])
    wihT = asc(np.asarray(inputs["L0_ffn_gru_w_ih"]).T.astype(f32)[:, PERM_XW])
    ctxwT = asc(np.asarray(inputs["L0_ffn_ctx_w"]).T.astype(f32))
    ctxbf = asc(np.asarray(inputs["L0_ffn_ctx_b"]).astype(f32).reshape(8, 128).T)
    exprwT = asc(np.asarray(inputs["L0_ffn_expr_w"]).T.astype(f32))
    g9 = np.concatenate([np.asarray(inputs["L0_ffn_gate_w"]),
                         np.asarray(inputs["L0_ffn_shared_gate_w"])[None]],
                        axis=0).transpose(0, 2, 1).astype(f32)
    u9 = np.concatenate([np.asarray(inputs["L0_ffn_up_w"]),
                         np.asarray(inputs["L0_ffn_shared_up_w"])[None]],
                        axis=0).transpose(0, 2, 1).astype(f32)
    d9 = np.concatenate([np.asarray(inputs["L0_ffn_down_w"]),
                         np.asarray(inputs["L0_ffn_shared_down_w"])[None]],
                        axis=0).transpose(0, 2, 1).astype(f32)
    gL = asc(g9.reshape(NE, 8, 128, 16, 128).transpose(0, 3, 2, 1, 4).reshape(NE * 16, 128, 1024))
    uL = asc(u9.reshape(NE, 8, 128, 16, 128).transpose(0, 3, 2, 1, 4).reshape(NE * 16, 128, 1024))
    dL = asc(d9.reshape(NE, 16, 128, 8, 128).transpose(0, 3, 2, 1, 4).reshape(NE * 8, 128, 2048))

    inv = 1.0 / (10000.0 ** (np.arange(0, 64, 2, dtype=np.float64) / 64.0))
    ang = np.arange(S, dtype=np.float64)[:, None] * inv[None, :]
    cosE = asc(np.tile(np.cos(ang).astype(f32), (1, 8)))
    sinE = asc(np.tile(np.sin(ang).astype(f32), (1, 8)))
    eyeb = asc(np.eye(128, dtype=f32).astype(BF))
    eyef = asc(np.eye(128, dtype=f32))
    ku = np.arange(128)
    umask = asc(np.where(ku[:, None] > ku[None, :], np.float32(-1e9),
                         np.float32(0.0)).astype(BF))
    sel8 = np.zeros((128, 8 * 128), dtype=np.float32)
    for e in range(8):
        sel8[e, e * 128:(e + 1) * 128] = 1.0
    sel8 = asc(sel8.astype(BF))

    nc = build()
    in_maps = []
    for i in range(8):
        in_maps.append({
            "x": asc(xfull[i].astype(f32)),
            "wqT": wqT, "wkT": wkT, "wvT": wvT, "woT": woT,
            "brow": brow, "bo_f": bo_f,
            "whhT": whhT, "wihT": wihT,
            "ctxwT": ctxwT, "ctxbf": ctxbf,
            "exprwT": exprwT,
            "g9L": gL, "u9L": uL, "d9L": dL,
            "cosE": cosE, "sinE": sinE,
            "eyeb": eyeb, "eyef": eyef, "umask": umask, "sel8": sel8,
        })
    res = run_bass_kernel_spmd(nc, in_maps, core_ids=list(range(8)))
    out = np.stack([np.asarray(r["y"]) for r in res.results], axis=0)
    return out


if __name__ == "__main__":
    nc = build()
    print("built ok")


# version 07
# speedup vs baseline: 1.1018x; 1.0718x over previous
import contextlib
import numpy as np
import ml_dtypes
import concourse.bass as bass
import concourse.mybir as mybir
from concourse.bass_utils import run_bass_kernel_spmd

F32 = mybir.dt.float32
BF16 = mybir.dt.bfloat16
AF = mybir.ActivationFunctionType
ALU = mybir.AluOpType
AX = mybir.AxisListType.X
BF = ml_dtypes.bfloat16

S, D = 2048, 1024
NT = S // 128
G3 = 3072
TP1 = S + 1
NB = S // 2
FF = 2048
NE = 9


def _perm_w():
    idx = []
    for g in range(4):
        idx.extend(range(256 * g, 256 * (g + 1)))
        idx.extend(range(1024 + 256 * g, 1024 + 256 * (g + 1)))
    idx.extend(range(2048, 3072))
    return np.array(idx)


def _perm_xw():
    idx = []
    for g in range(4):
        idx.extend(range(256 * g, 256 * (g + 1)))
        idx.extend(range(1024 + 256 * g, 1024 + 256 * (g + 1)))
        idx.extend(range(2048 + 256 * g, 2048 + 256 * (g + 1)))
    return np.array(idx)


def build():
    nc = bass.Bass("TRN2", target_bir_lowering=False, debug=False)
    dt = nc.dram_tensor
    x_in = dt("x", [S, D], F32, kind="ExternalInput").ap()
    wqT_in = dt("wqT", [D, D], F32, kind="ExternalInput").ap()
    wkT_in = dt("wkT", [D, D], F32, kind="ExternalInput").ap()
    wvT_in = dt("wvT", [D, D], F32, kind="ExternalInput").ap()
    woT_in = dt("woT", [D, D], F32, kind="ExternalInput").ap()
    brow_in = dt("brow", [4, D], F32, kind="ExternalInput").ap()
    bo_f_in = dt("bo_f", [128, 8], F32, kind="ExternalInput").ap()
    whhT_in = dt("whhT", [D, G3], F32, kind="ExternalInput").ap()
    wihT_in = dt("wihT", [D, G3], F32, kind="ExternalInput").ap()
    ctxwT_in = dt("ctxwT", [D, D], F32, kind="ExternalInput").ap()
    ctxbf_in = dt("ctxbf", [128, 8], F32, kind="ExternalInput").ap()
    exprwT_in = dt("exprwT", [D, D], F32, kind="ExternalInput").ap()
    gL_in = dt("g9L", [NE * 16, 128, 1024], F32, kind="ExternalInput").ap()
    uL_in = dt("u9L", [NE * 16, 128, 1024], F32, kind="ExternalInput").ap()
    dL_in = dt("d9L", [NE * 8, 128, 2048], F32, kind="ExternalInput").ap()
    cosE_in = dt("cosE", [S, 256], F32, kind="ExternalInput").ap()
    sinE_in = dt("sinE", [S, 256], F32, kind="ExternalInput").ap()
    eyeb_in = dt("eyeb", [128, 128], BF16, kind="ExternalInput").ap()
    eyef_in = dt("eyef", [128, 128], F32, kind="ExternalInput").ap()
    umask_in = dt("umask", [128, 128], BF16, kind="ExternalInput").ap()
    sel8_in = dt("sel8", [128, 8 * 128], BF16, kind="ExternalInput").ap()
    out_d = dt("y", [S, D], F32, kind="ExternalOutput").ap()
    xw_d = dt("xw_scr", [S + 2, G3], BF16, kind="Internal").ap()
    gB_d = dt("gB", [NE * 16, 128, 1024], BF16, kind="Internal").ap()
    uB_d = dt("uB", [NE * 16, 128, 1024], BF16, kind="Internal").ap()
    dB_d = dt("dB", [NE * 8, 128, 2048], BF16, kind="Internal").ap()
    h0tmp_d = dt("h0tmp", [128, 8], F32, kind="Internal").ap()
    pen_in_d = dt("pen_in", [1, 128], F32, kind="Internal").ap()
    pen_out_d = dt("pen_out", [1, 128], F32, kind="Internal").ap()

    es = contextlib.ExitStack()
    with es:
        XNT = es.enter_context(nc.sbuf_tensor("XNT", [128, 8 * S], BF16))
        ATT = es.enter_context(nc.sbuf_tensor("ATT", [128, 8 * S], BF16))
        EYEB = es.enter_context(nc.sbuf_tensor("EYEB", [128, 128], BF16))
        UMASK = es.enter_context(nc.sbuf_tensor("UMASK", [128, 128], BF16))
        SELRT = es.enter_context(nc.sbuf_tensor("SELRT", [128, 128], BF16))
        SEL64 = es.enter_context(nc.sbuf_tensor("SEL64", [128, 128], BF16))
        E32 = es.enter_context(nc.sbuf_tensor("E32c", [128, 4], BF16))
        ONE1 = es.enter_context(nc.sbuf_tensor("ONE1", [128, 1], F32))
        ONEB = es.enter_context(nc.sbuf_tensor("ONEB", [128, 1], BF16))
        MISC = es.enter_context(nc.sbuf_tensor("MISC", [128, 32], F32))
        CW = es.enter_context(nc.sbuf_tensor("CWtm", [128, NT * 8], F32))
        PEN = es.enter_context(nc.sbuf_tensor("PENt", [128, NT], F32))
        H0F = es.enter_context(nc.sbuf_tensor("H0F", [128, 8], F32))
        H0S = es.enter_context(nc.sbuf_tensor("H0S", [128, 256], F32))
        BOF = es.enter_context(nc.sbuf_tensor("BOF", [128, 8], F32))
        SER = es.enter_context(nc.semaphore("SER"))
        LD = es.enter_context(nc.semaphore("LDd"))
        LS = es.enter_context(nc.semaphore("LSd"))
        SM = es.enter_context(nc.semaphore("SMg"))
        STp = es.enter_context(nc.semaphore("STg"))
        SA = es.enter_context(nc.semaphore("SAg"))
        SV = es.enter_context(nc.semaphore("SVg"))
        SD0 = es.enter_context(nc.semaphore("SD0"))
        SD1 = es.enter_context(nc.semaphore("SD1"))
        SGp = es.enter_context(nc.semaphore("SGg"))
        WD = es.enter_context(nc.semaphore("WDd"))
        CCS = es.enter_context(nc.semaphore("CCS"))
        PCD = es.enter_context(nc.semaphore("PCD"))

        ld_cnt = [0]
        ls_cnt = [0]
        ser_base = [0]
        wd_cum = []

        def emit_block(runs, extra_gpsimd=None):
            base = ser_base[0]
            with nc.Block() as blk:
                for ekey, attr in [("v", "vector"), ("s", "scalar"), ("t", "tensor"),
                                   ("g", "gpsimd"), ("y", "sync")]:
                    if ekey == "g" and extra_gpsimd is not None:
                        blk.gpsimd(lambda g: extra_gpsimd(g, base))
                        continue

                    def cb(eng, ekey=ekey):
                        for i, (k, fn) in enumerate(runs):
                            if k != ekey:
                                continue
                            eng.wait_ge(SER, base + i)
                            fn(eng).then_inc(SER, 1)
                    getattr(blk, attr)(cb)
            ser_base[0] = base + len(runs)
            return base

        def mk_gdma(pairs, slow=False):
            def f(g):
                for dst, src in pairs:
                    g.dma_start(out=dst, in_=src,
                                allow_slow_non_contiguous=slow).then_inc(LD, 16)
                    ld_cnt[0] += 16
                g.wait_ge(LD, ld_cnt[0])
                return g.nop()
            return f

        def mk_sdma(pairs, slow=False):
            def f(y):
                for dst, src in pairs:
                    y.dma_start(out=dst, in_=src,
                                allow_slow_non_contiguous=slow).then_inc(LS, 16)
                    ls_cnt[0] += 16
                y.wait_ge(LS, ls_cnt[0])
                return y.nop()
            return f

        pre = contextlib.ExitStack()
        SC1 = pre.enter_context(nc.sbuf_tensor("SC1", [128, D], F32))
        SC3 = pre.enter_context(nc.sbuf_tensor("SC3", [128, 256], F32))
        SC4 = pre.enter_context(nc.sbuf_tensor("SC4", [128, 256], F32))
        SCB = pre.enter_context(nc.sbuf_tensor("SCBa", [128, 1024], BF16))
        STT = pre.enter_context(nc.sbuf_tensor("STT", [128, 8], F32))
        WSTG = pre.enter_context(nc.sbuf_tensor("WSTG", [128, 8 * D], BF16))
        QT = pre.enter_context(nc.sbuf_tensor("QTf", [128, 4 * S], BF16))
        KT = pre.enter_context(nc.sbuf_tensor("KTf", [128, 4 * S], BF16))
        VA = pre.enter_context(nc.sbuf_tensor("VAf", [128, NT * 16 * 65], BF16))
        CCT = pre.enter_context(nc.sbuf_tensor("CCT", [128, 8 * S], BF16))
        PTB = pre.enter_context(nc.sbuf_tensor("PTBb", [128, 1024], BF16))
        CTRG = pre.enter_context(nc.sbuf_tensor("CTRG", [128, 512], F32))
        SPS = pre.enter_context(nc.psum_tensor("SPS", [128, 1024], F32))
        TPB = pre.enter_context(nc.psum_tensor("TPB", [128, 1024], BF16))
        AVP = pre.enter_context(nc.psum_tensor("AVP", [128, 2048], F32))

        runs = []
        R = lambda k, f: runs.append((k, f))

        R("g", mk_gdma([(EYEB[:, :], eyeb_in), (UMASK[:, :], umask_in),
                        (BOF[:, :], bo_f_in)]))

        def f_init(v):
            v.memset(SELRT[:, :], 0.0)
            v.memset(SEL64[:, :], 0.0)
            v.memset(E32[:, :], 0.0)
            v.memset(PTB[:, :], 0.0)
            v.drain()
            v.memset(SELRT[0:1, :], 1.0)
            v.memset(SEL64[64:65, :], 1.0)
            for g in range(4):
                v.memset(E32[32 * g:32 * g + 1, g:g + 1], 1.0)
            v.memset(ONE1[:, :], 1.0)
            v.memset(ONEB[:, :], 1.0)
            v.memset(H0S[:, :], 0.0)
            v.memset(VA[:, :], 1.0)
            v.drain()
            return v.nop()
        R("v", f_init)

        for tt in range(NT):
            R("g", mk_gdma([(SC1[:, :], x_in[tt * 128:(tt + 1) * 128, :])]))

            def f1(v):
                v.reduce_sum(STT[:, 0:1], SC1[:, :], axis=AX)
                v.drain()
                v.tensor_scalar_mul(STT[:, 1:2], STT[:, 0:1], 1.0 / D)
                v.drain()
                v.tensor_scalar(SC1[:, :], SC1[:, :], STT[:, 1:2], None, op0=ALU.subtract)
                v.drain()
                return v.nop()
            R("v", f1)

            def f2(s):
                return s.activation(SCB[:, 0:D], SC1[:, :], AF.Square, accum_out=STT[:, 2:3])
            R("s", f2)

            def f2b(v):
                v.tensor_scalar(STT[:, 5:6], STT[:, 2:3], 1.0 / D, 1e-5,
                                op0=ALU.mult, op1=ALU.add)
                v.drain()
                return v.nop()
            R("v", f2b)

            def f2c(s):
                return s.activation(STT[:, 3:4], STT[:, 5:6], AF.Sqrt)
            R("s", f2c)

            def f3(v):
                v.reciprocal(STT[:, 4:5], STT[:, 3:4])
                v.drain()
                v.tensor_scalar(SC1[:, :], SC1[:, :], STT[:, 4:5], None, op0=ALU.mult)
                v.drain()
                v.tensor_copy(SCB[:, 0:D], SC1[:, :])
                v.drain()
                return v.nop()
            R("v", f3)

            def f4(t):
                for i in range(8):
                    ins = t.matmul(TPB[:, i * 128:(i + 1) * 128],
                                   SCB[:, i * 128:(i + 1) * 128], EYEB[:, :],
                                   start=True, stop=True, is_transpose=True)
                return ins
            R("t", f4)

            def f5(v, tt=tt):
                for i in range(8):
                    ins = v.tensor_copy(XNT[:, i * S + tt * 128:i * S + (tt + 1) * 128],
                                        TPB[:, i * 128:(i + 1) * 128])
                return ins
            R("v", f5)

        R("g", mk_gdma([(WSTG[:, :].rearrange("p (c j) -> p c j", c=8),
                        wvT_in.rearrange("(c p) j -> p c j", p=128)),
                       (PTB[0:1, 0:D], brow_in[2:3, :])]))
        for tt in range(NT):
            def fv_t(t, tt=tt):
                for jh in range(2):
                    for c in range(8):
                        t.matmul(SPS[:, jh * 512:jh * 512 + 512],
                                 XNT[:, c * S + tt * 128:c * S + (tt + 1) * 128],
                                 WSTG[:, c * D + jh * 512:c * D + jh * 512 + 512],
                                 start=(c == 0), stop=False)
                    ins = t.matmul(SPS[:, jh * 512:jh * 512 + 512], SELRT[:, :],
                                   PTB[:, jh * 512:jh * 512 + 512],
                                   start=False, stop=True)
                return ins
            R("t", fv_t)

            def fv_v(v, tt=tt):
                for jh in range(2):
                    base = (tt * 16 + jh * 8) * 65
                    ins = v.tensor_copy(
                        VA[:, base:base + 8 * 65].rearrange("p (h e) -> p h e", h=8)[:, :, 0:64],
                        SPS[:, jh * 512:jh * 512 + 512].rearrange("p (h e) -> p h e", h=8))
                return ins
            R("v", fv_v)

        for hg in range(2):
            R("g", mk_gdma([
                (WSTG[:, 0:4096].rearrange("p (c j) -> p c j", c=8),
                 wqT_in[:, hg * 512:(hg + 1) * 512].rearrange("(c p) j -> p c j", p=128)),
                (WSTG[:, 4096:8192].rearrange("p (c j) -> p c j", c=8),
                 wkT_in[:, hg * 512:(hg + 1) * 512].rearrange("(c p) j -> p c j", p=128)),
                (PTB[0:1, 0:512], brow_in[0:1, hg * 512:(hg + 1) * 512]),
                (PTB[0:1, 512:1024], brow_in[1:2, hg * 512:(hg + 1) * 512]),
            ]))
            for tt in range(NT):
                R("y", mk_sdma([(CTRG[:, 0:256], cosE_in[tt * 128:(tt + 1) * 128, :]),
                                (CTRG[:, 256:512], sinE_in[tt * 128:(tt + 1) * 128, :])]))

                def fp_t(t, tt=tt):
                    for qk in range(2):
                        for c in range(8):
                            t.matmul(SPS[:, qk * 512:qk * 512 + 512],
                                     XNT[:, c * S + tt * 128:c * S + (tt + 1) * 128],
                                     WSTG[:, qk * 4096 + c * 512:qk * 4096 + c * 512 + 512],
                                     start=(c == 0), stop=False)
                        ins = t.matmul(SPS[:, qk * 512:qk * 512 + 512], SELRT[:, :],
                                       PTB[:, qk * 512:qk * 512 + 512],
                                       start=False, stop=True)
                    return ins
                R("t", fp_t)

                for qk in range(2):
                    scl = 0.125 if qk == 0 else 1.0

                    def fp_rope(v, qk=qk, scl=scl):
                        v.tensor_copy(SC1[:, 0:512], SPS[:, qk * 512:qk * 512 + 512])
                        v.drain()
                        QE = SC1[:, 0:512].rearrange("p (h d) -> p h d", h=8)[:, :, 0:64:2]
                        QO = SC1[:, 0:512].rearrange("p (h d) -> p h d", h=8)[:, :, 1:64:2]
                        CO = CTRG[:, 0:256].rearrange("p (h i) -> p h i", h=8)
                        SI = CTRG[:, 256:512].rearrange("p (h i) -> p h i", h=8)
                        S3 = SC3[:, :].rearrange("p (h i) -> p h i", h=8)
                        S4 = SC4[:, :].rearrange("p (h i) -> p h i", h=8)
                        RE = SC1[:, 512:1024].rearrange("p (h d) -> p h d", h=8)[:, :, 0:64:2]
                        RO = SC1[:, 512:1024].rearrange("p (h d) -> p h d", h=8)[:, :, 1:64:2]
                        v.tensor_tensor(S3, QE, CO, op=ALU.mult)
                        v.drain()
                        v.tensor_tensor(S4, QO, SI, op=ALU.mult)
                        v.drain()
                        v.tensor_tensor(RE, S3, S4, op=ALU.subtract)
                        v.drain()
                        v.tensor_tensor(S3, QE, SI, op=ALU.mult)
                        v.drain()
                        v.tensor_tensor(S4, QO, CO, op=ALU.mult)
                        v.drain()
                        v.tensor_tensor(RO, S3, S4, op=ALU.add)
                        v.drain()
                        v.tensor_scalar_mul(SCB[:, 0:512], SC1[:, 512:1024], scl)
                        v.drain()
                        return v.nop()
                    R("v", fp_rope)

                    def fp_tr(t, qk=qk):
                        for jj in range(4):
                            o0 = qk * 512 + jj * 128
                            ins = t.matmul(TPB[:, o0:o0 + 128],
                                           SCB[:, jj * 128:(jj + 1) * 128], EYEB[:, :],
                                           start=True, stop=True, is_transpose=True)
                        return ins
                    R("t", fp_tr)

                    def fp_ev(v, tt=tt, qk=qk):
                        DSTQK = QT if qk == 0 else KT
                        for jj in range(4):
                            o0 = qk * 512 + jj * 128
                            ins = v.tensor_copy(DSTQK[:, jj * S + tt * 128:jj * S + (tt + 1) * 128],
                                                TPB[:, o0:o0 + 128])
                        return ins
                    R("v", fp_ev)

            for hl in range(8):
                h = hg * 8 + hl
                p0 = (hl % 2) * 64
                cj = hl // 2
                ccj = hg * 4 + cj
                for kt in range(NT):
                    q0 = kt * 128
                    for b2 in range(q0 // 1024, 2):
                        lo = max(q0, b2 * 1024)
                        hi = (b2 + 1) * 1024
                        rel = b2 * 1024

                        def fs_t(t, p0=p0, cj=cj, kt=kt, q0=q0, lo=lo, hi=hi, rel=rel):
                            kl = KT[p0:p0 + 64, cj * S + kt * 128:cj * S + (kt + 1) * 128]
                            c0 = lo
                            ins = None
                            if lo == q0:
                                t.matmul(SPS[:, q0 - rel:q0 - rel + 128], kl,
                                         QT[p0:p0 + 64, cj * S + q0:cj * S + q0 + 128],
                                         start=True, stop=False)
                                ins = t.matmul(SPS[:, q0 - rel:q0 - rel + 128],
                                               EYEB[:, :], UMASK[:, :],
                                               start=False, stop=True)
                                c0 = q0 + 128
                            while c0 < hi:
                                c1 = min(((c0 - rel) // 512 + 1) * 512 + rel, hi)
                                ins = t.matmul(SPS[:, c0 - rel:c1 - rel], kl,
                                               QT[p0:p0 + 64, cj * S + c0:cj * S + c1],
                                               start=True, stop=True)
                                c0 = c1
                            return ins
                        R("t", fs_t)

                        if lo > b2 * 1024:
                            def fs_pad(v, lo=lo, rel=rel):
                                v.memset(PTB[:, 0:lo - rel], 0.0)
                                v.drain()
                                return v.nop()
                            R("v", fs_pad)

                        def fs_exp(s, lo=lo, hi=hi, rel=rel):
                            return s.activation(PTB[:, lo - rel:1024],
                                                SPS[:, lo - rel:hi - rel], AF.Exp)
                        R("s", fs_exp)

                        def fs_av(t, h=h, kt=kt, b2=b2):
                            for sb in range(2):
                                qs = b2 * 1024 + sb * 512
                                ins = t.matmul(AVP[0:65, qs:qs + 512],
                                               VA[:, (kt * 16 + h) * 65:(kt * 16 + h) * 65 + 65],
                                               PTB[:, sb * 512:sb * 512 + 512],
                                               start=(kt == 0), stop=(kt == qs // 128 + 3))
                            return ins
                        R("t", fs_av)

                for b2 in range(2):
                    def fn_rec(v, b2=b2):
                        v.reciprocal(SC1[64:65, 0:1024], AVP[64:65, b2 * 1024:b2 * 1024 + 1024])
                        v.drain()
                        v.tensor_copy(PTB[64:65, 0:1024], SC1[64:65, 0:1024])
                        v.drain()
                        return v.nop()
                    R("v", fn_rec)

                    def fn_bc(t):
                        for sb in range(2):
                            ins = t.matmul(SPS[:, sb * 512:(sb + 1) * 512], SEL64[:, :],
                                           PTB[:, sb * 512:(sb + 1) * 512], start=True, stop=True)
                        return ins
                    R("t", fn_bc)

                    def fn_out(v, p0=p0, ccj=ccj, b2=b2):
                        v.tensor_copy(SC1[:, 0:1024], SPS[:, 0:1024])
                        v.drain()
                        if p0 == 0:
                            v.tensor_tensor(CCT[0:64, ccj * S + b2 * 1024:ccj * S + b2 * 1024 + 1024],
                                            AVP[0:64, b2 * 1024:b2 * 1024 + 1024],
                                            SC1[0:64, 0:1024], op=ALU.mult)
                        else:
                            v.tensor_tensor(SCB[0:64, 0:1024],
                                            AVP[0:64, b2 * 1024:b2 * 1024 + 1024],
                                            SC1[0:64, 0:1024], op=ALU.mult)
                        v.drain()
                        return v.nop()
                    R("v", fn_out)
                    if p0 != 0:
                        R("y", mk_sdma([(CCT[64:128, ccj * S + b2 * 1024:ccj * S + b2 * 1024 + 1024],
                                         SCB[0:64, 0:1024])]))

        R("g", mk_gdma([(WSTG[:, :].rearrange("p (c j) -> p c j", c=8),
                        woT_in.rearrange("(c p) j -> p c j", p=128))]))
        for jc in range(8):
            for tsp in range(2):
                def fo_t(t, jc=jc, tsp=tsp):
                    for half in range(2):
                        t0 = (tsp * 2 + half) * 512
                        for c in range(8):
                            ins = t.matmul(SPS[:, half * 512:half * 512 + 512],
                                           WSTG[:, c * D + jc * 128:c * D + jc * 128 + 128],
                                           CCT[:, c * S + t0:c * S + t0 + 512],
                                           start=(c == 0), stop=(c == 7))
                    return ins
                R("t", fo_t)

                def fo_v(v, jc=jc, tsp=tsp):
                    for half in range(2):
                        t0 = (tsp * 2 + half) * 512
                        ins = v.tensor_scalar(ATT[:, jc * S + t0:jc * S + t0 + 512],
                                              SPS[:, half * 512:half * 512 + 512],
                                              BOF[:, jc:jc + 1], None, op0=ALU.add)
                    return ins
                R("v", fo_v)

        emit_block(runs)
        pre.close()

        g4 = contextlib.ExitStack()
        HST = g4.enter_context(nc.sbuf_tensor("HST", [128, 8 * TP1], BF16))

        xsc = contextlib.ExitStack()
        WIH = xsc.enter_context(nc.sbuf_tensor("WIH", [128, 8 * G3], BF16))
        XSC = xsc.enter_context(nc.sbuf_tensor("XSC", [128, G3], BF16))
        CXB = xsc.enter_context(nc.sbuf_tensor("CXBx", [128, 8], F32))
        xPSa = xsc.enter_context(nc.psum_tensor("xPSa", [128, 512], F32))
        xPSb = xsc.enter_context(nc.psum_tensor("xPSb", [128, 512], F32))

        runs = []
        R = lambda k, f: runs.append((k, f))
        R("g", mk_gdma([(WIH[:, :].rearrange("p (c j) -> p c j", c=8),
                        wihT_in.rearrange("(c p) j -> p c j", p=128)),
                       (CXB[:, :], ctxbf_in)]))
        for tt in range(NT):
            for spp in range(3):
                def fx_t(t, tt=tt, spp=spp):
                    for half in range(2):
                        sl = spp * 2 + half
                        ps = xPSa if half == 0 else xPSb
                        for c in range(8):
                            ins = t.matmul(ps[:, :],
                                           XNT[:, c * S + tt * 128:c * S + (tt + 1) * 128],
                                           WIH[:, c * G3 + sl * 512:c * G3 + sl * 512 + 512],
                                           start=(c == 0), stop=(c == 7))
                    return ins
                R("t", fx_t)

                def fx_v(v, spp=spp):
                    for half in range(2):
                        sl = spp * 2 + half
                        ps = xPSa if half == 0 else xPSb
                        ins = v.tensor_copy(XSC[:, sl * 512:sl * 512 + 512], ps[:, :])
                    return ins
                R("v", fx_v)
            R("y", mk_sdma([(xw_d[tt * 128:(tt + 1) * 128, :], XSC[:, :])]))

        def fz(v):
            v.memset(XSC[:, :], 0.0)
            v.drain()
            return v.nop()
        R("v", fz)
        R("y", mk_sdma([(xw_d[S:S + 1, :], XSC[0:1, :]),
                        (xw_d[S + 1:S + 2, :], XSC[1:2, :])]))

        R("g", mk_gdma([(WIH[:, 0:8 * D].rearrange("p (c j) -> p c j", c=8),
                        ctxwT_in.rearrange("(c p) j -> p c j", p=128))]))

        def fh_v1(v):
            for c in range(8):
                v.reduce_sum(MISC[:, c:c + 1], XNT[:, c * S:(c + 1) * S], axis=AX)
                v.drain()
            v.tensor_scalar_mul(MISC[:, 0:8], MISC[:, 0:8], 1.0 / S)
            v.drain()
            v.tensor_copy(XSC[:, 0:8], MISC[:, 0:8])
            v.drain()
            return v.nop()
        R("v", fh_v1)

        def fh_t(t):
            for j in range(8):
                for k in range(8):
                    ins = t.matmul(xPSa[:, j:j + 1],
                                   WIH[:, k * D + j * 128:k * D + j * 128 + 128],
                                   XSC[:, k:k + 1], start=(k == 0), stop=(k == 7))
            return ins
        R("t", fh_t)

        def fh_v2(v):
            return v.tensor_tensor(H0F[:, :], xPSa[:, 0:8], CXB[:, :], op=ALU.add)
        R("v", fh_v2)
        R("y", mk_sdma([(h0tmp_d, H0F[:, :])]))
        R("g", mk_gdma([(H0S[0:128:32, :].rearrange("g (h f) -> g h f", h=2),
                         h0tmp_d.rearrange("p (g h) -> g h p", g=4))], slow=True))

        def f_pcast(g):
            for e in range(NE):
                g.dma_start(out=gB_d[e * 16:(e + 1) * 16],
                            in_=gL_in[e * 16:(e + 1) * 16]).then_inc(PCD, 16)
                g.dma_start(out=uB_d[e * 16:(e + 1) * 16],
                            in_=uL_in[e * 16:(e + 1) * 16]).then_inc(PCD, 16)
                g.dma_start(out=dB_d[e * 8:(e + 1) * 8],
                            in_=dL_in[e * 8:(e + 1) * 8]).then_inc(PCD, 16)
            return g.nop()
        R("g", f_pcast)
        emit_block(runs)
        xsc.close()
        P_gru = ser_base[0]

        gsc = contextlib.ExitStack()
        WT = gsc.enter_context(nc.sbuf_tensor("WTg", [128, 8 * G3], BF16))
        HTR = gsc.enter_context(nc.sbuf_tensor("HTR", [128, 16], BF16))
        SGb = gsc.enter_context(nc.sbuf_tensor("SGb", [128, 1024], F32))
        NG1 = gsc.enter_context(nc.sbuf_tensor("NG1", [128, 512], F32))
        T1 = gsc.enter_context(nc.sbuf_tensor("T1g", [128, 512], F32))
        T2 = gsc.enter_context(nc.sbuf_tensor("T2g", [128, 512], F32))
        NN1 = gsc.enter_context(nc.sbuf_tensor("NN1", [128, 512], F32))
        ZH = gsc.enter_context(nc.sbuf_tensor("ZHg", [128, 512], F32))
        NT2 = gsc.enter_context(nc.sbuf_tensor("NT2", [128, 512], F32))
        SG2 = gsc.enter_context(nc.sbuf_tensor("SG2", [128, 512], F32))
        RGS = gsc.enter_context(nc.sbuf_tensor("RGS", [128, 2 * 768], BF16))
        psA = [gsc.enter_context(nc.psum_tensor(f"psA{p}", [128, 512], F32)) for p in range(2)]
        psB = [gsc.enter_context(nc.psum_tensor(f"psB{p}", [128, 256], F32)) for p in range(2)]
        psTg = [gsc.enter_context(nc.psum_tensor(f"psTg{g}", [128, 4], F32)) for g in range(4)]
        H = H0S

        with nc.Block() as blk:
            @blk.gpsimd
            def _(g):
                g.wait_ge(SER, P_gru)
                g.dma_start(out=WT[:, :].rearrange("p (c j) -> p c j", c=8),
                            in_=whhT_in.rearrange("(c p) j -> p c j", p=128)).then_inc(LD, 16)
                ld_cnt[0] += 16
                g.wait_ge(LD, ld_cnt[0])
                g.nop().then_inc(SGp, 1)
                with g.Fori(0, NB) as i:
                    for par in range(2):
                        step = i * 2 + par
                        sveb = i * 14 + par * 7 + 1
                        g.wait_ge(SV, sveb + 7)
                        col = step + 1
                        g.tensor_copy(HST[:, :].rearrange("p (c t) -> p c t", c=8)[:, :, bass.ts(col, 1)],
                                      HTR[:, par * 8:par * 8 + 8].rearrange("p (c o) -> p c o", o=1)).then_inc(SGp, 1)

            @blk.vector
            def _(v):
                v.wait_ge(SER, P_gru)
                for sb in [SGb, NG1, NT2, SG2, T2, T1, NN1, ZH]:
                    v.memset(sb[:, :], 0.0)
                v.memset(RGS[:, :], 0.0)
                for ps in psA + psB + psTg:
                    v.memset(ps[:, :], 0.0)
                v.drain()
                v.tensor_copy(HST[:, :].rearrange("p (c t) -> p c t", c=8)[:, :, 0:1],
                              H0F[:, :].rearrange("p (c o) -> p c o", o=1))
                v.tensor_copy(HTR[:, 8:16], H0F[:, :])
                v.drain()
                v.nop().then_inc(SV, 1)
                with v.Fori(0, NB) as i:
                    for par in range(2):
                        sveb = i * 14 + par * 7 + 1
                        sab = i * 6 + par * 3 + 6
                        pB = psB[par]
                        q = par * 256
                        q2 = par * 512
                        v.wait_ge(SA, sab + 1)
                        v.tensor_tensor(T1[0:97, q:q + 256], SGb[0:97, q2:q2 + 256],
                                        pB[0:97, :], op=ALU.mult).then_inc(SV, 1)
                        v.tensor_tensor(ZH[0:97, q:q + 256], SGb[0:97, q2 + 256:q2 + 512],
                                        H[0:97, :], op=ALU.mult).then_inc(SV, 1)
                        v.wait_ge(SV, sveb + 1)
                        v.tensor_tensor(T2[0:97, q:q + 256], T1[0:97, q:q + 256],
                                        RGS[0:97, par * 768 + 512:par * 768 + 768], op=ALU.add).then_inc(SV, 1)
                        v.wait_ge(SA, sab + 3)
                        v.tensor_scalar(NT2[0:97, q:q + 256], SG2[0:97, q:q + 256], 2.0, 1.0,
                                        op0=ALU.mult, op1=ALU.subtract).then_inc(SV, 1)
                        v.wait_ge(SV, sveb + 4)
                        v.tensor_tensor(NN1[0:97, q:q + 256], NT2[0:97, q:q + 256],
                                        NG1[0:97, q:q + 256], op=ALU.mult).then_inc(SV, 1)
                        v.wait_ge(SV, sveb + 5)
                        v.tensor_tensor(H[0:97, :], NN1[0:97, q:q + 256],
                                        ZH[0:97, q:q + 256], op=ALU.add).then_inc(SV, 1)
                        v.wait_ge(STp, i * 2 + par + 1)
                        for g4i in range(3):
                            v.tensor_copy(HTR[:, par * 8 + 2 * g4i:par * 8 + 2 * g4i + 2],
                                          psTg[g4i][:, par * 2:par * 2 + 2])
                        v.tensor_copy(HTR[:, par * 8 + 6:par * 8 + 8],
                                      psTg[3][:, par * 2:par * 2 + 2]).then_inc(SV, 1)

            @blk.scalar
            def _(s):
                s.wait_ge(SER, P_gru)
                s.sem_inc(SA, 6)
                with s.Fori(0, NB) as i:
                    for par in range(2):
                        sveb = i * 14 + par * 7 + 1
                        q = par * 256
                        q2 = par * 512
                        s.wait_ge(SM, i * 2 + par + 1)
                        s.activation(SGb[0:97, q2:q2 + 512], psA[par][0:97, :],
                                     AF.Sigmoid).then_inc(SA, 1)
                        s.activation(NG1[0:97, q:q + 256], psA[par][0:97, 256:512],
                                     AF.Sigmoid, scale=-1.0).then_inc(SA, 1)
                        s.wait_ge(SV, sveb + 3)
                        s.activation(SG2[0:97, q:q + 256], T2[0:97, q:q + 256],
                                     AF.Sigmoid, scale=2.0).then_inc(SA, 1)

            @blk.tensor
            def _(t):
                t.wait_ge(SER, P_gru)
                t.wait_ge(SGp, 1)
                with t.Fori(0, NB) as i:
                    for par in range(2):
                        sveb = i * 14 + par * 7 + 1
                        pA, pB = psA[par], psB[par]
                        opar = 1 - par
                        t.wait_ge(SV, sveb)
                        t.wait_ge(SA, i * 6 + par * 3 + 2)
                        for c in range(8):
                            lh = HTR[:, opar * 8 + c:opar * 8 + c + 1]
                            base = c * G3
                            for g in range(4):
                                t.matmul(pA[32 * g:32 * g + 1, 0:512], lh,
                                         WT[:, base + 512 * g:base + 512 * g + 512],
                                         start=(c == 0), stop=False, tile_position=(0, 32 * g))
                                t.matmul(pB[32 * g:32 * g + 1, 0:256], lh,
                                         WT[:, base + 2048 + 256 * g:base + 2048 + 256 * (g + 1)],
                                         start=(c == 0), stop=(c == 7), tile_position=(0, 32 * g))
                        SDp = SD0 if par == 0 else SD1
                        t.wait_ge(SDp, i * 16 + 16)
                        for g in range(4):
                            ins = t.matmul(pA[32 * g:32 * g + 1, 0:512], E32[:, g:g + 1],
                                           RGS[:, par * 768:par * 768 + 512],
                                           start=False, stop=True, tile_position=(0, 32 * g))
                        ins.then_inc(SM, 1)
                        t.wait_ge(SV, sveb + 6)
                        t.drain()
                        for c in range(8):
                            g, half = c // 2, c % 2
                            ins = t.matmul(psTg[g][:, par * 2 + half:par * 2 + half + 1],
                                           H[32 * g:32 * g + 1, 128 * half:128 * half + 128],
                                           ONE1[32 * g:32 * g + 1, 0:1],
                                           start=True, stop=True, tile_position=(32 * g, 0))
                        t.drain()
                        ins.then_inc(STp, 1)

            @blk.sync
            def _(y):
                y.wait_ge(SER, P_gru)
                for j in range(2):
                    SDp = SD0 if j % 2 == 0 else SD1
                    y.dma_start(out=RGS[0:128:32, j * 768:(j + 1) * 768],
                                in_=xw_d[j:j + 1, :].rearrange("r (g q) -> (r g) q", g=4)).then_inc(SDp, 16)
                with y.Fori(0, NB) as i:
                    for par in range(2):
                        step = i * 2 + par
                        sveb = i * 14 + par * 7 + 1
                        y.wait_ge(SV, sveb + 3)
                        r = step + 2
                        SDp = SD0 if par == 0 else SD1
                        y.dma_start(out=RGS[0:128:32, par * 768:par * 768 + 768],
                                    in_=xw_d[bass.ts(r, 1), :].rearrange("r (g q) -> (r g) q", g=4)).then_inc(SDp, 16)
        gsc.close()
        SV_END = 1 + 7 * S
        SGP_END = 1 + S

        ssc = contextlib.ExitStack()
        EXT = ssc.enter_context(nc.sbuf_tensor("EXT", [128, 8 * S], BF16))
        WES = ssc.enter_context(nc.sbuf_tensor("WES", [128, 8 * D], BF16))
        PRD = ssc.enter_context(nc.sbuf_tensor("PRD", [128, 80 * 128], BF16))
        SSB = ssc.enter_context(nc.sbuf_tensor("SSB", [128, 80], F32))
        G2B = ssc.enter_context(nc.sbuf_tensor("G2B", [128, 64], F32))
        SCs = ssc.enter_context(nc.sbuf_tensor("SCs", [128, 64], F32))
        PENR = ssc.enter_context(nc.sbuf_tensor("PENR", [128, 128], F32))
        sPS = ssc.enter_context(nc.psum_tensor("sPS", [128, 512], F32))
        sPT = ssc.enter_context(nc.psum_tensor("sPT", [128, 128], F32))

        runs = []
        R = lambda k, f: runs.append((k, f))

        def f_bridge(v):
            v.wait_ge(SV, SV_END)
            v.wait_ge(SGp, SGP_END)
            v.memset(PENR[:, :], 1.0)
            v.drain()
            return v.nop()
        R("v", f_bridge)

        R("g", mk_gdma([(WES[:, :].rearrange("p (c j) -> p c j", c=8),
                        exprwT_in.rearrange("(c p) j -> p c j", p=128))]))
        for jc in range(8):
            for tsp in range(4):
                def fe_t(t, jc=jc, tsp=tsp):
                    t0 = tsp * 512
                    for c in range(8):
                        ins = t.matmul(sPS[:, :], WES[:, c * D + jc * 128:c * D + jc * 128 + 128],
                                       XNT[:, c * S + t0:c * S + t0 + 512],
                                       start=(c == 0), stop=(c == 7))
                    return ins
                R("t", fe_t)

                def fe_v(v, jc=jc, tsp=tsp):
                    t0 = tsp * 512
                    return v.tensor_copy(EXT[:, jc * S + t0:jc * S + t0 + 512], sPS[:, :])
                R("v", fe_v)

        for tt in range(NT):
            def f_prod(v, tt=tt):
                for e in range(8):
                    for f in range(8):
                        a = HST[:, e * TP1 + 1 + tt * 128:e * TP1 + 1 + (tt + 1) * 128]
                        b = HST[:, f * TP1 + 1 + tt * 128:f * TP1 + 1 + (tt + 1) * 128]
                        v.tensor_tensor(PRD[:, (e * 8 + f) * 128:(e * 8 + f) * 128 + 128],
                                        a, b, op=ALU.mult)
                for e in range(8):
                    a = HST[:, e * TP1 + 1 + tt * 128:e * TP1 + 1 + (tt + 1) * 128]
                    xb = EXT[:, e * S + tt * 128:e * S + (tt + 1) * 128]
                    v.tensor_tensor(PRD[:, (64 + e) * 128:(64 + e) * 128 + 128],
                                    a, xb, op=ALU.mult)
                    v.tensor_tensor(PRD[:, (72 + e) * 128:(72 + e) * 128 + 128],
                                    xb, xb, op=ALU.mult)
                v.drain()
                return v.nop()
            R("v", f_prod)

            def f_sums(t):
                for idx in range(80):
                    ins = t.matmul(sPT[:, idx:idx + 1],
                                   PRD[:, idx * 128:(idx + 1) * 128], ONEB[:, :],
                                   start=True, stop=True)
                return ins
            R("t", f_sums)

            def f_ev(v):
                return v.tensor_copy(SSB[:, :], sPT[:, 0:80])
            R("v", f_ev)

            def f_sq(s):
                s.activation(MISC[:, 8:16], SSB[:, 0:72:9], AF.Sqrt)
                s.drain()
                return s.activation(MISC[:, 16:24], SSB[:, 72:80], AF.Sqrt)
            R("s", f_sq)

            def f_elt(v, tt=tt):
                v.tensor_scalar_max(MISC[:, 8:16], MISC[:, 8:16], 1e-12)
                v.drain()
                v.reciprocal(MISC[:, 24:32], MISC[:, 8:16])
                v.drain()
                for f in range(8):
                    v.tensor_scalar(G2B[:, f:64:8], SSB[:, f:64:8],
                                    MISC[:, 24 + f:25 + f], None, op0=ALU.mult)
                v.drain()
                for e in range(8):
                    v.tensor_scalar(G2B[:, 8 * e:8 * e + 8], G2B[:, 8 * e:8 * e + 8],
                                    MISC[:, 24 + e:25 + e], None, op0=ALU.mult)
                v.drain()
                v.tensor_scalar(G2B[:, 0:64:9], G2B[:, 0:64:9], 1.0, None, op0=ALU.subtract)
                v.drain()
                v.tensor_tensor(G2B[:, :], G2B[:, :], G2B[:, :], op=ALU.mult)
                v.drain()
                for e in range(8):
                    v.reduce_sum(SCs[:, e:e + 1], G2B[:, 8 * e:8 * e + 8], axis=AX)
                v.drain()
                v.tensor_scalar_max(SCs[:, 8:16], SCs[:, 0:8], 1e-24)
                v.drain()
                v.reciprocal(SCs[:, 16:24], SCs[:, 8:16])
                v.drain()
                v.tensor_tensor(SCs[:, 24:32], SCs[:, 0:8], SCs[:, 16:24], op=ALU.mult)
                v.drain()
                v.reduce_sum(PEN[:, tt:tt + 1], SCs[:, 24:32], axis=AX)
                v.drain()
                v.tensor_tensor(SCs[:, 32:40], SSB[:, 64:72], MISC[:, 24:32], op=ALU.mult)
                v.drain()
                v.tensor_tensor(SCs[:, 40:48], MISC[:, 8:16], MISC[:, 24:32], op=ALU.mult)
                v.drain()
                v.tensor_scalar_max(SCs[:, 40:48], SCs[:, 40:48], 1e-8)
                v.drain()
                v.tensor_scalar_max(SCs[:, 48:56], MISC[:, 16:24], 1e-8)
                v.drain()
                v.tensor_tensor(SCs[:, 56:64], SCs[:, 40:48], SCs[:, 48:56], op=ALU.mult)
                v.drain()
                v.reciprocal(SCs[:, 40:48], SCs[:, 56:64])
                v.drain()
                v.tensor_tensor(SCs[:, 48:56], SCs[:, 32:40], SCs[:, 40:48], op=ALU.mult)
                v.drain()
                v.tensor_scalar(CW[:, tt * 8:tt * 8 + 8], SCs[:, 48:56], -1.0, 1.0,
                                op0=ALU.mult, op1=ALU.add)
                v.drain()
                return v.nop()
            R("v", f_elt)

        def f_pt(t):
            return t.matmul(sPT[0:1, 0:16], ONE1[:, :], PEN[:, :], start=True, stop=True)
        R("t", f_pt)

        def f_pv(v):
            v.tensor_copy(MISC[0:1, 0:16], sPT[0:1, 0:16])
            v.drain()
            v.reduce_sum(MISC[0:1, 16:17], MISC[0:1, 0:16], axis=AX)
            v.drain()
            v.tensor_scalar_mul(MISC[0:1, 16:17], MISC[0:1, 16:17], 1.0 / (8.0 * S))
            v.drain()
            return v.nop()
        R("v", f_pv)

        def f_pb(v):
            v.tensor_scalar(PENR[0:1, 0:128], PENR[0:1, 0:128],
                            MISC[0:1, 16:17], None, op0=ALU.mult)
            v.drain()
            return v.nop()
        R("v", f_pb)
        R("y", mk_sdma([(pen_in_d, PENR[0:1, 0:128])]))

        def f_cc(g):
            g.collective_compute("AllReduce", ALU.add,
                                 replica_groups=[list(range(8))],
                                 ins=[pen_in_d], outs=[pen_out_d]).then_inc(CCS, 1)
            g.wait_ge(CCS, 1)
            return g.nop()
        R("g", f_cc)
        R("y", mk_sdma([(MISC[:, 17:18], pen_out_d.rearrange("o p -> p o"))], slow=True))

        def f_fac(v):
            v.tensor_scalar(MISC[:, 18:19], MISC[:, 17:18], 1.0, None, op0=ALU.add)
            v.drain()
            return v.nop()
        R("v", f_fac)

        for tt in range(NT):
            def f_cw1(v, tt=tt):
                cd = CW[:, tt * 8:tt * 8 + 8]
                v.reduce_max(MISC[:, 20:21], cd, axis=AX)
                v.drain()
                v.tensor_scalar(SCs[:, 0:8], cd, MISC[:, 20:21], None, op0=ALU.is_equal)
                v.drain()
                v.tensor_scalar(SCs[:, 8:16], SCs[:, 0:8], 1e30, None, op0=ALU.mult)
                v.drain()
                v.tensor_tensor(SCs[:, 16:24], cd, SCs[:, 8:16], op=ALU.subtract)
                v.drain()
                v.reduce_max(MISC[:, 21:22], SCs[:, 16:24], axis=AX)
                v.drain()
                v.tensor_scalar(SCs[:, 24:32], cd, MISC[:, 21:22], None, op0=ALU.is_equal)
                v.drain()
                v.tensor_tensor(MISC[:, 22:23], MISC[:, 20:21], MISC[:, 21:22], op=ALU.subtract)
                v.drain()
                return v.nop()
            R("v", f_cw1)

            def f_cw2(s):
                return s.activation(MISC[:, 23:24], MISC[:, 22:23], AF.Sigmoid,
                                    scale=MISC[:, 18:19])
            R("s", f_cw2)

            def f_cw3(v, tt=tt):
                v.tensor_scalar(MISC[:, 24:25], MISC[:, 23:24], -1.0, 1.0,
                                op0=ALU.mult, op1=ALU.add)
                v.drain()
                v.tensor_scalar(SCs[:, 32:40], SCs[:, 0:8], MISC[:, 23:24], None, op0=ALU.mult)
                v.drain()
                v.tensor_scalar(SCs[:, 40:48], SCs[:, 24:32], MISC[:, 24:25], None, op0=ALU.mult)
                v.drain()
                v.tensor_tensor(CW[:, tt * 8:tt * 8 + 8], SCs[:, 32:40],
                                SCs[:, 40:48], op=ALU.add)
                v.drain()
                return v.nop()
            R("v", f_cw3)
        emit_block(runs)
        ssc.close()
        g4.close()

        msc = contextlib.ExitStack()
        Y = msc.enter_context(nc.sbuf_tensor("Yacc", [128, 8 * 1024], F32))
        AT = msc.enter_context(nc.sbuf_tensor("ATm", [128, 16 * 1024], BF16))
        GUW = msc.enter_context(nc.sbuf_tensor("GUW", [128, 4 * 2048], BF16))
        DWN = msc.enter_context(nc.sbuf_tensor("DWN", [128, 4 * 2048], BF16))
        CWTR = msc.enter_context(nc.sbuf_tensor("CWTR", [128, S], BF16))
        CWBE = msc.enter_context(nc.sbuf_tensor("CWBE", [128, 1024], F32))
        SEL8 = msc.enter_context(nc.sbuf_tensor("SEL8", [128, 8 * 128], BF16))
        U12 = msc.enter_context(nc.sbuf_tensor("U12", [128, 3072], F32))
        CWB8 = msc.enter_context(nc.sbuf_tensor("CWB8", [128, 8], BF16))
        EYEF = msc.enter_context(nc.sbuf_tensor("EYEF", [128, 128], F32))
        psc = contextlib.ExitStack()
        tPS = psc.enter_context(nc.psum_tensor("tPS", [128, 128], BF16))

        runs = []
        R = lambda k, f: runs.append((k, f))
        consume_run = []
        chunk_dmas = []
        NCH = 2 * NE * 24

        R("y", mk_sdma([(EYEF[:, :], eyef_in), (SEL8[:, :], sel8_in)]))

        def f_cz(v):
            v.memset(CWTR[:, :], 0.0)
            v.drain()
            return v.nop()
        R("v", f_cz)
        for tt in range(NT):
            def f_ct1(v, tt=tt):
                v.tensor_copy(CWB8[:, :], CW[:, tt * 8:tt * 8 + 8])
                v.drain()
                return v.nop()
            R("v", f_ct1)

            def f_ct2(t):
                return t.matmul(tPS[0:8, 0:128], CWB8[:, :], EYEB[:, :],
                                start=True, stop=True, is_transpose=True)
            R("t", f_ct2)

            def f_ct3(v, tt=tt):
                return v.tensor_copy(CWTR[0:8, tt * 128:(tt + 1) * 128], tPS[0:8, 0:128])
            R("v", f_ct3)

        P_moe = emit_block(runs) + len(runs)
        psc.close()
        mPS8 = [msc.enter_context(nc.psum_tensor(f"mPS8_{p}", [128, 512], F32)) for p in range(8)]

        NST = es.enter_context(nc.semaphore("NST"))
        NSL = es.enter_context(nc.semaphore("NSL"))
        NAU = es.enter_context(nc.semaphore("NAU"))
        NTD = es.enter_context(nc.semaphore("NTD"))
        NDV = es.enter_context(nc.semaphore("NDV"))
        NCB = es.enter_context(nc.semaphore("NCB"))
        NXN = es.enter_context(nc.semaphore("NXN"))
        NEP = es.enter_context(nc.semaphore("NEP"))
        WDS = [es.enter_context(nc.semaphore(f"WDS{k}")) for k in range(8)]

        sched = []
        for mb in range(2):
            for e in range(NE):
                sched.append((mb, e))
        chunk_marks = []
        gi = 0
        di = 0
        for mb, e in sched:
            for fc in range(16):
                chunk_marks.append(("G", gi + 1))
                chunk_dmas.append(2)
                gi += 1
            for hc in range(8):
                chunk_marks.append(("D", di + 1))
                chunk_dmas.append(1)
                di += 1
        assert len(chunk_marks) == NCH
        wd_cum.extend(np.cumsum(chunk_dmas).tolist())
        slot_cum = []
        _cnt = [0] * 8
        for mb, e in sched:
            for fc in range(16):
                _cnt[fc % 4] += 2
                slot_cum.append((fc % 4, _cnt[fc % 4]))
            for hc in range(8):
                _cnt[4 + hc % 4] += 1
                slot_cum.append((4 + hc % 4, _cnt[4 + hc % 4]))
        EP_STEPS = 7
        EP_MB = 8 * EP_STEPS

        with nc.Block() as blk:
            @blk.tensor
            def _(t):
                t.wait_ge(SER, P_moe)
                gi = 0
                di = 0
                ci = 0
                xn = 0
                for mb, e in sched:
                    dn_before = di
                    if e < 8:
                        if gi >= 2:
                            t.wait_ge(NAU, gi - 1)
                        t.wait_ge(NDV, dn_before)
                        if mb == 1 and e == 0:
                            t.wait_ge(NEP, EP_MB)
                        for sb in range(2):
                            ins = t.matmul(mPS8[sb][:, :], SEL8[:, e * 128:(e + 1) * 128],
                                           CWTR[:, mb * 1024 + sb * 512:mb * 1024 + sb * 512 + 512],
                                           start=True, stop=True)
                        ins.then_inc(NCB, 1)
                        xn += 1
                    for fc in range(16):
                        par = fc % 2
                        s4 = fc % 4
                        t.wait_ge(WDS[slot_cum[ci][0]], 16 * slot_cum[ci][1])
                        if gi >= 2:
                            t.wait_ge(NAU, gi - 1)
                        if fc == 0:
                            t.wait_ge(NDV, dn_before)
                            if e < 8:
                                t.wait_ge(NXN, xn)
                            if mb == 1 and e == 0:
                                t.wait_ge(NEP, EP_MB)
                        for gu in range(2):
                            for half in range(2):
                                ps = mPS8[4 * par + gu * 2 + half]
                                for c in range(8):
                                    rhs = XNT[:, c * S + mb * 1024 + half * 512:
                                              c * S + mb * 1024 + half * 512 + 512]
                                    ins = t.matmul(
                                        ps[:, :],
                                        GUW[:, s4 * 2048 + gu * 1024 + c * 128:
                                             s4 * 2048 + gu * 1024 + c * 128 + 128],
                                        rhs, start=(c == 0), stop=(c == 7))
                        ins.then_inc(NST, 1)
                        gi += 1
                        ci += 1
                    aulast = gi
                    for hc in range(8):
                        q = hc % 2
                        q4 = hc % 4
                        t.wait_ge(WDS[slot_cum[ci][0]], 16 * slot_cum[ci][1])
                        if hc == 0:
                            t.wait_ge(NAU, aulast)
                        if di >= 2:
                            t.wait_ge(NDV, di - 1)
                        for half in range(2):
                            ps = mPS8[2 * q + half]
                            for fcc in range(16):
                                ins = t.matmul(ps[:, :],
                                               DWN[:, q4 * 2048 + fcc * 128:q4 * 2048 + fcc * 128 + 128],
                                               AT[:, fcc * 1024 + half * 512:fcc * 1024 + half * 512 + 512],
                                               start=(fcc == 0), stop=(fcc == 15))
                        ins.then_inc(NTD, 1)
                        di += 1
                        ci += 1
                    if e == NE - 1:
                        for lt in range(8):
                            base_i = mb * EP_MB + lt * EP_STEPS
                            for chalf in range(2):
                                t.wait_ge(NEP, base_i + 2 + chalf * 2)
                                for i in range(4):
                                    c = chalf * 4 + i
                                    ins = t.matmul(mPS8[i][:, 0:128],
                                                   U12[:, c * 128:(c + 1) * 128],
                                                   EYEF[:, :], start=True, stop=True)
                                ins.then_inc(NEP, 1)

            @blk.scalar
            def _(s):
                s.wait_ge(SER, P_moe)
                for gi in range(2 * NE * 16):
                    par = gi % 2
                    s.wait_ge(NST, gi + 1)
                    if gi >= 2:
                        s.wait_ge(NAU, gi - 1)
                    s.activation(U12[:, par * 1024:par * 1024 + 512], mPS8[4 * par][:, :], AF.Silu)
                    s.activation(U12[:, par * 1024 + 512:par * 1024 + 1024],
                                 mPS8[4 * par + 1][:, :], AF.Silu).then_inc(NSL, 1)

            @blk.vector
            def _(v):
                v.wait_ge(SER, P_moe)
                gi = 0
                di = 0
                cb = 0
                for mb, e in sched:
                    if e < 8:
                        cb += 1
                        v.wait_ge(NCB, cb)
                        v.tensor_copy(CWBE[:, 0:512], mPS8[0][:, :])
                        ins = v.tensor_copy(CWBE[:, 512:1024], mPS8[1][:, :])
                        ins.then_inc(NXN, 1)
                    for fc in range(16):
                        par = fc % 2
                        v.wait_ge(NSL, gi + 1)
                        if e < 8:
                            for half in range(2):
                                v.tensor_tensor(
                                    U12[:, 2048 + half * 512:2048 + half * 512 + 512],
                                    U12[:, par * 1024 + half * 512:par * 1024 + half * 512 + 512],
                                    mPS8[4 * par + 2 + half][:, :], op=ALU.mult)
                            v.drain()
                            for half in range(2):
                                ins = v.tensor_tensor(
                                    AT[:, fc * 1024 + half * 512:fc * 1024 + half * 512 + 512],
                                    U12[:, 2048 + half * 512:2048 + half * 512 + 512],
                                    CWBE[:, half * 512:half * 512 + 512], op=ALU.mult)
                        else:
                            for half in range(2):
                                ins = v.tensor_tensor(
                                    AT[:, fc * 1024 + half * 512:fc * 1024 + half * 512 + 512],
                                    U12[:, par * 1024 + half * 512:par * 1024 + half * 512 + 512],
                                    mPS8[4 * par + 2 + half][:, :], op=ALU.mult)
                        ins.then_inc(NAU, 1)
                        gi += 1
                    v.drain()
                    for hc in range(8):
                        q = hc % 2
                        v.wait_ge(NTD, di + 1)
                        for half in range(2):
                            dst = Y[:, hc * 1024 + half * 512:hc * 1024 + half * 512 + 512]
                            if e == 0:
                                ins = v.tensor_copy(dst, mPS8[2 * q + half][:, :])
                            else:
                                ins = v.tensor_tensor(dst, dst, mPS8[2 * q + half][:, :], op=ALU.add)
                        ins.then_inc(NDV, 1)
                        di += 1
                    if e == NE - 1:
                        for lt in range(8):
                            tt = mb * 8 + lt
                            base_i = mb * EP_MB + lt * EP_STEPS
                            v.wait_ge(NEP, base_i + 1)
                            for c in range(8):
                                v.tensor_copy(U12[:, c * 128:(c + 1) * 128],
                                              ATT[:, c * S + tt * 128:c * S + (tt + 1) * 128])
                            v.drain()
                            for c in range(8):
                                v.tensor_tensor(U12[:, c * 128:(c + 1) * 128],
                                                U12[:, c * 128:(c + 1) * 128],
                                                Y[:, c * 1024 + lt * 128:c * 1024 + (lt + 1) * 128],
                                                op=ALU.add)
                            v.drain()
                            v.nop().then_inc(NEP, 1)
                            for chalf in range(2):
                                v.wait_ge(NEP, base_i + 3 + chalf * 2)
                                for i in range(4):
                                    c = chalf * 4 + i
                                    ins = v.tensor_tensor(U12[:, 1024 + c * 128:1024 + (c + 1) * 128],
                                                          mPS8[i][:, 0:128],
                                                          U12[:, 2048 + c * 128:2048 + (c + 1) * 128],
                                                          op=ALU.add)
                                ins.then_inc(NEP, 1)

            @blk.sync
            def _(y):
                y.wait_ge(SER, P_moe)
                for mb in range(2):
                    for lt in range(8):
                        tt = mb * 8 + lt
                        base_i = mb * EP_MB + lt * EP_STEPS
                        y.wait_ge(NEP, base_i)
                        if lt == 0:
                            y.wait_ge(NDV, 72 * (mb + 1))
                        y.dma_start(out=U12[:, 2048:3072],
                                    in_=x_in[tt * 128:(tt + 1) * 128, :]).then_inc(LS, 16)
                        ls_cnt[0] += 16
                        y.wait_ge(LS, ls_cnt[0])
                        y.nop().then_inc(NEP, 1)
                        y.wait_ge(NEP, base_i + 6)
                        y.dma_start(out=out_d[tt * 128:(tt + 1) * 128, :],
                                    in_=U12[:, 1024:2048]).then_inc(LS, 16)
                        ls_cnt[0] += 16
                        y.wait_ge(LS, ls_cnt[0])
                        y.nop().then_inc(NEP, 1)

            @blk.gpsimd
            def _(g):
                g.wait_ge(SER, P_moe)
                g.wait_ge(PCD, 16 * 3 * NE)
                ci = 0
                for mb, e in sched:
                    for fc in range(16):
                        s4 = fc % 4
                        if ci >= 4:
                            kind, val = chunk_marks[ci - 4]
                            g.wait_ge(NST if kind == "G" else NTD, val)
                        g.dma_start(out=GUW[:, s4 * 2048:s4 * 2048 + 1024],
                                    in_=gB_d[e * 16 + fc]).then_inc(WDS[s4], 16)
                        g.dma_start(out=GUW[:, s4 * 2048 + 1024:s4 * 2048 + 2048],
                                    in_=uB_d[e * 16 + fc]).then_inc(WDS[s4], 16)
                        ci += 1
                    for hc in range(8):
                        q4 = hc % 4
                        if ci >= 4:
                            kind, val = chunk_marks[ci - 4]
                            g.wait_ge(NST if kind == "G" else NTD, val)
                        g.dma_start(out=DWN[:, q4 * 2048:q4 * 2048 + 2048],
                                    in_=dB_d[e * 8 + hc]).then_inc(WDS[4 + q4], 16)
                        ci += 1
        msc.close()
    return nc


PERM_W = _perm_w()
PERM_XW = _perm_xw()


def kernel(**inputs):
    xfull = np.asarray(inputs["L0_attn_x"])
    f32 = np.float32
    asc = np.ascontiguousarray
    wqT = asc(np.asarray(inputs["L0_attn_Wq"]).T.astype(f32))
    wkT = asc(np.asarray(inputs["L0_attn_Wk"]).T.astype(f32))
    wvT = asc(np.asarray(inputs["L0_attn_Wv"]).T.astype(f32))
    woT = asc(np.asarray(inputs["L0_attn_Wo"]).T.astype(f32))
    bq = np.asarray(inputs["L0_attn_bq"]).astype(f32)
    bk = np.asarray(inputs["L0_attn_bk"]).astype(f32)
    bv = np.asarray(inputs["L0_attn_bv"]).astype(f32)
    bo = np.asarray(inputs["L0_attn_bo"]).astype(f32)
    brow = asc(np.stack([bq, bk, bv, bo]))
    bo_f = asc(bo.reshape(8, 128).T)
    whhT = asc(np.asarray(inputs["L0_ffn_gru_w_hh"]).T.astype(f32)[:, PERM_W])
    wihT = asc(np.asarray(inputs["L0_ffn_gru_w_ih"]).T.astype(f32)[:, PERM_XW])
    ctxwT = asc(np.asarray(inputs["L0_ffn_ctx_w"]).T.astype(f32))
    ctxbf = asc(np.asarray(inputs["L0_ffn_ctx_b"]).astype(f32).reshape(8, 128).T)
    exprwT = asc(np.asarray(inputs["L0_ffn_expr_w"]).T.astype(f32))
    g9 = np.concatenate([np.asarray(inputs["L0_ffn_gate_w"]),
                         np.asarray(inputs["L0_ffn_shared_gate_w"])[None]],
                        axis=0).transpose(0, 2, 1).astype(f32)
    u9 = np.concatenate([np.asarray(inputs["L0_ffn_up_w"]),
                         np.asarray(inputs["L0_ffn_shared_up_w"])[None]],
                        axis=0).transpose(0, 2, 1).astype(f32)
    d9 = np.concatenate([np.asarray(inputs["L0_ffn_down_w"]),
                         np.asarray(inputs["L0_ffn_shared_down_w"])[None]],
                        axis=0).transpose(0, 2, 1).astype(f32)
    gL = asc(g9.reshape(NE, 8, 128, 16, 128).transpose(0, 3, 2, 1, 4).reshape(NE * 16, 128, 1024))
    uL = asc(u9.reshape(NE, 8, 128, 16, 128).transpose(0, 3, 2, 1, 4).reshape(NE * 16, 128, 1024))
    dL = asc(d9.reshape(NE, 16, 128, 8, 128).transpose(0, 3, 2, 1, 4).reshape(NE * 8, 128, 2048))

    inv = 1.0 / (10000.0 ** (np.arange(0, 64, 2, dtype=np.float64) / 64.0))
    ang = np.arange(S, dtype=np.float64)[:, None] * inv[None, :]
    cosE = asc(np.tile(np.cos(ang).astype(f32), (1, 8)))
    sinE = asc(np.tile(np.sin(ang).astype(f32), (1, 8)))
    eyeb = asc(np.eye(128, dtype=f32).astype(BF))
    eyef = asc(np.eye(128, dtype=f32))
    ku = np.arange(128)
    umask = asc(np.where(ku[:, None] > ku[None, :], np.float32(-1e9),
                         np.float32(0.0)).astype(BF))
    sel8 = np.zeros((128, 8 * 128), dtype=np.float32)
    for e in range(8):
        sel8[e, e * 128:(e + 1) * 128] = 1.0
    sel8 = asc(sel8.astype(BF))

    nc = build()
    in_maps = []
    for i in range(8):
        in_maps.append({
            "x": asc(xfull[i].astype(f32)),
            "wqT": wqT, "wkT": wkT, "wvT": wvT, "woT": woT,
            "brow": brow, "bo_f": bo_f,
            "whhT": whhT, "wihT": wihT,
            "ctxwT": ctxwT, "ctxbf": ctxbf,
            "exprwT": exprwT,
            "g9L": gL, "u9L": uL, "d9L": dL,
            "cosE": cosE, "sinE": sinE,
            "eyeb": eyeb, "eyef": eyef, "umask": umask, "sel8": sel8,
        })
    res = run_bass_kernel_spmd(nc, in_maps, core_ids=list(range(8)))
    out = np.stack([np.asarray(r["y"]) for r in res.results], axis=0)
    return out


if __name__ == "__main__":
    nc = build()
    print("built ok")


# version 08
# speedup vs baseline: 1.1604x; 1.0532x over previous
import contextlib
import numpy as np
import ml_dtypes
import concourse.bass as bass
import concourse.mybir as mybir
from concourse.bass_utils import run_bass_kernel_spmd

F32 = mybir.dt.float32
BF16 = mybir.dt.bfloat16
AF = mybir.ActivationFunctionType
ALU = mybir.AluOpType
AX = mybir.AxisListType.X
BF = ml_dtypes.bfloat16

S, D = 2048, 1024
NT = S // 128
G3 = 3072
TP1 = S + 1
NB = S // 2
FF = 2048
NE = 9


def _perm_w():
    idx = []
    for g in range(4):
        idx.extend(range(256 * g, 256 * (g + 1)))
        idx.extend(range(1024 + 256 * g, 1024 + 256 * (g + 1)))
    idx.extend(range(2048, 3072))
    return np.array(idx)


def _perm_xw():
    idx = []
    for g in range(4):
        idx.extend(range(256 * g, 256 * (g + 1)))
        idx.extend(range(1024 + 256 * g, 1024 + 256 * (g + 1)))
        idx.extend(range(2048 + 256 * g, 2048 + 256 * (g + 1)))
    return np.array(idx)


def build():
    nc = bass.Bass("TRN2", target_bir_lowering=False, debug=False)
    dt = nc.dram_tensor
    x_in = dt("x", [S, D], F32, kind="ExternalInput").ap()
    wqT_in = dt("wqT", [D, D], F32, kind="ExternalInput").ap()
    wkT_in = dt("wkT", [D, D], F32, kind="ExternalInput").ap()
    wvT_in = dt("wvT", [D, D], F32, kind="ExternalInput").ap()
    woT_in = dt("woT", [D, D], F32, kind="ExternalInput").ap()
    brow_in = dt("brow", [4, D], F32, kind="ExternalInput").ap()
    bo_f_in = dt("bo_f", [128, 8], F32, kind="ExternalInput").ap()
    whhT_in = dt("whhT", [D, G3], F32, kind="ExternalInput").ap()
    wihT_in = dt("wihT", [D, G3], F32, kind="ExternalInput").ap()
    ctxwT_in = dt("ctxwT", [D, D], F32, kind="ExternalInput").ap()
    ctxbf_in = dt("ctxbf", [128, 8], F32, kind="ExternalInput").ap()
    exprwT_in = dt("exprwT", [D, D], F32, kind="ExternalInput").ap()
    gL_in = dt("g9L", [NE * 16, 128, 1024], F32, kind="ExternalInput").ap()
    uL_in = dt("u9L", [NE * 16, 128, 1024], F32, kind="ExternalInput").ap()
    dL_in = dt("d9L", [NE * 8, 128, 2048], F32, kind="ExternalInput").ap()
    cosE_in = dt("cosE", [S, 256], F32, kind="ExternalInput").ap()
    sinE_in = dt("sinE", [S, 256], F32, kind="ExternalInput").ap()
    eyeb_in = dt("eyeb", [128, 128], BF16, kind="ExternalInput").ap()
    eyef_in = dt("eyef", [128, 128], F32, kind="ExternalInput").ap()
    umask_in = dt("umask", [128, 128], BF16, kind="ExternalInput").ap()
    sel8_in = dt("sel8", [128, 8 * 128], BF16, kind="ExternalInput").ap()
    out_d = dt("y", [S, D], F32, kind="ExternalOutput").ap()
    xw_d = dt("xw_scr", [S + 2, G3], BF16, kind="Internal").ap()
    gB_d = dt("gB", [NE * 16, 128, 1024], BF16, kind="Internal").ap()
    uB_d = dt("uB", [NE * 16, 128, 1024], BF16, kind="Internal").ap()
    dB_d = dt("dB", [NE * 8, 128, 2048], BF16, kind="Internal").ap()
    h0tmp_d = dt("h0tmp", [128, 8], F32, kind="Internal").ap()
    pen_in_d = dt("pen_in", [1, 128], F32, kind="Internal").ap()
    pen_out_d = dt("pen_out", [1, 128], F32, kind="Internal").ap()

    es = contextlib.ExitStack()
    with es:
        XNT = es.enter_context(nc.sbuf_tensor("XNT", [128, 8 * S], BF16))
        ATT = es.enter_context(nc.sbuf_tensor("ATT", [128, 8 * S], BF16))
        EYEB = es.enter_context(nc.sbuf_tensor("EYEB", [128, 128], BF16))
        UMASK = es.enter_context(nc.sbuf_tensor("UMASK", [128, 128], BF16))
        SELRT = es.enter_context(nc.sbuf_tensor("SELRT", [128, 128], BF16))
        SEL64 = es.enter_context(nc.sbuf_tensor("SEL64", [128, 128], BF16))
        E32 = es.enter_context(nc.sbuf_tensor("E32c", [128, 4], BF16))
        ONE1 = es.enter_context(nc.sbuf_tensor("ONE1", [128, 1], F32))
        ONEB = es.enter_context(nc.sbuf_tensor("ONEB", [128, 1], BF16))
        MISC = es.enter_context(nc.sbuf_tensor("MISC", [128, 32], F32))
        CW = es.enter_context(nc.sbuf_tensor("CWtm", [128, NT * 8], F32))
        PEN = es.enter_context(nc.sbuf_tensor("PENt", [128, NT], F32))
        H0F = es.enter_context(nc.sbuf_tensor("H0F", [128, 8], F32))
        H0S = es.enter_context(nc.sbuf_tensor("H0S", [128, 256], F32))
        BOF = es.enter_context(nc.sbuf_tensor("BOF", [128, 8], F32))
        SER = es.enter_context(nc.semaphore("SER"))
        LD = es.enter_context(nc.semaphore("LDd"))
        LS = es.enter_context(nc.semaphore("LSd"))
        SM = es.enter_context(nc.semaphore("SMg"))
        STp = es.enter_context(nc.semaphore("STg"))
        SA = es.enter_context(nc.semaphore("SAg"))
        SV = es.enter_context(nc.semaphore("SVg"))
        SD0 = es.enter_context(nc.semaphore("SD0"))
        SD1 = es.enter_context(nc.semaphore("SD1"))
        SGp = es.enter_context(nc.semaphore("SGg"))
        WD = es.enter_context(nc.semaphore("WDd"))
        CCS = es.enter_context(nc.semaphore("CCS"))
        PCD = es.enter_context(nc.semaphore("PCD"))
        SBg = es.enter_context(nc.semaphore("SBg"))

        ld_cnt = [0]
        ls_cnt = [0]
        ser_base = [0]
        wd_cum = []

        def emit_block(runs, extra_gpsimd=None):
            base = ser_base[0]
            with nc.Block() as blk:
                for ekey, attr in [("v", "vector"), ("s", "scalar"), ("t", "tensor"),
                                   ("g", "gpsimd"), ("y", "sync")]:
                    if ekey == "g" and extra_gpsimd is not None:
                        blk.gpsimd(lambda g: extra_gpsimd(g, base))
                        continue

                    def cb(eng, ekey=ekey):
                        for i, (k, fn) in enumerate(runs):
                            if k != ekey:
                                continue
                            eng.wait_ge(SER, base + i)
                            fn(eng).then_inc(SER, 1)
                    getattr(blk, attr)(cb)
            ser_base[0] = base + len(runs)
            return base

        def mk_gdma(pairs, slow=False):
            def f(g):
                for dst, src in pairs:
                    g.dma_start(out=dst, in_=src,
                                allow_slow_non_contiguous=slow).then_inc(LD, 16)
                    ld_cnt[0] += 16
                g.wait_ge(LD, ld_cnt[0])
                return g.nop()
            return f

        def mk_sdma(pairs, slow=False):
            def f(y):
                for dst, src in pairs:
                    y.dma_start(out=dst, in_=src,
                                allow_slow_non_contiguous=slow).then_inc(LS, 16)
                    ls_cnt[0] += 16
                y.wait_ge(LS, ls_cnt[0])
                return y.nop()
            return f

        pre = contextlib.ExitStack()
        SC1 = pre.enter_context(nc.sbuf_tensor("SC1", [128, D], F32))
        SC3 = pre.enter_context(nc.sbuf_tensor("SC3", [128, 256], F32))
        SC4 = pre.enter_context(nc.sbuf_tensor("SC4", [128, 256], F32))
        SCB = pre.enter_context(nc.sbuf_tensor("SCBa", [128, 1024], BF16))
        STT = pre.enter_context(nc.sbuf_tensor("STT", [128, 8], F32))
        WSTG = pre.enter_context(nc.sbuf_tensor("WSTG", [128, 8 * D], BF16))
        QT = pre.enter_context(nc.sbuf_tensor("QTf", [128, 4 * S], BF16))
        KT = pre.enter_context(nc.sbuf_tensor("KTf", [128, 4 * S], BF16))
        VA = pre.enter_context(nc.sbuf_tensor("VAf", [128, NT * 16 * 65], BF16))
        CCT = pre.enter_context(nc.sbuf_tensor("CCT", [128, 8 * S], BF16))
        PTB = pre.enter_context(nc.sbuf_tensor("PTBb", [128, 1024], BF16))
        CTRG = pre.enter_context(nc.sbuf_tensor("CTRG", [128, 512], F32))
        SPS = pre.enter_context(nc.psum_tensor("SPS", [128, 1024], F32))
        TPB = pre.enter_context(nc.psum_tensor("TPB", [128, 1024], BF16))
        AVP = pre.enter_context(nc.psum_tensor("AVP", [128, 2048], F32))

        runs = []
        R = lambda k, f: runs.append((k, f))

        R("g", mk_gdma([(EYEB[:, :], eyeb_in), (UMASK[:, :], umask_in),
                        (BOF[:, :], bo_f_in)]))

        def f_init(v):
            v.memset(SELRT[:, :], 0.0)
            v.memset(SEL64[:, :], 0.0)
            v.memset(E32[:, :], 0.0)
            v.memset(PTB[:, :], 0.0)
            v.drain()
            v.memset(SELRT[0:1, :], 1.0)
            v.memset(SEL64[64:65, :], 1.0)
            for g in range(4):
                v.memset(E32[32 * g:32 * g + 1, g:g + 1], 1.0)
            v.memset(ONE1[:, :], 1.0)
            v.memset(ONEB[:, :], 1.0)
            v.memset(H0S[:, :], 0.0)
            v.memset(VA[:, :], 1.0)
            v.drain()
            return v.nop()
        R("v", f_init)

        for tt in range(NT):
            R("g", mk_gdma([(SC1[:, :], x_in[tt * 128:(tt + 1) * 128, :])]))

            def f1(v):
                v.reduce_sum(STT[:, 0:1], SC1[:, :], axis=AX)
                v.drain()
                v.tensor_scalar_mul(STT[:, 1:2], STT[:, 0:1], 1.0 / D)
                v.drain()
                v.tensor_scalar(SC1[:, :], SC1[:, :], STT[:, 1:2], None, op0=ALU.subtract)
                v.drain()
                return v.nop()
            R("v", f1)

            def f2(s):
                return s.activation(SCB[:, 0:D], SC1[:, :], AF.Square, accum_out=STT[:, 2:3])
            R("s", f2)

            def f2b(v):
                v.tensor_scalar(STT[:, 5:6], STT[:, 2:3], 1.0 / D, 1e-5,
                                op0=ALU.mult, op1=ALU.add)
                v.drain()
                return v.nop()
            R("v", f2b)

            def f2c(s):
                return s.activation(STT[:, 3:4], STT[:, 5:6], AF.Sqrt)
            R("s", f2c)

            def f3(v):
                v.reciprocal(STT[:, 4:5], STT[:, 3:4])
                v.drain()
                v.tensor_scalar(SC1[:, :], SC1[:, :], STT[:, 4:5], None, op0=ALU.mult)
                v.drain()
                v.tensor_copy(SCB[:, 0:D], SC1[:, :])
                v.drain()
                return v.nop()
            R("v", f3)

            def f4(t):
                for i in range(8):
                    ins = t.matmul(TPB[:, i * 128:(i + 1) * 128],
                                   SCB[:, i * 128:(i + 1) * 128], EYEB[:, :],
                                   start=True, stop=True, is_transpose=True)
                return ins
            R("t", f4)

            def f5(v, tt=tt):
                for i in range(8):
                    ins = v.tensor_copy(XNT[:, i * S + tt * 128:i * S + (tt + 1) * 128],
                                        TPB[:, i * 128:(i + 1) * 128])
                return ins
            R("v", f5)

        R("g", mk_gdma([(WSTG[:, :].rearrange("p (c j) -> p c j", c=8),
                        wvT_in.rearrange("(c p) j -> p c j", p=128)),
                       (PTB[0:1, 0:D], brow_in[2:3, :])]))
        for tt in range(NT):
            def fv_t(t, tt=tt):
                for jh in range(2):
                    for c in range(8):
                        t.matmul(SPS[:, jh * 512:jh * 512 + 512],
                                 XNT[:, c * S + tt * 128:c * S + (tt + 1) * 128],
                                 WSTG[:, c * D + jh * 512:c * D + jh * 512 + 512],
                                 start=(c == 0), stop=False)
                    ins = t.matmul(SPS[:, jh * 512:jh * 512 + 512], SELRT[:, :],
                                   PTB[:, jh * 512:jh * 512 + 512],
                                   start=False, stop=True)
                return ins
            R("t", fv_t)

            def fv_v(v, tt=tt):
                for jh in range(2):
                    base = (tt * 16 + jh * 8) * 65
                    ins = v.tensor_copy(
                        VA[:, base:base + 8 * 65].rearrange("p (h e) -> p h e", h=8)[:, :, 0:64],
                        SPS[:, jh * 512:jh * 512 + 512].rearrange("p (h e) -> p h e", h=8))
                return ins
            R("v", fv_v)

        for hg in range(2):
            R("g", mk_gdma([
                (WSTG[:, 0:4096].rearrange("p (c j) -> p c j", c=8),
                 wqT_in[:, hg * 512:(hg + 1) * 512].rearrange("(c p) j -> p c j", p=128)),
                (WSTG[:, 4096:8192].rearrange("p (c j) -> p c j", c=8),
                 wkT_in[:, hg * 512:(hg + 1) * 512].rearrange("(c p) j -> p c j", p=128)),
                (PTB[0:1, 0:512], brow_in[0:1, hg * 512:(hg + 1) * 512]),
                (PTB[0:1, 512:1024], brow_in[1:2, hg * 512:(hg + 1) * 512]),
            ]))
            for tt in range(NT):
                R("y", mk_sdma([(CTRG[:, 0:256], cosE_in[tt * 128:(tt + 1) * 128, :]),
                                (CTRG[:, 256:512], sinE_in[tt * 128:(tt + 1) * 128, :])]))

                def fp_t(t, tt=tt):
                    for qk in range(2):
                        for c in range(8):
                            t.matmul(SPS[:, qk * 512:qk * 512 + 512],
                                     XNT[:, c * S + tt * 128:c * S + (tt + 1) * 128],
                                     WSTG[:, qk * 4096 + c * 512:qk * 4096 + c * 512 + 512],
                                     start=(c == 0), stop=False)
                        ins = t.matmul(SPS[:, qk * 512:qk * 512 + 512], SELRT[:, :],
                                       PTB[:, qk * 512:qk * 512 + 512],
                                       start=False, stop=True)
                    return ins
                R("t", fp_t)

                for qk in range(2):
                    scl = 0.125 if qk == 0 else 1.0

                    def fp_rope(v, qk=qk, scl=scl):
                        v.tensor_copy(SC1[:, 0:512], SPS[:, qk * 512:qk * 512 + 512])
                        v.drain()
                        QE = SC1[:, 0:512].rearrange("p (h d) -> p h d", h=8)[:, :, 0:64:2]
                        QO = SC1[:, 0:512].rearrange("p (h d) -> p h d", h=8)[:, :, 1:64:2]
                        CO = CTRG[:, 0:256].rearrange("p (h i) -> p h i", h=8)
                        SI = CTRG[:, 256:512].rearrange("p (h i) -> p h i", h=8)
                        S3 = SC3[:, :].rearrange("p (h i) -> p h i", h=8)
                        S4 = SC4[:, :].rearrange("p (h i) -> p h i", h=8)
                        RE = SC1[:, 512:1024].rearrange("p (h d) -> p h d", h=8)[:, :, 0:64:2]
                        RO = SC1[:, 512:1024].rearrange("p (h d) -> p h d", h=8)[:, :, 1:64:2]
                        v.tensor_tensor(S3, QE, CO, op=ALU.mult)
                        v.drain()
                        v.tensor_tensor(S4, QO, SI, op=ALU.mult)
                        v.drain()
                        v.tensor_tensor(RE, S3, S4, op=ALU.subtract)
                        v.drain()
                        v.tensor_tensor(S3, QE, SI, op=ALU.mult)
                        v.drain()
                        v.tensor_tensor(S4, QO, CO, op=ALU.mult)
                        v.drain()
                        v.tensor_tensor(RO, S3, S4, op=ALU.add)
                        v.drain()
                        v.tensor_scalar_mul(SCB[:, 0:512], SC1[:, 512:1024], scl)
                        v.drain()
                        return v.nop()
                    R("v", fp_rope)

                    def fp_tr(t, qk=qk):
                        for jj in range(4):
                            o0 = qk * 512 + jj * 128
                            ins = t.matmul(TPB[:, o0:o0 + 128],
                                           SCB[:, jj * 128:(jj + 1) * 128], EYEB[:, :],
                                           start=True, stop=True, is_transpose=True)
                        return ins
                    R("t", fp_tr)

                    def fp_ev(v, tt=tt, qk=qk):
                        DSTQK = QT if qk == 0 else KT
                        for jj in range(4):
                            o0 = qk * 512 + jj * 128
                            ins = v.tensor_copy(DSTQK[:, jj * S + tt * 128:jj * S + (tt + 1) * 128],
                                                TPB[:, o0:o0 + 128])
                        return ins
                    R("v", fp_ev)

            for hl in range(8):
                h = hg * 8 + hl
                p0 = (hl % 2) * 64
                cj = hl // 2
                ccj = hg * 4 + cj
                for kt in range(NT):
                    q0 = kt * 128
                    for b2 in range(q0 // 1024, 2):
                        lo = max(q0, b2 * 1024)
                        hi = (b2 + 1) * 1024
                        rel = b2 * 1024

                        def fs_t(t, p0=p0, cj=cj, kt=kt, q0=q0, lo=lo, hi=hi, rel=rel):
                            kl = KT[p0:p0 + 64, cj * S + kt * 128:cj * S + (kt + 1) * 128]
                            c0 = lo
                            ins = None
                            if lo == q0:
                                t.matmul(SPS[:, q0 - rel:q0 - rel + 128], kl,
                                         QT[p0:p0 + 64, cj * S + q0:cj * S + q0 + 128],
                                         start=True, stop=False)
                                ins = t.matmul(SPS[:, q0 - rel:q0 - rel + 128],
                                               EYEB[:, :], UMASK[:, :],
                                               start=False, stop=True)
                                c0 = q0 + 128
                            while c0 < hi:
                                c1 = min(((c0 - rel) // 512 + 1) * 512 + rel, hi)
                                ins = t.matmul(SPS[:, c0 - rel:c1 - rel], kl,
                                               QT[p0:p0 + 64, cj * S + c0:cj * S + c1],
                                               start=True, stop=True)
                                c0 = c1
                            return ins
                        R("t", fs_t)

                        if lo > b2 * 1024:
                            def fs_pad(v, lo=lo, rel=rel):
                                v.memset(PTB[:, 0:lo - rel], 0.0)
                                v.drain()
                                return v.nop()
                            R("v", fs_pad)

                        def fs_exp(s, lo=lo, hi=hi, rel=rel):
                            return s.activation(PTB[:, lo - rel:1024],
                                                SPS[:, lo - rel:hi - rel], AF.Exp)
                        R("s", fs_exp)

                        def fs_av(t, h=h, kt=kt, b2=b2):
                            for sb in range(2):
                                qs = b2 * 1024 + sb * 512
                                ins = t.matmul(AVP[0:65, qs:qs + 512],
                                               VA[:, (kt * 16 + h) * 65:(kt * 16 + h) * 65 + 65],
                                               PTB[:, sb * 512:sb * 512 + 512],
                                               start=(kt == 0), stop=(kt == qs // 128 + 3))
                            return ins
                        R("t", fs_av)

                for b2 in range(2):
                    def fn_rec(v, b2=b2):
                        v.reciprocal(SC1[64:65, 0:1024], AVP[64:65, b2 * 1024:b2 * 1024 + 1024])
                        v.drain()
                        v.tensor_copy(PTB[64:65, 0:1024], SC1[64:65, 0:1024])
                        v.drain()
                        return v.nop()
                    R("v", fn_rec)

                    def fn_bc(t):
                        for sb in range(2):
                            ins = t.matmul(SPS[:, sb * 512:(sb + 1) * 512], SEL64[:, :],
                                           PTB[:, sb * 512:(sb + 1) * 512], start=True, stop=True)
                        return ins
                    R("t", fn_bc)

                    def fn_out(v, p0=p0, ccj=ccj, b2=b2):
                        v.tensor_copy(SC1[:, 0:1024], SPS[:, 0:1024])
                        v.drain()
                        if p0 == 0:
                            v.tensor_tensor(CCT[0:64, ccj * S + b2 * 1024:ccj * S + b2 * 1024 + 1024],
                                            AVP[0:64, b2 * 1024:b2 * 1024 + 1024],
                                            SC1[0:64, 0:1024], op=ALU.mult)
                        else:
                            v.tensor_tensor(SCB[0:64, 0:1024],
                                            AVP[0:64, b2 * 1024:b2 * 1024 + 1024],
                                            SC1[0:64, 0:1024], op=ALU.mult)
                        v.drain()
                        return v.nop()
                    R("v", fn_out)
                    if p0 != 0:
                        R("y", mk_sdma([(CCT[64:128, ccj * S + b2 * 1024:ccj * S + b2 * 1024 + 1024],
                                         SCB[0:64, 0:1024])]))

        R("g", mk_gdma([(WSTG[:, :].rearrange("p (c j) -> p c j", c=8),
                        woT_in.rearrange("(c p) j -> p c j", p=128))]))
        for jc in range(8):
            for tsp in range(2):
                def fo_t(t, jc=jc, tsp=tsp):
                    for half in range(2):
                        t0 = (tsp * 2 + half) * 512
                        for c in range(8):
                            ins = t.matmul(SPS[:, half * 512:half * 512 + 512],
                                           WSTG[:, c * D + jc * 128:c * D + jc * 128 + 128],
                                           CCT[:, c * S + t0:c * S + t0 + 512],
                                           start=(c == 0), stop=(c == 7))
                    return ins
                R("t", fo_t)

                def fo_v(v, jc=jc, tsp=tsp):
                    for half in range(2):
                        t0 = (tsp * 2 + half) * 512
                        ins = v.tensor_scalar(ATT[:, jc * S + t0:jc * S + t0 + 512],
                                              SPS[:, half * 512:half * 512 + 512],
                                              BOF[:, jc:jc + 1], None, op0=ALU.add)
                    return ins
                R("v", fo_v)

        emit_block(runs)
        pre.close()

        g4 = contextlib.ExitStack()
        HST = g4.enter_context(nc.sbuf_tensor("HST", [128, 8 * TP1], BF16))

        xsc = contextlib.ExitStack()
        WIH = xsc.enter_context(nc.sbuf_tensor("WIH", [128, 8 * G3], BF16))
        XSC = xsc.enter_context(nc.sbuf_tensor("XSC", [128, G3], BF16))
        CXB = xsc.enter_context(nc.sbuf_tensor("CXBx", [128, 8], F32))
        xPSa = xsc.enter_context(nc.psum_tensor("xPSa", [128, 512], F32))
        xPSb = xsc.enter_context(nc.psum_tensor("xPSb", [128, 512], F32))

        runs = []
        R = lambda k, f: runs.append((k, f))
        R("g", mk_gdma([(WIH[:, :].rearrange("p (c j) -> p c j", c=8),
                        wihT_in.rearrange("(c p) j -> p c j", p=128)),
                       (CXB[:, :], ctxbf_in)]))
        for tt in range(NT):
            for spp in range(3):
                def fx_t(t, tt=tt, spp=spp):
                    for half in range(2):
                        sl = spp * 2 + half
                        ps = xPSa if half == 0 else xPSb
                        for c in range(8):
                            ins = t.matmul(ps[:, :],
                                           XNT[:, c * S + tt * 128:c * S + (tt + 1) * 128],
                                           WIH[:, c * G3 + sl * 512:c * G3 + sl * 512 + 512],
                                           start=(c == 0), stop=(c == 7))
                    return ins
                R("t", fx_t)

                def fx_v(v, spp=spp):
                    for half in range(2):
                        sl = spp * 2 + half
                        ps = xPSa if half == 0 else xPSb
                        ins = v.tensor_copy(XSC[:, sl * 512:sl * 512 + 512], ps[:, :])
                    return ins
                R("v", fx_v)
            R("y", mk_sdma([(xw_d[tt * 128:(tt + 1) * 128, :], XSC[:, :])]))

        def fz(v):
            v.memset(XSC[:, :], 0.0)
            v.drain()
            return v.nop()
        R("v", fz)
        R("y", mk_sdma([(xw_d[S:S + 1, :], XSC[0:1, :]),
                        (xw_d[S + 1:S + 2, :], XSC[1:2, :])]))

        R("g", mk_gdma([(WIH[:, 0:8 * D].rearrange("p (c j) -> p c j", c=8),
                        ctxwT_in.rearrange("(c p) j -> p c j", p=128))]))

        def fh_v1(v):
            for c in range(8):
                v.reduce_sum(MISC[:, c:c + 1], XNT[:, c * S:(c + 1) * S], axis=AX)
                v.drain()
            v.tensor_scalar_mul(MISC[:, 0:8], MISC[:, 0:8], 1.0 / S)
            v.drain()
            v.tensor_copy(XSC[:, 0:8], MISC[:, 0:8])
            v.drain()
            return v.nop()
        R("v", fh_v1)

        def fh_t(t):
            for j in range(8):
                for k in range(8):
                    ins = t.matmul(xPSa[:, j:j + 1],
                                   WIH[:, k * D + j * 128:k * D + j * 128 + 128],
                                   XSC[:, k:k + 1], start=(k == 0), stop=(k == 7))
            return ins
        R("t", fh_t)

        def fh_v2(v):
            return v.tensor_tensor(H0F[:, :], xPSa[:, 0:8], CXB[:, :], op=ALU.add)
        R("v", fh_v2)
        R("y", mk_sdma([(h0tmp_d, H0F[:, :])]))
        R("g", mk_gdma([(H0S[0:128:32, :].rearrange("g (h f) -> g h f", h=2),
                         h0tmp_d.rearrange("p (g h) -> g h p", g=4))], slow=True))

        def f_pcast(g):
            for e in range(NE):
                g.dma_start(out=gB_d[e * 16:(e + 1) * 16],
                            in_=gL_in[e * 16:(e + 1) * 16]).then_inc(PCD, 16)
                g.dma_start(out=uB_d[e * 16:(e + 1) * 16],
                            in_=uL_in[e * 16:(e + 1) * 16]).then_inc(PCD, 16)
                g.dma_start(out=dB_d[e * 8:(e + 1) * 8],
                            in_=dL_in[e * 8:(e + 1) * 8]).then_inc(PCD, 16)
            return g.nop()
        R("g", f_pcast)
        emit_block(runs)
        xsc.close()
        P_gru = ser_base[0]

        gsc = contextlib.ExitStack()
        WT = gsc.enter_context(nc.sbuf_tensor("WTg", [128, 8 * G3], BF16))
        HTR = gsc.enter_context(nc.sbuf_tensor("HTR", [128, 16], BF16))
        SGb = gsc.enter_context(nc.sbuf_tensor("SGb", [128, 1024], F32))
        NG1 = gsc.enter_context(nc.sbuf_tensor("NG1", [128, 512], F32))
        T1 = gsc.enter_context(nc.sbuf_tensor("T1g", [128, 512], F32))
        T2 = gsc.enter_context(nc.sbuf_tensor("T2g", [128, 512], F32))
        NN1 = gsc.enter_context(nc.sbuf_tensor("NN1", [128, 512], F32))
        ZH = gsc.enter_context(nc.sbuf_tensor("ZHg", [128, 512], F32))
        NG2 = gsc.enter_context(nc.sbuf_tensor("NG2g", [128, 512], F32))
        ZHN = gsc.enter_context(nc.sbuf_tensor("ZHNg", [128, 512], F32))
        SG2 = gsc.enter_context(nc.sbuf_tensor("SG2", [128, 512], F32))
        RGS = gsc.enter_context(nc.sbuf_tensor("RGS", [128, 2 * 768], BF16))
        psA = [gsc.enter_context(nc.psum_tensor(f"psA{p}", [128, 512], F32)) for p in range(2)]
        psB = [gsc.enter_context(nc.psum_tensor(f"psB{p}", [128, 256], F32)) for p in range(2)]
        psTg = [gsc.enter_context(nc.psum_tensor(f"psTg{g}", [128, 4], F32)) for g in range(4)]
        H = H0S

        with nc.Block() as blk:
            @blk.gpsimd
            def _(g):
                g.wait_ge(SER, P_gru)
                g.dma_start(out=WT[:, :].rearrange("p (c j) -> p c j", c=8),
                            in_=whhT_in.rearrange("(c p) j -> p c j", p=128)).then_inc(LD, 16)
                ld_cnt[0] += 16
                g.wait_ge(LD, ld_cnt[0])
                g.nop().then_inc(SGp, 1)
                with g.Fori(0, NB) as i:
                    for par in range(2):
                        step = i * 2 + par
                        sveb = i * 16 + par * 8 + 1
                        g.wait_ge(SV, sveb + 8)
                        col = step + 1
                        g.tensor_copy(HST[:, :].rearrange("p (c t) -> p c t", c=8)[:, :, bass.ts(col, 1)],
                                      HTR[:, par * 8:par * 8 + 8].rearrange("p (c o) -> p c o", o=1)).then_inc(SGp, 1)

            @blk.vector
            def _(v):
                v.wait_ge(SER, P_gru)
                for sb in [SGb, NG1, NG2, ZHN, SG2, T2, T1, NN1, ZH]:
                    v.memset(sb[:, :], 0.0)
                v.memset(RGS[:, :], 0.0)
                for ps in psA + psB + psTg:
                    v.memset(ps[:, :], 0.0)
                v.drain()
                v.tensor_copy(HST[:, :].rearrange("p (c t) -> p c t", c=8)[:, :, 0:1],
                              H0F[:, :].rearrange("p (c o) -> p c o", o=1))
                v.tensor_copy(HTR[:, 8:16], H0F[:, :])
                v.drain()
                v.nop().then_inc(SV, 1)
                with v.Fori(0, NB) as i:
                    for par in range(2):
                        sveb = i * 16 + par * 8 + 1
                        sab = i * 6 + par * 3 + 6
                        pB = psB[par]
                        q = par * 256
                        q2 = par * 512
                        v.wait_ge(SA, sab + 1)
                        v.wait_ge(SBg, i * 2 + par + 1)
                        v.tensor_tensor(T1[0:97, q:q + 256], SGb[0:97, q2:q2 + 256],
                                        pB[0:97, :], op=ALU.mult).then_inc(SV, 1)
                        v.tensor_tensor(ZH[0:97, q:q + 256], SGb[0:97, q2 + 256:q2 + 512],
                                        H[0:97, :], op=ALU.mult).then_inc(SV, 1)
                        v.wait_ge(SV, sveb + 1)
                        v.tensor_tensor(T2[0:97, q:q + 256], T1[0:97, q:q + 256],
                                        RGS[0:97, par * 768 + 512:par * 768 + 768], op=ALU.add).then_inc(SV, 1)
                        v.wait_ge(SA, sab + 2)
                        v.tensor_scalar(NG2[0:97, q:q + 256], NG1[0:97, q:q + 256], 2.0, None,
                                        op0=ALU.mult).then_inc(SV, 1)
                        v.wait_ge(SV, sveb + 2)
                        v.tensor_tensor(ZHN[0:97, q:q + 256], ZH[0:97, q:q + 256],
                                        NG1[0:97, q:q + 256], op=ALU.subtract).then_inc(SV, 1)
                        v.wait_ge(SV, sveb + 5)
                        v.wait_ge(SA, sab + 3)
                        v.tensor_tensor(NN1[0:97, q:q + 256], SG2[0:97, q:q + 256],
                                        NG2[0:97, q:q + 256], op=ALU.mult).then_inc(SV, 1)
                        v.wait_ge(SV, sveb + 6)
                        v.tensor_tensor(H[0:97, :], NN1[0:97, q:q + 256],
                                        ZHN[0:97, q:q + 256], op=ALU.add).then_inc(SV, 1)
                        v.wait_ge(STp, i * 2 + par + 1)
                        for g4i in range(3):
                            v.tensor_copy(HTR[:, par * 8 + 2 * g4i:par * 8 + 2 * g4i + 2],
                                          psTg[g4i][:, par * 2:par * 2 + 2])
                        v.tensor_copy(HTR[:, par * 8 + 6:par * 8 + 8],
                                      psTg[3][:, par * 2:par * 2 + 2]).then_inc(SV, 1)

            @blk.scalar
            def _(s):
                s.wait_ge(SER, P_gru)
                s.sem_inc(SA, 6)
                with s.Fori(0, NB) as i:
                    for par in range(2):
                        sveb = i * 16 + par * 8 + 1
                        q = par * 256
                        q2 = par * 512
                        s.wait_ge(SM, i * 2 + par + 1)
                        s.activation(SGb[0:97, q2:q2 + 512], psA[par][0:97, :],
                                     AF.Sigmoid).then_inc(SA, 1)
                        s.activation(NG1[0:97, q:q + 256], psA[par][0:97, 256:512],
                                     AF.Sigmoid, scale=-1.0).then_inc(SA, 1)
                        s.wait_ge(SV, sveb + 3)
                        s.activation(SG2[0:97, q:q + 256], T2[0:97, q:q + 256],
                                     AF.Sigmoid, scale=2.0).then_inc(SA, 1)

            @blk.tensor
            def _(t):
                t.wait_ge(SER, P_gru)
                t.wait_ge(SGp, 1)
                with t.Fori(0, NB) as i:
                    for par in range(2):
                        sveb = i * 16 + par * 8 + 1
                        pA, pB = psA[par], psB[par]
                        opar = 1 - par
                        t.wait_ge(SV, sveb)
                        t.wait_ge(SA, i * 6 + par * 3 + 2)
                        for c in range(8):
                            lh = HTR[:, opar * 8 + c:opar * 8 + c + 1]
                            base = c * G3
                            for g in range(4):
                                t.matmul(pA[32 * g:32 * g + 1, 0:512], lh,
                                         WT[:, base + 512 * g:base + 512 * g + 512],
                                         start=(c == 0), stop=False, tile_position=(0, 32 * g))
                        SDp = SD0 if par == 0 else SD1
                        t.wait_ge(SDp, i * 16 + 16)
                        for g in range(4):
                            ins = t.matmul(pA[32 * g:32 * g + 1, 0:512], E32[:, g:g + 1],
                                           RGS[:, par * 768:par * 768 + 512],
                                           start=False, stop=True, tile_position=(0, 32 * g))
                        ins.then_inc(SM, 1)
                        for c in range(8):
                            lh = HTR[:, opar * 8 + c:opar * 8 + c + 1]
                            base = c * G3
                            for g in range(4):
                                ins = t.matmul(pB[32 * g:32 * g + 1, 0:256], lh,
                                               WT[:, base + 2048 + 256 * g:base + 2048 + 256 * (g + 1)],
                                               start=(c == 0), stop=(c == 7), tile_position=(0, 32 * g))
                        ins.then_inc(SBg, 1)
                        t.wait_ge(SV, sveb + 7)
                        t.drain()
                        for c in range(8):
                            g, half = c // 2, c % 2
                            ins = t.matmul(psTg[g][:, par * 2 + half:par * 2 + half + 1],
                                           H[32 * g:32 * g + 1, 128 * half:128 * half + 128],
                                           ONE1[32 * g:32 * g + 1, 0:1],
                                           start=True, stop=True, tile_position=(32 * g, 0))
                        t.drain()
                        ins.then_inc(STp, 1)

            @blk.sync
            def _(y):
                y.wait_ge(SER, P_gru)
                for j in range(2):
                    SDp = SD0 if j % 2 == 0 else SD1
                    y.dma_start(out=RGS[0:128:32, j * 768:(j + 1) * 768],
                                in_=xw_d[j:j + 1, :].rearrange("r (g q) -> (r g) q", g=4)).then_inc(SDp, 16)
                with y.Fori(0, NB) as i:
                    for par in range(2):
                        step = i * 2 + par
                        sveb = i * 16 + par * 8 + 1
                        y.wait_ge(SV, sveb + 3)
                        r = step + 2
                        SDp = SD0 if par == 0 else SD1
                        y.dma_start(out=RGS[0:128:32, par * 768:par * 768 + 768],
                                    in_=xw_d[bass.ts(r, 1), :].rearrange("r (g q) -> (r g) q", g=4)).then_inc(SDp, 16)
        gsc.close()
        SV_END = 1 + 8 * S
        SGP_END = 1 + S

        ssc = contextlib.ExitStack()
        EXT = ssc.enter_context(nc.sbuf_tensor("EXT", [128, 8 * S], BF16))
        WES = ssc.enter_context(nc.sbuf_tensor("WES", [128, 8 * D], BF16))
        PRD = ssc.enter_context(nc.sbuf_tensor("PRD", [128, 80 * 128], BF16))
        SSB = ssc.enter_context(nc.sbuf_tensor("SSB", [128, 80], F32))
        G2B = ssc.enter_context(nc.sbuf_tensor("G2B", [128, 64], F32))
        SCs = ssc.enter_context(nc.sbuf_tensor("SCs", [128, 64], F32))
        PENR = ssc.enter_context(nc.sbuf_tensor("PENR", [128, 128], F32))
        sPS = ssc.enter_context(nc.psum_tensor("sPS", [128, 512], F32))
        sPT = ssc.enter_context(nc.psum_tensor("sPT", [128, 128], F32))

        runs = []
        R = lambda k, f: runs.append((k, f))

        def f_bridge(v):
            v.wait_ge(SV, SV_END)
            v.wait_ge(SGp, SGP_END)
            v.memset(PENR[:, :], 1.0)
            v.drain()
            return v.nop()
        R("v", f_bridge)

        R("g", mk_gdma([(WES[:, :].rearrange("p (c j) -> p c j", c=8),
                        exprwT_in.rearrange("(c p) j -> p c j", p=128))]))
        for jc in range(8):
            for tsp in range(4):
                def fe_t(t, jc=jc, tsp=tsp):
                    t0 = tsp * 512
                    for c in range(8):
                        ins = t.matmul(sPS[:, :], WES[:, c * D + jc * 128:c * D + jc * 128 + 128],
                                       XNT[:, c * S + t0:c * S + t0 + 512],
                                       start=(c == 0), stop=(c == 7))
                    return ins
                R("t", fe_t)

                def fe_v(v, jc=jc, tsp=tsp):
                    t0 = tsp * 512
                    return v.tensor_copy(EXT[:, jc * S + t0:jc * S + t0 + 512], sPS[:, :])
                R("v", fe_v)

        for tt in range(NT):
            def f_prod(v, tt=tt):
                for e in range(8):
                    for f in range(8):
                        a = HST[:, e * TP1 + 1 + tt * 128:e * TP1 + 1 + (tt + 1) * 128]
                        b = HST[:, f * TP1 + 1 + tt * 128:f * TP1 + 1 + (tt + 1) * 128]
                        v.tensor_tensor(PRD[:, (e * 8 + f) * 128:(e * 8 + f) * 128 + 128],
                                        a, b, op=ALU.mult)
                for e in range(8):
                    a = HST[:, e * TP1 + 1 + tt * 128:e * TP1 + 1 + (tt + 1) * 128]
                    xb = EXT[:, e * S + tt * 128:e * S + (tt + 1) * 128]
                    v.tensor_tensor(PRD[:, (64 + e) * 128:(64 + e) * 128 + 128],
                                    a, xb, op=ALU.mult)
                    v.tensor_tensor(PRD[:, (72 + e) * 128:(72 + e) * 128 + 128],
                                    xb, xb, op=ALU.mult)
                v.drain()
                return v.nop()
            R("v", f_prod)

            def f_sums(t):
                for idx in range(80):
                    ins = t.matmul(sPT[:, idx:idx + 1],
                                   PRD[:, idx * 128:(idx + 1) * 128], ONEB[:, :],
                                   start=True, stop=True)
                return ins
            R("t", f_sums)

            def f_ev(v):
                return v.tensor_copy(SSB[:, :], sPT[:, 0:80])
            R("v", f_ev)

            def f_sq(s):
                s.activation(MISC[:, 8:16], SSB[:, 0:72:9], AF.Sqrt)
                s.drain()
                return s.activation(MISC[:, 16:24], SSB[:, 72:80], AF.Sqrt)
            R("s", f_sq)

            def f_elt(v, tt=tt):
                v.tensor_scalar_max(MISC[:, 8:16], MISC[:, 8:16], 1e-12)
                v.drain()
                v.reciprocal(MISC[:, 24:32], MISC[:, 8:16])
                v.drain()
                for f in range(8):
                    v.tensor_scalar(G2B[:, f:64:8], SSB[:, f:64:8],
                                    MISC[:, 24 + f:25 + f], None, op0=ALU.mult)
                v.drain()
                for e in range(8):
                    v.tensor_scalar(G2B[:, 8 * e:8 * e + 8], G2B[:, 8 * e:8 * e + 8],
                                    MISC[:, 24 + e:25 + e], None, op0=ALU.mult)
                v.drain()
                v.tensor_scalar(G2B[:, 0:64:9], G2B[:, 0:64:9], 1.0, None, op0=ALU.subtract)
                v.drain()
                v.tensor_tensor(G2B[:, :], G2B[:, :], G2B[:, :], op=ALU.mult)
                v.drain()
                for e in range(8):
                    v.reduce_sum(SCs[:, e:e + 1], G2B[:, 8 * e:8 * e + 8], axis=AX)
                v.drain()
                v.tensor_scalar_max(SCs[:, 8:16], SCs[:, 0:8], 1e-24)
                v.drain()
                v.reciprocal(SCs[:, 16:24], SCs[:, 8:16])
                v.drain()
                v.tensor_tensor(SCs[:, 24:32], SCs[:, 0:8], SCs[:, 16:24], op=ALU.mult)
                v.drain()
                v.reduce_sum(PEN[:, tt:tt + 1], SCs[:, 24:32], axis=AX)
                v.drain()
                v.tensor_tensor(SCs[:, 32:40], SSB[:, 64:72], MISC[:, 24:32], op=ALU.mult)
                v.drain()
                v.tensor_tensor(SCs[:, 40:48], MISC[:, 8:16], MISC[:, 24:32], op=ALU.mult)
                v.drain()
                v.tensor_scalar_max(SCs[:, 40:48], SCs[:, 40:48], 1e-8)
                v.drain()
                v.tensor_scalar_max(SCs[:, 48:56], MISC[:, 16:24], 1e-8)
                v.drain()
                v.tensor_tensor(SCs[:, 56:64], SCs[:, 40:48], SCs[:, 48:56], op=ALU.mult)
                v.drain()
                v.reciprocal(SCs[:, 40:48], SCs[:, 56:64])
                v.drain()
                v.tensor_tensor(SCs[:, 48:56], SCs[:, 32:40], SCs[:, 40:48], op=ALU.mult)
                v.drain()
                v.tensor_scalar(CW[:, tt * 8:tt * 8 + 8], SCs[:, 48:56], -1.0, 1.0,
                                op0=ALU.mult, op1=ALU.add)
                v.drain()
                return v.nop()
            R("v", f_elt)

        def f_pt(t):
            return t.matmul(sPT[0:1, 0:16], ONE1[:, :], PEN[:, :], start=True, stop=True)
        R("t", f_pt)

        def f_pv(v):
            v.tensor_copy(MISC[0:1, 0:16], sPT[0:1, 0:16])
            v.drain()
            v.reduce_sum(MISC[0:1, 16:17], MISC[0:1, 0:16], axis=AX)
            v.drain()
            v.tensor_scalar_mul(MISC[0:1, 16:17], MISC[0:1, 16:17], 1.0 / (8.0 * S))
            v.drain()
            return v.nop()
        R("v", f_pv)

        def f_pb(v):
            v.tensor_scalar(PENR[0:1, 0:128], PENR[0:1, 0:128],
                            MISC[0:1, 16:17], None, op0=ALU.mult)
            v.drain()
            return v.nop()
        R("v", f_pb)
        R("y", mk_sdma([(pen_in_d, PENR[0:1, 0:128])]))

        def f_cc(g):
            g.collective_compute("AllReduce", ALU.add,
                                 replica_groups=[list(range(8))],
                                 ins=[pen_in_d], outs=[pen_out_d]).then_inc(CCS, 1)
            g.wait_ge(CCS, 1)
            return g.nop()
        R("g", f_cc)
        R("y", mk_sdma([(MISC[:, 17:18], pen_out_d.rearrange("o p -> p o"))], slow=True))

        def f_fac(v):
            v.tensor_scalar(MISC[:, 18:19], MISC[:, 17:18], 1.0, None, op0=ALU.add)
            v.drain()
            return v.nop()
        R("v", f_fac)

        for tt in range(NT):
            def f_cw1(v, tt=tt):
                cd = CW[:, tt * 8:tt * 8 + 8]
                v.reduce_max(MISC[:, 20:21], cd, axis=AX)
                v.drain()
                v.tensor_scalar(SCs[:, 0:8], cd, MISC[:, 20:21], None, op0=ALU.is_equal)
                v.drain()
                v.tensor_scalar(SCs[:, 8:16], SCs[:, 0:8], 1e30, None, op0=ALU.mult)
                v.drain()
                v.tensor_tensor(SCs[:, 16:24], cd, SCs[:, 8:16], op=ALU.subtract)
                v.drain()
                v.reduce_max(MISC[:, 21:22], SCs[:, 16:24], axis=AX)
                v.drain()
                v.tensor_scalar(SCs[:, 24:32], cd, MISC[:, 21:22], None, op0=ALU.is_equal)
                v.drain()
                v.tensor_tensor(MISC[:, 22:23], MISC[:, 20:21], MISC[:, 21:22], op=ALU.subtract)
                v.drain()
                return v.nop()
            R("v", f_cw1)

            def f_cw2(s):
                return s.activation(MISC[:, 23:24], MISC[:, 22:23], AF.Sigmoid,
                                    scale=MISC[:, 18:19])
            R("s", f_cw2)

            def f_cw3(v, tt=tt):
                v.tensor_scalar(MISC[:, 24:25], MISC[:, 23:24], -1.0, 1.0,
                                op0=ALU.mult, op1=ALU.add)
                v.drain()
                v.tensor_scalar(SCs[:, 32:40], SCs[:, 0:8], MISC[:, 23:24], None, op0=ALU.mult)
                v.drain()
                v.tensor_scalar(SCs[:, 40:48], SCs[:, 24:32], MISC[:, 24:25], None, op0=ALU.mult)
                v.drain()
                v.tensor_tensor(CW[:, tt * 8:tt * 8 + 8], SCs[:, 32:40],
                                SCs[:, 40:48], op=ALU.add)
                v.drain()
                return v.nop()
            R("v", f_cw3)
        emit_block(runs)
        ssc.close()
        g4.close()

        msc = contextlib.ExitStack()
        Y = msc.enter_context(nc.sbuf_tensor("Yacc", [128, 8 * 1024], F32))
        AT = msc.enter_context(nc.sbuf_tensor("ATm", [128, 16 * 1024], BF16))
        GUW = msc.enter_context(nc.sbuf_tensor("GUW", [128, 4 * 2048], BF16))
        DWN = msc.enter_context(nc.sbuf_tensor("DWN", [128, 4 * 2048], BF16))
        CWTR = msc.enter_context(nc.sbuf_tensor("CWTR", [128, S], BF16))
        CWBE = msc.enter_context(nc.sbuf_tensor("CWBE", [128, 1024], F32))
        SEL8 = msc.enter_context(nc.sbuf_tensor("SEL8", [128, 8 * 128], BF16))
        U12 = msc.enter_context(nc.sbuf_tensor("U12", [128, 3072], F32))
        CWB8 = msc.enter_context(nc.sbuf_tensor("CWB8", [128, 8], BF16))
        EYEF = msc.enter_context(nc.sbuf_tensor("EYEF", [128, 128], F32))
        psc = contextlib.ExitStack()
        tPS = psc.enter_context(nc.psum_tensor("tPS", [128, 128], BF16))

        runs = []
        R = lambda k, f: runs.append((k, f))
        consume_run = []
        chunk_dmas = []
        NCH = 2 * NE * 24

        R("y", mk_sdma([(EYEF[:, :], eyef_in), (SEL8[:, :], sel8_in)]))

        def f_cz(v):
            v.memset(CWTR[:, :], 0.0)
            v.drain()
            return v.nop()
        R("v", f_cz)
        for tt in range(NT):
            def f_ct1(v, tt=tt):
                v.tensor_copy(CWB8[:, :], CW[:, tt * 8:tt * 8 + 8])
                v.drain()
                return v.nop()
            R("v", f_ct1)

            def f_ct2(t):
                return t.matmul(tPS[0:8, 0:128], CWB8[:, :], EYEB[:, :],
                                start=True, stop=True, is_transpose=True)
            R("t", f_ct2)

            def f_ct3(v, tt=tt):
                return v.tensor_copy(CWTR[0:8, tt * 128:(tt + 1) * 128], tPS[0:8, 0:128])
            R("v", f_ct3)

        P_moe = emit_block(runs) + len(runs)
        psc.close()
        mPS8 = [msc.enter_context(nc.psum_tensor(f"mPS8_{p}", [128, 512], F32)) for p in range(8)]

        NST = es.enter_context(nc.semaphore("NST"))
        NSL = es.enter_context(nc.semaphore("NSL"))
        NAU = es.enter_context(nc.semaphore("NAU"))
        NTD = es.enter_context(nc.semaphore("NTD"))
        NDV = es.enter_context(nc.semaphore("NDV"))
        NCB = es.enter_context(nc.semaphore("NCB"))
        NXN = es.enter_context(nc.semaphore("NXN"))
        NEP = es.enter_context(nc.semaphore("NEP"))
        WDS = [es.enter_context(nc.semaphore(f"WDS{k}")) for k in range(8)]

        sched = []
        for mb in range(2):
            for e in range(NE):
                sched.append((mb, e))
        chunk_marks = []
        gi = 0
        di = 0
        for mb, e in sched:
            for fc in range(16):
                chunk_marks.append(("G", gi + 1))
                chunk_dmas.append(2)
                gi += 1
            for hc in range(8):
                chunk_marks.append(("D", di + 1))
                chunk_dmas.append(1)
                di += 1
        assert len(chunk_marks) == NCH
        wd_cum.extend(np.cumsum(chunk_dmas).tolist())
        slot_cum = []
        _cnt = [0] * 8
        for mb, e in sched:
            for fc in range(16):
                _cnt[fc % 4] += 2
                slot_cum.append((fc % 4, _cnt[fc % 4]))
            for hc in range(8):
                _cnt[4 + hc % 4] += 1
                slot_cum.append((4 + hc % 4, _cnt[4 + hc % 4]))
        EP_STEPS = 7
        EP_MB = 8 * EP_STEPS

        with nc.Block() as blk:
            @blk.tensor
            def _(t):
                t.wait_ge(SER, P_moe)
                gi = 0
                di = 0
                ci = 0
                xn = 0
                for mb, e in sched:
                    dn_before = di
                    if e < 8:
                        if gi >= 2:
                            t.wait_ge(NAU, gi - 1)
                        t.wait_ge(NDV, dn_before)
                        if mb == 1 and e == 0:
                            t.wait_ge(NEP, EP_MB)
                        for sb in range(2):
                            ins = t.matmul(mPS8[sb][:, :], SEL8[:, e * 128:(e + 1) * 128],
                                           CWTR[:, mb * 1024 + sb * 512:mb * 1024 + sb * 512 + 512],
                                           start=True, stop=True)
                        ins.then_inc(NCB, 1)
                        xn += 1
                    for fc in range(16):
                        par = fc % 2
                        s4 = fc % 4
                        t.wait_ge(WDS[slot_cum[ci][0]], 16 * slot_cum[ci][1])
                        if gi >= 2:
                            t.wait_ge(NAU, gi - 1)
                        if fc == 0:
                            t.wait_ge(NDV, dn_before)
                            if e < 8:
                                t.wait_ge(NXN, xn)
                            if mb == 1 and e == 0:
                                t.wait_ge(NEP, EP_MB)
                        for gu in range(2):
                            for half in range(2):
                                ps = mPS8[4 * par + gu * 2 + half]
                                for c in range(8):
                                    rhs = XNT[:, c * S + mb * 1024 + half * 512:
                                              c * S + mb * 1024 + half * 512 + 512]
                                    ins = t.matmul(
                                        ps[:, :],
                                        GUW[:, s4 * 2048 + gu * 1024 + c * 128:
                                             s4 * 2048 + gu * 1024 + c * 128 + 128],
                                        rhs, start=(c == 0), stop=(c == 7))
                        ins.then_inc(NST, 1)
                        gi += 1
                        ci += 1
                    aulast = gi
                    for hc in range(8):
                        q = hc % 2
                        q4 = hc % 4
                        t.wait_ge(WDS[slot_cum[ci][0]], 16 * slot_cum[ci][1])
                        if hc == 0:
                            t.wait_ge(NAU, aulast)
                        if di >= 2:
                            t.wait_ge(NDV, di - 1)
                        for half in range(2):
                            ps = mPS8[2 * q + half]
                            for fcc in range(16):
                                ins = t.matmul(ps[:, :],
                                               DWN[:, q4 * 2048 + fcc * 128:q4 * 2048 + fcc * 128 + 128],
                                               AT[:, fcc * 1024 + half * 512:fcc * 1024 + half * 512 + 512],
                                               start=(fcc == 0), stop=(fcc == 15))
                        ins.then_inc(NTD, 1)
                        di += 1
                        ci += 1
                    if e == NE - 1:
                        for lt in range(8):
                            base_i = mb * EP_MB + lt * EP_STEPS
                            for chalf in range(2):
                                t.wait_ge(NEP, base_i + 2 + chalf * 2)
                                for i in range(4):
                                    c = chalf * 4 + i
                                    ins = t.matmul(mPS8[i][:, 0:128],
                                                   U12[:, c * 128:(c + 1) * 128],
                                                   EYEF[:, :], start=True, stop=True)
                                ins.then_inc(NEP, 1)

            @blk.scalar
            def _(s):
                s.wait_ge(SER, P_moe)
                for gi in range(2 * NE * 16):
                    par = gi % 2
                    s.wait_ge(NST, gi + 1)
                    if gi >= 2:
                        s.wait_ge(NAU, gi - 1)
                    s.activation(U12[:, par * 1024:par * 1024 + 512], mPS8[4 * par][:, :], AF.Silu)
                    s.activation(U12[:, par * 1024 + 512:par * 1024 + 1024],
                                 mPS8[4 * par + 1][:, :], AF.Silu).then_inc(NSL, 1)

            @blk.vector
            def _(v):
                v.wait_ge(SER, P_moe)
                gi = 0
                di = 0
                cb = 0
                for mb, e in sched:
                    if e < 8:
                        cb += 1
                        v.wait_ge(NCB, cb)
                        v.tensor_copy(CWBE[:, 0:512], mPS8[0][:, :])
                        ins = v.tensor_copy(CWBE[:, 512:1024], mPS8[1][:, :])
                        ins.then_inc(NXN, 1)
                    for fc in range(16):
                        par = fc % 2
                        v.wait_ge(NSL, gi + 1)
                        if e < 8:
                            for half in range(2):
                                v.tensor_tensor(
                                    U12[:, 2048 + half * 512:2048 + half * 512 + 512],
                                    U12[:, par * 1024 + half * 512:par * 1024 + half * 512 + 512],
                                    mPS8[4 * par + 2 + half][:, :], op=ALU.mult)
                            v.drain()
                            for half in range(2):
                                ins = v.tensor_tensor(
                                    AT[:, fc * 1024 + half * 512:fc * 1024 + half * 512 + 512],
                                    U12[:, 2048 + half * 512:2048 + half * 512 + 512],
                                    CWBE[:, half * 512:half * 512 + 512], op=ALU.mult)
                        else:
                            for half in range(2):
                                ins = v.tensor_tensor(
                                    AT[:, fc * 1024 + half * 512:fc * 1024 + half * 512 + 512],
                                    U12[:, par * 1024 + half * 512:par * 1024 + half * 512 + 512],
                                    mPS8[4 * par + 2 + half][:, :], op=ALU.mult)
                        ins.then_inc(NAU, 1)
                        gi += 1
                    v.drain()
                    for hc in range(8):
                        q = hc % 2
                        v.wait_ge(NTD, di + 1)
                        for half in range(2):
                            dst = Y[:, hc * 1024 + half * 512:hc * 1024 + half * 512 + 512]
                            if e == 0:
                                ins = v.tensor_copy(dst, mPS8[2 * q + half][:, :])
                            else:
                                ins = v.tensor_tensor(dst, dst, mPS8[2 * q + half][:, :], op=ALU.add)
                        ins.then_inc(NDV, 1)
                        di += 1
                    if e == NE - 1:
                        for lt in range(8):
                            tt = mb * 8 + lt
                            base_i = mb * EP_MB + lt * EP_STEPS
                            v.wait_ge(NEP, base_i + 1)
                            for c in range(8):
                                v.tensor_copy(U12[:, c * 128:(c + 1) * 128],
                                              ATT[:, c * S + tt * 128:c * S + (tt + 1) * 128])
                            v.drain()
                            for c in range(8):
                                v.tensor_tensor(U12[:, c * 128:(c + 1) * 128],
                                                U12[:, c * 128:(c + 1) * 128],
                                                Y[:, c * 1024 + lt * 128:c * 1024 + (lt + 1) * 128],
                                                op=ALU.add)
                            v.drain()
                            v.nop().then_inc(NEP, 1)
                            for chalf in range(2):
                                v.wait_ge(NEP, base_i + 3 + chalf * 2)
                                for i in range(4):
                                    c = chalf * 4 + i
                                    ins = v.tensor_tensor(U12[:, 1024 + c * 128:1024 + (c + 1) * 128],
                                                          mPS8[i][:, 0:128],
                                                          U12[:, 2048 + c * 128:2048 + (c + 1) * 128],
                                                          op=ALU.add)
                                ins.then_inc(NEP, 1)

            @blk.sync
            def _(y):
                y.wait_ge(SER, P_moe)
                for mb in range(2):
                    for lt in range(8):
                        tt = mb * 8 + lt
                        base_i = mb * EP_MB + lt * EP_STEPS
                        y.wait_ge(NEP, base_i)
                        if lt == 0:
                            y.wait_ge(NDV, 72 * (mb + 1))
                        y.dma_start(out=U12[:, 2048:3072],
                                    in_=x_in[tt * 128:(tt + 1) * 128, :]).then_inc(LS, 16)
                        ls_cnt[0] += 16
                        y.wait_ge(LS, ls_cnt[0])
                        y.nop().then_inc(NEP, 1)
                        y.wait_ge(NEP, base_i + 6)
                        y.dma_start(out=out_d[tt * 128:(tt + 1) * 128, :],
                                    in_=U12[:, 1024:2048]).then_inc(LS, 16)
                        ls_cnt[0] += 16
                        y.wait_ge(LS, ls_cnt[0])
                        y.nop().then_inc(NEP, 1)

            @blk.gpsimd
            def _(g):
                g.wait_ge(SER, P_moe)
                g.wait_ge(PCD, 16 * 3 * NE)
                ci = 0
                for mb, e in sched:
                    for fc in range(16):
                        s4 = fc % 4
                        if ci >= 4:
                            kind, val = chunk_marks[ci - 4]
                            g.wait_ge(NST if kind == "G" else NTD, val)
                        g.dma_start(out=GUW[:, s4 * 2048:s4 * 2048 + 1024],
                                    in_=gB_d[e * 16 + fc]).then_inc(WDS[s4], 16)
                        g.dma_start(out=GUW[:, s4 * 2048 + 1024:s4 * 2048 + 2048],
                                    in_=uB_d[e * 16 + fc]).then_inc(WDS[s4], 16)
                        ci += 1
                    for hc in range(8):
                        q4 = hc % 4
                        if ci >= 4:
                            kind, val = chunk_marks[ci - 4]
                            g.wait_ge(NST if kind == "G" else NTD, val)
                        g.dma_start(out=DWN[:, q4 * 2048:q4 * 2048 + 2048],
                                    in_=dB_d[e * 8 + hc]).then_inc(WDS[4 + q4], 16)
                        ci += 1
        msc.close()
    return nc


PERM_W = _perm_w()
PERM_XW = _perm_xw()


def kernel(**inputs):
    xfull = np.asarray(inputs["L0_attn_x"])
    f32 = np.float32
    asc = np.ascontiguousarray
    wqT = asc(np.asarray(inputs["L0_attn_Wq"]).T.astype(f32))
    wkT = asc(np.asarray(inputs["L0_attn_Wk"]).T.astype(f32))
    wvT = asc(np.asarray(inputs["L0_attn_Wv"]).T.astype(f32))
    woT = asc(np.asarray(inputs["L0_attn_Wo"]).T.astype(f32))
    bq = np.asarray(inputs["L0_attn_bq"]).astype(f32)
    bk = np.asarray(inputs["L0_attn_bk"]).astype(f32)
    bv = np.asarray(inputs["L0_attn_bv"]).astype(f32)
    bo = np.asarray(inputs["L0_attn_bo"]).astype(f32)
    brow = asc(np.stack([bq, bk, bv, bo]))
    bo_f = asc(bo.reshape(8, 128).T)
    whhT = asc(np.asarray(inputs["L0_ffn_gru_w_hh"]).T.astype(f32)[:, PERM_W])
    wihT = asc(np.asarray(inputs["L0_ffn_gru_w_ih"]).T.astype(f32)[:, PERM_XW])
    ctxwT = asc(np.asarray(inputs["L0_ffn_ctx_w"]).T.astype(f32))
    ctxbf = asc(np.asarray(inputs["L0_ffn_ctx_b"]).astype(f32).reshape(8, 128).T)
    exprwT = asc(np.asarray(inputs["L0_ffn_expr_w"]).T.astype(f32))
    g9 = np.concatenate([np.asarray(inputs["L0_ffn_gate_w"]),
                         np.asarray(inputs["L0_ffn_shared_gate_w"])[None]],
                        axis=0).transpose(0, 2, 1).astype(f32)
    u9 = np.concatenate([np.asarray(inputs["L0_ffn_up_w"]),
                         np.asarray(inputs["L0_ffn_shared_up_w"])[None]],
                        axis=0).transpose(0, 2, 1).astype(f32)
    d9 = np.concatenate([np.asarray(inputs["L0_ffn_down_w"]),
                         np.asarray(inputs["L0_ffn_shared_down_w"])[None]],
                        axis=0).transpose(0, 2, 1).astype(f32)
    gL = asc(g9.reshape(NE, 8, 128, 16, 128).transpose(0, 3, 2, 1, 4).reshape(NE * 16, 128, 1024))
    uL = asc(u9.reshape(NE, 8, 128, 16, 128).transpose(0, 3, 2, 1, 4).reshape(NE * 16, 128, 1024))
    dL = asc(d9.reshape(NE, 16, 128, 8, 128).transpose(0, 3, 2, 1, 4).reshape(NE * 8, 128, 2048))

    inv = 1.0 / (10000.0 ** (np.arange(0, 64, 2, dtype=np.float64) / 64.0))
    ang = np.arange(S, dtype=np.float64)[:, None] * inv[None, :]
    cosE = asc(np.tile(np.cos(ang).astype(f32), (1, 8)))
    sinE = asc(np.tile(np.sin(ang).astype(f32), (1, 8)))
    eyeb = asc(np.eye(128, dtype=f32).astype(BF))
    eyef = asc(np.eye(128, dtype=f32))
    ku = np.arange(128)
    umask = asc(np.where(ku[:, None] > ku[None, :], np.float32(-1e9),
                         np.float32(0.0)).astype(BF))
    sel8 = np.zeros((128, 8 * 128), dtype=np.float32)
    for e in range(8):
        sel8[e, e * 128:(e + 1) * 128] = 1.0
    sel8 = asc(sel8.astype(BF))

    nc = build()
    in_maps = []
    for i in range(8):
        in_maps.append({
            "x": asc(xfull[i].astype(f32)),
            "wqT": wqT, "wkT": wkT, "wvT": wvT, "woT": woT,
            "brow": brow, "bo_f": bo_f,
            "whhT": whhT, "wihT": wihT,
            "ctxwT": ctxwT, "ctxbf": ctxbf,
            "exprwT": exprwT,
            "g9L": gL, "u9L": uL, "d9L": dL,
            "cosE": cosE, "sinE": sinE,
            "eyeb": eyeb, "eyef": eyef, "umask": umask, "sel8": sel8,
        })
    res = run_bass_kernel_spmd(nc, in_maps, core_ids=list(range(8)))
    out = np.stack([np.asarray(r["y"]) for r in res.results], axis=0)
    return out


if __name__ == "__main__":
    nc = build()
    print("built ok")


# version 09 (final)
# speedup vs baseline: 1.1702x; 1.0084x over previous
import contextlib
import numpy as np
import ml_dtypes
import concourse.bass as bass
import concourse.mybir as mybir
from concourse.bass_utils import run_bass_kernel_spmd

F32 = mybir.dt.float32
BF16 = mybir.dt.bfloat16
AF = mybir.ActivationFunctionType
ALU = mybir.AluOpType
AX = mybir.AxisListType.X
BF = ml_dtypes.bfloat16

S, D = 2048, 1024
NT = S // 128
G3 = 3072
TP1 = S + 1
NB = S // 2
FF = 2048
NE = 9


def _perm_w():
    idx = []
    for g in range(4):
        idx.extend(range(256 * g, 256 * (g + 1)))
        idx.extend(range(1024 + 256 * g, 1024 + 256 * (g + 1)))
    idx.extend(range(2048, 3072))
    return np.array(idx)


def _perm_xw():
    idx = []
    for g in range(4):
        idx.extend(range(256 * g, 256 * (g + 1)))
        idx.extend(range(1024 + 256 * g, 1024 + 256 * (g + 1)))
        idx.extend(range(2048 + 256 * g, 2048 + 256 * (g + 1)))
    return np.array(idx)


def build():
    nc = bass.Bass("TRN2", target_bir_lowering=False, debug=False)
    dt = nc.dram_tensor
    x_in = dt("x", [S, D], F32, kind="ExternalInput").ap()
    wqT_in = dt("wqT", [D, D], F32, kind="ExternalInput").ap()
    wkT_in = dt("wkT", [D, D], F32, kind="ExternalInput").ap()
    wvT_in = dt("wvT", [D, D], F32, kind="ExternalInput").ap()
    woT_in = dt("woT", [D, D], F32, kind="ExternalInput").ap()
    brow_in = dt("brow", [4, D], F32, kind="ExternalInput").ap()
    bo_f_in = dt("bo_f", [128, 8], F32, kind="ExternalInput").ap()
    whhT_in = dt("whhT", [D, G3], F32, kind="ExternalInput").ap()
    wihT_in = dt("wihT", [D, G3], F32, kind="ExternalInput").ap()
    ctxwT_in = dt("ctxwT", [D, D], F32, kind="ExternalInput").ap()
    ctxbf_in = dt("ctxbf", [128, 8], F32, kind="ExternalInput").ap()
    exprwT_in = dt("exprwT", [D, D], F32, kind="ExternalInput").ap()
    gL_in = dt("g9L", [NE * 16, 128, 1024], F32, kind="ExternalInput").ap()
    uL_in = dt("u9L", [NE * 16, 128, 1024], F32, kind="ExternalInput").ap()
    dL_in = dt("d9L", [NE * 8, 128, 2048], F32, kind="ExternalInput").ap()
    cosE_in = dt("cosE", [S, 256], F32, kind="ExternalInput").ap()
    sinE_in = dt("sinE", [S, 256], F32, kind="ExternalInput").ap()
    eyeb_in = dt("eyeb", [128, 128], BF16, kind="ExternalInput").ap()
    eyef_in = dt("eyef", [128, 128], F32, kind="ExternalInput").ap()
    umask_in = dt("umask", [128, 128], BF16, kind="ExternalInput").ap()
    sel8_in = dt("sel8", [128, 8 * 128], BF16, kind="ExternalInput").ap()
    out_d = dt("y", [S, D], F32, kind="ExternalOutput").ap()
    xw_d = dt("xw_scr", [S + 2, G3], BF16, kind="Internal").ap()
    gB_d = dt("gB", [NE * 16, 128, 1024], BF16, kind="Internal").ap()
    uB_d = dt("uB", [NE * 16, 128, 1024], BF16, kind="Internal").ap()
    dB_d = dt("dB", [NE * 8, 128, 2048], BF16, kind="Internal").ap()
    h0tmp_d = dt("h0tmp", [128, 8], F32, kind="Internal").ap()
    pen_in_d = dt("pen_in", [1, 128], F32, kind="Internal").ap()
    pen_out_d = dt("pen_out", [1, 128], F32, kind="Internal").ap()

    es = contextlib.ExitStack()
    with es:
        XNT = es.enter_context(nc.sbuf_tensor("XNT", [128, 8 * S], BF16))
        ATT = es.enter_context(nc.sbuf_tensor("ATT", [128, 8 * S], BF16))
        EYEB = es.enter_context(nc.sbuf_tensor("EYEB", [128, 128], BF16))
        UMASK = es.enter_context(nc.sbuf_tensor("UMASK", [128, 128], BF16))
        SELRT = es.enter_context(nc.sbuf_tensor("SELRT", [128, 128], BF16))
        SEL64 = es.enter_context(nc.sbuf_tensor("SEL64", [128, 128], BF16))
        E32 = es.enter_context(nc.sbuf_tensor("E32c", [128, 4], BF16))
        ONE1 = es.enter_context(nc.sbuf_tensor("ONE1", [128, 1], F32))
        ONEB = es.enter_context(nc.sbuf_tensor("ONEB", [128, 1], BF16))
        MISC = es.enter_context(nc.sbuf_tensor("MISC", [128, 32], F32))
        CW = es.enter_context(nc.sbuf_tensor("CWtm", [128, NT * 8], F32))
        PEN = es.enter_context(nc.sbuf_tensor("PENt", [128, NT], F32))
        H0F = es.enter_context(nc.sbuf_tensor("H0F", [128, 8], F32))
        H0S = es.enter_context(nc.sbuf_tensor("H0S", [128, 256], F32))
        BOF = es.enter_context(nc.sbuf_tensor("BOF", [128, 8], F32))
        SER = es.enter_context(nc.semaphore("SER"))
        LD = es.enter_context(nc.semaphore("LDd"))
        LS = es.enter_context(nc.semaphore("LSd"))
        SM = es.enter_context(nc.semaphore("SMg"))
        STp = es.enter_context(nc.semaphore("STg"))
        SA = es.enter_context(nc.semaphore("SAg"))
        SV = es.enter_context(nc.semaphore("SVg"))
        SD0 = es.enter_context(nc.semaphore("SD0"))
        SD1 = es.enter_context(nc.semaphore("SD1"))
        SGp = es.enter_context(nc.semaphore("SGg"))
        WD = es.enter_context(nc.semaphore("WDd"))
        CCS = es.enter_context(nc.semaphore("CCS"))
        PCD = es.enter_context(nc.semaphore("PCD"))
        SBg = es.enter_context(nc.semaphore("SBg"))

        ld_cnt = [0]
        ls_cnt = [0]
        ser_base = [0]
        wd_cum = []

        def emit_block(runs, extra_gpsimd=None):
            base = ser_base[0]
            with nc.Block() as blk:
                for ekey, attr in [("v", "vector"), ("s", "scalar"), ("t", "tensor"),
                                   ("g", "gpsimd"), ("y", "sync")]:
                    if ekey == "g" and extra_gpsimd is not None:
                        blk.gpsimd(lambda g: extra_gpsimd(g, base))
                        continue

                    def cb(eng, ekey=ekey):
                        for i, (k, fn) in enumerate(runs):
                            if k != ekey:
                                continue
                            eng.wait_ge(SER, base + i)
                            fn(eng).then_inc(SER, 1)
                    getattr(blk, attr)(cb)
            ser_base[0] = base + len(runs)
            return base

        def mk_gdma(pairs, slow=False):
            def f(g):
                for dst, src in pairs:
                    g.dma_start(out=dst, in_=src,
                                allow_slow_non_contiguous=slow).then_inc(LD, 16)
                    ld_cnt[0] += 16
                g.wait_ge(LD, ld_cnt[0])
                return g.nop()
            return f

        def mk_sdma(pairs, slow=False):
            def f(y):
                for dst, src in pairs:
                    y.dma_start(out=dst, in_=src,
                                allow_slow_non_contiguous=slow).then_inc(LS, 16)
                    ls_cnt[0] += 16
                y.wait_ge(LS, ls_cnt[0])
                return y.nop()
            return f

        pre = contextlib.ExitStack()
        SC1 = pre.enter_context(nc.sbuf_tensor("SC1", [128, D], F32))
        SC3 = pre.enter_context(nc.sbuf_tensor("SC3", [128, 256], F32))
        SC4 = pre.enter_context(nc.sbuf_tensor("SC4", [128, 256], F32))
        SCB = pre.enter_context(nc.sbuf_tensor("SCBa", [128, 1024], BF16))
        STT = pre.enter_context(nc.sbuf_tensor("STT", [128, 8], F32))
        WSTG = pre.enter_context(nc.sbuf_tensor("WSTG", [128, 8 * D], BF16))
        QT = pre.enter_context(nc.sbuf_tensor("QTf", [128, 4 * S], BF16))
        KT = pre.enter_context(nc.sbuf_tensor("KTf", [128, 4 * S], BF16))
        VA = pre.enter_context(nc.sbuf_tensor("VAf", [128, NT * 16 * 65], BF16))
        CCT = pre.enter_context(nc.sbuf_tensor("CCT", [128, 8 * S], BF16))
        PTB = pre.enter_context(nc.sbuf_tensor("PTBb", [128, 1024], BF16))
        CTRG = pre.enter_context(nc.sbuf_tensor("CTRG", [128, 512], F32))
        SPS = pre.enter_context(nc.psum_tensor("SPS", [128, 1024], F32))
        TPB = pre.enter_context(nc.psum_tensor("TPB", [128, 1024], BF16))
        AVP = pre.enter_context(nc.psum_tensor("AVP", [128, 2048], F32))

        runs = []
        R = lambda k, f: runs.append((k, f))

        R("g", mk_gdma([(EYEB[:, :], eyeb_in), (UMASK[:, :], umask_in),
                        (BOF[:, :], bo_f_in)]))

        def f_init(v):
            v.memset(SELRT[:, :], 0.0)
            v.memset(SEL64[:, :], 0.0)
            v.memset(E32[:, :], 0.0)
            v.memset(PTB[:, :], 0.0)
            v.drain()
            v.memset(SELRT[0:1, :], 1.0)
            v.memset(SEL64[64:65, :], 1.0)
            for g in range(4):
                v.memset(E32[32 * g:32 * g + 1, g:g + 1], 1.0)
            v.memset(ONE1[:, :], 1.0)
            v.memset(ONEB[:, :], 1.0)
            v.memset(H0S[:, :], 0.0)
            v.memset(VA[:, :], 1.0)
            v.drain()
            return v.nop()
        R("v", f_init)

        for tt in range(NT):
            R("g", mk_gdma([(SC1[:, :], x_in[tt * 128:(tt + 1) * 128, :])]))

            def f1(v):
                v.reduce_sum(STT[:, 0:1], SC1[:, :], axis=AX)
                v.drain()
                v.tensor_scalar_mul(STT[:, 1:2], STT[:, 0:1], 1.0 / D)
                v.drain()
                v.tensor_scalar(SC1[:, :], SC1[:, :], STT[:, 1:2], None, op0=ALU.subtract)
                v.drain()
                return v.nop()
            R("v", f1)

            def f2(s):
                return s.activation(SCB[:, 0:D], SC1[:, :], AF.Square, accum_out=STT[:, 2:3])
            R("s", f2)

            def f2b(v):
                v.tensor_scalar(STT[:, 5:6], STT[:, 2:3], 1.0 / D, 1e-5,
                                op0=ALU.mult, op1=ALU.add)
                v.drain()
                return v.nop()
            R("v", f2b)

            def f2c(s):
                return s.activation(STT[:, 3:4], STT[:, 5:6], AF.Sqrt)
            R("s", f2c)

            def f3(v):
                v.reciprocal(STT[:, 4:5], STT[:, 3:4])
                v.drain()
                v.tensor_scalar(SC1[:, :], SC1[:, :], STT[:, 4:5], None, op0=ALU.mult)
                v.drain()
                v.tensor_copy(SCB[:, 0:D], SC1[:, :])
                v.drain()
                return v.nop()
            R("v", f3)

            def f4(t):
                for i in range(8):
                    ins = t.matmul(TPB[:, i * 128:(i + 1) * 128],
                                   SCB[:, i * 128:(i + 1) * 128], EYEB[:, :],
                                   start=True, stop=True, is_transpose=True)
                return ins
            R("t", f4)

            def f5(v, tt=tt):
                for i in range(8):
                    ins = v.tensor_copy(XNT[:, i * S + tt * 128:i * S + (tt + 1) * 128],
                                        TPB[:, i * 128:(i + 1) * 128])
                return ins
            R("v", f5)

        R("g", mk_gdma([(WSTG[:, :].rearrange("p (c j) -> p c j", c=8),
                        wvT_in.rearrange("(c p) j -> p c j", p=128)),
                       (PTB[0:1, 0:D], brow_in[2:3, :])]))
        for tt in range(NT):
            def fv_t(t, tt=tt):
                for jh in range(2):
                    for c in range(8):
                        t.matmul(SPS[:, jh * 512:jh * 512 + 512],
                                 XNT[:, c * S + tt * 128:c * S + (tt + 1) * 128],
                                 WSTG[:, c * D + jh * 512:c * D + jh * 512 + 512],
                                 start=(c == 0), stop=False)
                    ins = t.matmul(SPS[:, jh * 512:jh * 512 + 512], SELRT[:, :],
                                   PTB[:, jh * 512:jh * 512 + 512],
                                   start=False, stop=True)
                return ins
            R("t", fv_t)

            def fv_v(v, tt=tt):
                for jh in range(2):
                    base = (tt * 16 + jh * 8) * 65
                    ins = v.tensor_copy(
                        VA[:, base:base + 8 * 65].rearrange("p (h e) -> p h e", h=8)[:, :, 0:64],
                        SPS[:, jh * 512:jh * 512 + 512].rearrange("p (h e) -> p h e", h=8))
                return ins
            R("v", fv_v)

        for hg in range(2):
            R("g", mk_gdma([
                (WSTG[:, 0:4096].rearrange("p (c j) -> p c j", c=8),
                 wqT_in[:, hg * 512:(hg + 1) * 512].rearrange("(c p) j -> p c j", p=128)),
                (WSTG[:, 4096:8192].rearrange("p (c j) -> p c j", c=8),
                 wkT_in[:, hg * 512:(hg + 1) * 512].rearrange("(c p) j -> p c j", p=128)),
                (PTB[0:1, 0:512], brow_in[0:1, hg * 512:(hg + 1) * 512]),
                (PTB[0:1, 512:1024], brow_in[1:2, hg * 512:(hg + 1) * 512]),
            ]))
            for tt in range(NT):
                R("y", mk_sdma([(CTRG[:, 0:256], cosE_in[tt * 128:(tt + 1) * 128, :]),
                                (CTRG[:, 256:512], sinE_in[tt * 128:(tt + 1) * 128, :])]))

                def fp_t(t, tt=tt):
                    for qk in range(2):
                        for c in range(8):
                            t.matmul(SPS[:, qk * 512:qk * 512 + 512],
                                     XNT[:, c * S + tt * 128:c * S + (tt + 1) * 128],
                                     WSTG[:, qk * 4096 + c * 512:qk * 4096 + c * 512 + 512],
                                     start=(c == 0), stop=False)
                        ins = t.matmul(SPS[:, qk * 512:qk * 512 + 512], SELRT[:, :],
                                       PTB[:, qk * 512:qk * 512 + 512],
                                       start=False, stop=True)
                    return ins
                R("t", fp_t)

                for qk in range(2):
                    scl = 0.125 if qk == 0 else 1.0

                    def fp_rope(v, qk=qk, scl=scl):
                        v.tensor_copy(SC1[:, 0:512], SPS[:, qk * 512:qk * 512 + 512])
                        v.drain()
                        QE = SC1[:, 0:512].rearrange("p (h d) -> p h d", h=8)[:, :, 0:64:2]
                        QO = SC1[:, 0:512].rearrange("p (h d) -> p h d", h=8)[:, :, 1:64:2]
                        CO = CTRG[:, 0:256].rearrange("p (h i) -> p h i", h=8)
                        SI = CTRG[:, 256:512].rearrange("p (h i) -> p h i", h=8)
                        S3 = SC3[:, :].rearrange("p (h i) -> p h i", h=8)
                        S4 = SC4[:, :].rearrange("p (h i) -> p h i", h=8)
                        RE = SC1[:, 512:1024].rearrange("p (h d) -> p h d", h=8)[:, :, 0:64:2]
                        RO = SC1[:, 512:1024].rearrange("p (h d) -> p h d", h=8)[:, :, 1:64:2]
                        v.tensor_tensor(S3, QE, CO, op=ALU.mult)
                        v.drain()
                        v.tensor_tensor(S4, QO, SI, op=ALU.mult)
                        v.drain()
                        v.tensor_tensor(RE, S3, S4, op=ALU.subtract)
                        v.drain()
                        v.tensor_tensor(S3, QE, SI, op=ALU.mult)
                        v.drain()
                        v.tensor_tensor(S4, QO, CO, op=ALU.mult)
                        v.drain()
                        v.tensor_tensor(RO, S3, S4, op=ALU.add)
                        v.drain()
                        v.tensor_scalar_mul(SCB[:, 0:512], SC1[:, 512:1024], scl)
                        v.drain()
                        return v.nop()
                    R("v", fp_rope)

                    def fp_tr(t, qk=qk):
                        for jj in range(4):
                            o0 = qk * 512 + jj * 128
                            ins = t.matmul(TPB[:, o0:o0 + 128],
                                           SCB[:, jj * 128:(jj + 1) * 128], EYEB[:, :],
                                           start=True, stop=True, is_transpose=True)
                        return ins
                    R("t", fp_tr)

                    def fp_ev(v, tt=tt, qk=qk):
                        DSTQK = QT if qk == 0 else KT
                        for jj in range(4):
                            o0 = qk * 512 + jj * 128
                            ins = v.tensor_copy(DSTQK[:, jj * S + tt * 128:jj * S + (tt + 1) * 128],
                                                TPB[:, o0:o0 + 128])
                        return ins
                    R("v", fp_ev)

            for hl in range(8):
                h = hg * 8 + hl
                p0 = (hl % 2) * 64
                cj = hl // 2
                ccj = hg * 4 + cj
                for kt in range(NT):
                    q0 = kt * 128
                    for b2 in range(q0 // 1024, 2):
                        lo = max(q0, b2 * 1024)
                        hi = (b2 + 1) * 1024
                        rel = b2 * 1024

                        def fs_t(t, p0=p0, cj=cj, kt=kt, q0=q0, lo=lo, hi=hi, rel=rel):
                            kl = KT[p0:p0 + 64, cj * S + kt * 128:cj * S + (kt + 1) * 128]
                            c0 = lo
                            ins = None
                            if lo == q0:
                                t.matmul(SPS[:, q0 - rel:q0 - rel + 128], kl,
                                         QT[p0:p0 + 64, cj * S + q0:cj * S + q0 + 128],
                                         start=True, stop=False)
                                ins = t.matmul(SPS[:, q0 - rel:q0 - rel + 128],
                                               EYEB[:, :], UMASK[:, :],
                                               start=False, stop=True)
                                c0 = q0 + 128
                            while c0 < hi:
                                c1 = min(((c0 - rel) // 512 + 1) * 512 + rel, hi)
                                ins = t.matmul(SPS[:, c0 - rel:c1 - rel], kl,
                                               QT[p0:p0 + 64, cj * S + c0:cj * S + c1],
                                               start=True, stop=True)
                                c0 = c1
                            return ins
                        R("t", fs_t)

                        if lo > b2 * 1024:
                            def fs_pad(v, lo=lo, rel=rel):
                                v.memset(PTB[:, 0:lo - rel], 0.0)
                                v.drain()
                                return v.nop()
                            R("v", fs_pad)

                        def fs_exp(s, lo=lo, hi=hi, rel=rel):
                            return s.activation(PTB[:, lo - rel:1024],
                                                SPS[:, lo - rel:hi - rel], AF.Exp)
                        R("s", fs_exp)

                        def fs_av(t, h=h, kt=kt, b2=b2):
                            for sb in range(2):
                                qs = b2 * 1024 + sb * 512
                                ins = t.matmul(AVP[0:65, qs:qs + 512],
                                               VA[:, (kt * 16 + h) * 65:(kt * 16 + h) * 65 + 65],
                                               PTB[:, sb * 512:sb * 512 + 512],
                                               start=(kt == 0), stop=(kt == qs // 128 + 3))
                            return ins
                        R("t", fs_av)

                for b2 in range(2):
                    def fn_rec(v, b2=b2):
                        v.reciprocal(SC1[64:65, 0:1024], AVP[64:65, b2 * 1024:b2 * 1024 + 1024])
                        v.drain()
                        v.tensor_copy(PTB[64:65, 0:1024], SC1[64:65, 0:1024])
                        v.drain()
                        return v.nop()
                    R("v", fn_rec)

                    def fn_bc(t):
                        for sb in range(2):
                            ins = t.matmul(SPS[:, sb * 512:(sb + 1) * 512], SEL64[:, :],
                                           PTB[:, sb * 512:(sb + 1) * 512], start=True, stop=True)
                        return ins
                    R("t", fn_bc)

                    def fn_out(v, p0=p0, ccj=ccj, b2=b2):
                        v.tensor_copy(SC1[:, 0:1024], SPS[:, 0:1024])
                        v.drain()
                        if p0 == 0:
                            v.tensor_tensor(CCT[0:64, ccj * S + b2 * 1024:ccj * S + b2 * 1024 + 1024],
                                            AVP[0:64, b2 * 1024:b2 * 1024 + 1024],
                                            SC1[0:64, 0:1024], op=ALU.mult)
                        else:
                            v.tensor_tensor(SCB[0:64, 0:1024],
                                            AVP[0:64, b2 * 1024:b2 * 1024 + 1024],
                                            SC1[0:64, 0:1024], op=ALU.mult)
                        v.drain()
                        return v.nop()
                    R("v", fn_out)
                    if p0 != 0:
                        R("y", mk_sdma([(CCT[64:128, ccj * S + b2 * 1024:ccj * S + b2 * 1024 + 1024],
                                         SCB[0:64, 0:1024])]))

        R("g", mk_gdma([(WSTG[:, :].rearrange("p (c j) -> p c j", c=8),
                        woT_in.rearrange("(c p) j -> p c j", p=128))]))
        for jc in range(8):
            for tsp in range(2):
                def fo_t(t, jc=jc, tsp=tsp):
                    for half in range(2):
                        t0 = (tsp * 2 + half) * 512
                        for c in range(8):
                            ins = t.matmul(SPS[:, half * 512:half * 512 + 512],
                                           WSTG[:, c * D + jc * 128:c * D + jc * 128 + 128],
                                           CCT[:, c * S + t0:c * S + t0 + 512],
                                           start=(c == 0), stop=(c == 7))
                    return ins
                R("t", fo_t)

                def fo_v(v, jc=jc, tsp=tsp):
                    for half in range(2):
                        t0 = (tsp * 2 + half) * 512
                        ins = v.tensor_scalar(ATT[:, jc * S + t0:jc * S + t0 + 512],
                                              SPS[:, half * 512:half * 512 + 512],
                                              BOF[:, jc:jc + 1], None, op0=ALU.add)
                    return ins
                R("v", fo_v)

        emit_block(runs)
        pre.close()

        g4 = contextlib.ExitStack()
        HST = g4.enter_context(nc.sbuf_tensor("HST", [128, 8 * TP1], BF16))

        xsc = contextlib.ExitStack()
        WIH = xsc.enter_context(nc.sbuf_tensor("WIH", [128, 8 * G3], BF16))
        XSC = xsc.enter_context(nc.sbuf_tensor("XSC", [128, G3], BF16))
        CXB = xsc.enter_context(nc.sbuf_tensor("CXBx", [128, 8], F32))
        xPSa = xsc.enter_context(nc.psum_tensor("xPSa", [128, 512], F32))
        xPSb = xsc.enter_context(nc.psum_tensor("xPSb", [128, 512], F32))

        runs = []
        R = lambda k, f: runs.append((k, f))
        R("g", mk_gdma([(WIH[:, :].rearrange("p (c j) -> p c j", c=8),
                        wihT_in.rearrange("(c p) j -> p c j", p=128)),
                       (CXB[:, :], ctxbf_in)]))
        for tt in range(NT):
            for spp in range(3):
                def fx_t(t, tt=tt, spp=spp):
                    for half in range(2):
                        sl = spp * 2 + half
                        ps = xPSa if half == 0 else xPSb
                        for c in range(8):
                            ins = t.matmul(ps[:, :],
                                           XNT[:, c * S + tt * 128:c * S + (tt + 1) * 128],
                                           WIH[:, c * G3 + sl * 512:c * G3 + sl * 512 + 512],
                                           start=(c == 0), stop=(c == 7))
                    return ins
                R("t", fx_t)

                def fx_v(v, spp=spp):
                    for half in range(2):
                        sl = spp * 2 + half
                        ps = xPSa if half == 0 else xPSb
                        ins = v.tensor_copy(XSC[:, sl * 512:sl * 512 + 512], ps[:, :])
                    return ins
                R("v", fx_v)
            R("y", mk_sdma([(xw_d[tt * 128:(tt + 1) * 128, :], XSC[:, :])]))

        def fz(v):
            v.memset(XSC[:, :], 0.0)
            v.drain()
            return v.nop()
        R("v", fz)
        R("y", mk_sdma([(xw_d[S:S + 1, :], XSC[0:1, :]),
                        (xw_d[S + 1:S + 2, :], XSC[1:2, :])]))

        R("g", mk_gdma([(WIH[:, 0:8 * D].rearrange("p (c j) -> p c j", c=8),
                        ctxwT_in.rearrange("(c p) j -> p c j", p=128))]))

        def fh_v1(v):
            for c in range(8):
                v.reduce_sum(MISC[:, c:c + 1], XNT[:, c * S:(c + 1) * S], axis=AX)
                v.drain()
            v.tensor_scalar_mul(MISC[:, 0:8], MISC[:, 0:8], 1.0 / S)
            v.drain()
            v.tensor_copy(XSC[:, 0:8], MISC[:, 0:8])
            v.drain()
            return v.nop()
        R("v", fh_v1)

        def fh_t(t):
            for j in range(8):
                for k in range(8):
                    ins = t.matmul(xPSa[:, j:j + 1],
                                   WIH[:, k * D + j * 128:k * D + j * 128 + 128],
                                   XSC[:, k:k + 1], start=(k == 0), stop=(k == 7))
            return ins
        R("t", fh_t)

        def fh_v2(v):
            return v.tensor_tensor(H0F[:, :], xPSa[:, 0:8], CXB[:, :], op=ALU.add)
        R("v", fh_v2)
        R("y", mk_sdma([(h0tmp_d, H0F[:, :])]))
        R("g", mk_gdma([(H0S[0:128:32, :].rearrange("g (h f) -> g h f", h=2),
                         h0tmp_d.rearrange("p (g h) -> g h p", g=4))], slow=True))

        def f_pcast(g):
            for e in range(NE):
                g.dma_start(out=gB_d[e * 16:(e + 1) * 16],
                            in_=gL_in[e * 16:(e + 1) * 16]).then_inc(PCD, 16)
                g.dma_start(out=uB_d[e * 16:(e + 1) * 16],
                            in_=uL_in[e * 16:(e + 1) * 16]).then_inc(PCD, 16)
                g.dma_start(out=dB_d[e * 8:(e + 1) * 8],
                            in_=dL_in[e * 8:(e + 1) * 8]).then_inc(PCD, 16)
            return g.nop()
        R("g", f_pcast)
        emit_block(runs)
        xsc.close()
        P_gru = ser_base[0]

        gsc = contextlib.ExitStack()
        WT = gsc.enter_context(nc.sbuf_tensor("WTg", [128, 8 * G3], BF16))
        HTR = gsc.enter_context(nc.sbuf_tensor("HTR", [128, 16], BF16))
        SGb = gsc.enter_context(nc.sbuf_tensor("SGb", [128, 1024], F32))
        NG1 = gsc.enter_context(nc.sbuf_tensor("NG1", [128, 512], F32))
        T1 = gsc.enter_context(nc.sbuf_tensor("T1g", [128, 512], F32))
        T2 = gsc.enter_context(nc.sbuf_tensor("T2g", [128, 512], F32))
        NN1 = gsc.enter_context(nc.sbuf_tensor("NN1", [128, 512], F32))
        ZH = gsc.enter_context(nc.sbuf_tensor("ZHg", [128, 512], F32))
        NG2 = gsc.enter_context(nc.sbuf_tensor("NG2g", [128, 512], F32))
        ZHN = gsc.enter_context(nc.sbuf_tensor("ZHNg", [128, 512], F32))
        SG2 = gsc.enter_context(nc.sbuf_tensor("SG2", [128, 512], F32))
        RGS = gsc.enter_context(nc.sbuf_tensor("RGS", [128, 2 * 768], BF16))
        psA = [gsc.enter_context(nc.psum_tensor(f"psA{p}", [128, 512], F32)) for p in range(2)]
        psB = [gsc.enter_context(nc.psum_tensor(f"psB{p}", [128, 256], F32)) for p in range(2)]
        psTg = [gsc.enter_context(nc.psum_tensor(f"psTg{g}", [128, 4], F32)) for g in range(4)]
        H = H0S

        with nc.Block() as blk:
            @blk.gpsimd
            def _(g):
                g.wait_ge(SER, P_gru)
                g.dma_start(out=WT[:, :].rearrange("p (c j) -> p c j", c=8),
                            in_=whhT_in.rearrange("(c p) j -> p c j", p=128)).then_inc(LD, 16)
                ld_cnt[0] += 16
                g.wait_ge(LD, ld_cnt[0])
                g.nop().then_inc(SGp, 1)
                with g.Fori(0, NB // 2) as i:
                    for kk in range(4):
                        par = kk % 2
                        step = i * 4 + kk
                        sveb = i * 32 + kk * 8 + 1
                        g.wait_ge(SV, sveb + 8)
                        col = step + 1
                        g.tensor_copy(HST[:, :].rearrange("p (c t) -> p c t", c=8)[:, :, bass.ts(col, 1)],
                                      HTR[:, par * 8:par * 8 + 8].rearrange("p (c o) -> p c o", o=1)).then_inc(SGp, 1)

            @blk.vector
            def _(v):
                v.wait_ge(SER, P_gru)
                for sb in [SGb, NG1, NG2, ZHN, SG2, T2, T1, NN1, ZH]:
                    v.memset(sb[:, :], 0.0)
                v.memset(RGS[:, :], 0.0)
                for ps in psA + psB + psTg:
                    v.memset(ps[:, :], 0.0)
                v.drain()
                v.tensor_copy(HST[:, :].rearrange("p (c t) -> p c t", c=8)[:, :, 0:1],
                              H0F[:, :].rearrange("p (c o) -> p c o", o=1))
                v.tensor_copy(HTR[:, 8:16], H0F[:, :])
                v.drain()
                v.nop().then_inc(SV, 1)
                with v.Fori(0, NB // 2) as i:
                    for kk in range(4):
                        par = kk % 2
                        sveb = i * 32 + kk * 8 + 1
                        sab = i * 12 + kk * 3 + 6
                        pB = psB[par]
                        q = par * 256
                        q2 = par * 512
                        v.wait_ge(SA, sab + 1)
                        v.wait_ge(SBg, i * 4 + kk + 1)
                        v.tensor_tensor(T1[0:97, q:q + 256], SGb[0:97, q2:q2 + 256],
                                        pB[0:97, :], op=ALU.mult).then_inc(SV, 1)
                        v.tensor_tensor(ZH[0:97, q:q + 256], SGb[0:97, q2 + 256:q2 + 512],
                                        H[0:97, :], op=ALU.mult).then_inc(SV, 1)
                        v.wait_ge(SV, sveb + 1)
                        v.tensor_tensor(T2[0:97, q:q + 256], T1[0:97, q:q + 256],
                                        RGS[0:97, par * 768 + 512:par * 768 + 768], op=ALU.add).then_inc(SV, 1)
                        v.wait_ge(SA, sab + 2)
                        v.tensor_scalar(NG2[0:97, q:q + 256], NG1[0:97, q:q + 256], 2.0, None,
                                        op0=ALU.mult).then_inc(SV, 1)
                        v.wait_ge(SV, sveb + 2)
                        v.tensor_tensor(ZHN[0:97, q:q + 256], ZH[0:97, q:q + 256],
                                        NG1[0:97, q:q + 256], op=ALU.subtract).then_inc(SV, 1)
                        v.wait_ge(SV, sveb + 5)
                        v.wait_ge(SA, sab + 3)
                        v.tensor_tensor(NN1[0:97, q:q + 256], SG2[0:97, q:q + 256],
                                        NG2[0:97, q:q + 256], op=ALU.mult).then_inc(SV, 1)
                        v.wait_ge(SV, sveb + 6)
                        v.tensor_tensor(H[0:97, :], NN1[0:97, q:q + 256],
                                        ZHN[0:97, q:q + 256], op=ALU.add).then_inc(SV, 1)
                        v.wait_ge(STp, i * 4 + kk + 1)
                        for g4i in range(3):
                            v.tensor_copy(HTR[:, par * 8 + 2 * g4i:par * 8 + 2 * g4i + 2],
                                          psTg[g4i][:, par * 2:par * 2 + 2])
                        v.tensor_copy(HTR[:, par * 8 + 6:par * 8 + 8],
                                      psTg[3][:, par * 2:par * 2 + 2]).then_inc(SV, 1)

            @blk.scalar
            def _(s):
                s.wait_ge(SER, P_gru)
                s.sem_inc(SA, 6)
                with s.Fori(0, NB // 2) as i:
                    for kk in range(4):
                        par = kk % 2
                        sveb = i * 32 + kk * 8 + 1
                        q = par * 256
                        q2 = par * 512
                        s.wait_ge(SM, i * 4 + kk + 1)
                        s.activation(SGb[0:97, q2:q2 + 512], psA[par][0:97, :],
                                     AF.Sigmoid).then_inc(SA, 1)
                        s.activation(NG1[0:97, q:q + 256], psA[par][0:97, 256:512],
                                     AF.Sigmoid, scale=-1.0).then_inc(SA, 1)
                        s.wait_ge(SV, sveb + 3)
                        s.activation(SG2[0:97, q:q + 256], T2[0:97, q:q + 256],
                                     AF.Sigmoid, scale=2.0).then_inc(SA, 1)

            @blk.tensor
            def _(t):
                t.wait_ge(SER, P_gru)
                t.wait_ge(SGp, 1)
                with t.Fori(0, NB // 2) as i:
                    for kk in range(4):
                        par = kk % 2
                        sveb = i * 32 + kk * 8 + 1
                        pA, pB = psA[par], psB[par]
                        opar = 1 - par
                        t.wait_ge(SV, sveb)
                        t.wait_ge(SA, i * 12 + kk * 3 + 2)
                        for c in range(8):
                            lh = HTR[:, opar * 8 + c:opar * 8 + c + 1]
                            base = c * G3
                            for g in range(4):
                                t.matmul(pA[32 * g:32 * g + 1, 0:512], lh,
                                         WT[:, base + 512 * g:base + 512 * g + 512],
                                         start=(c == 0), stop=False, tile_position=(0, 32 * g))
                        SDp = SD0 if par == 0 else SD1
                        t.wait_ge(SDp, i * 32 + (kk // 2) * 16 + 16)
                        for g in range(4):
                            ins = t.matmul(pA[32 * g:32 * g + 1, 0:512], E32[:, g:g + 1],
                                           RGS[:, par * 768:par * 768 + 512],
                                           start=False, stop=True, tile_position=(0, 32 * g))
                        ins.then_inc(SM, 1)
                        for c in range(8):
                            lh = HTR[:, opar * 8 + c:opar * 8 + c + 1]
                            base = c * G3
                            for g in range(4):
                                ins = t.matmul(pB[32 * g:32 * g + 1, 0:256], lh,
                                               WT[:, base + 2048 + 256 * g:base + 2048 + 256 * (g + 1)],
                                               start=(c == 0), stop=(c == 7), tile_position=(0, 32 * g))
                        ins.then_inc(SBg, 1)
                        t.wait_ge(SV, sveb + 7)
                        t.drain()
                        for c in range(8):
                            g, half = c // 2, c % 2
                            ins = t.matmul(psTg[g][:, par * 2 + half:par * 2 + half + 1],
                                           H[32 * g:32 * g + 1, 128 * half:128 * half + 128],
                                           ONE1[32 * g:32 * g + 1, 0:1],
                                           start=True, stop=True, tile_position=(32 * g, 0))
                        t.drain()
                        ins.then_inc(STp, 1)

            @blk.sync
            def _(y):
                y.wait_ge(SER, P_gru)
                for j in range(2):
                    SDp = SD0 if j % 2 == 0 else SD1
                    y.dma_start(out=RGS[0:128:32, j * 768:(j + 1) * 768],
                                in_=xw_d[j:j + 1, :].rearrange("r (g q) -> (r g) q", g=4)).then_inc(SDp, 16)
                with y.Fori(0, NB // 2) as i:
                    for kk in range(4):
                        par = kk % 2
                        step = i * 4 + kk
                        sveb = i * 32 + kk * 8 + 1
                        y.wait_ge(SV, sveb + 3)
                        r = step + 2
                        SDp = SD0 if par == 0 else SD1
                        y.dma_start(out=RGS[0:128:32, par * 768:par * 768 + 768],
                                    in_=xw_d[bass.ts(r, 1), :].rearrange("r (g q) -> (r g) q", g=4)).then_inc(SDp, 16)
        gsc.close()
        SV_END = 1 + 8 * S
        SGP_END = 1 + S

        ssc = contextlib.ExitStack()
        EXT = ssc.enter_context(nc.sbuf_tensor("EXT", [128, 8 * S], BF16))
        WES = ssc.enter_context(nc.sbuf_tensor("WES", [128, 8 * D], BF16))
        PRD = ssc.enter_context(nc.sbuf_tensor("PRD", [128, 80 * 128], BF16))
        SSB = ssc.enter_context(nc.sbuf_tensor("SSB", [128, 80], F32))
        G2B = ssc.enter_context(nc.sbuf_tensor("G2B", [128, 64], F32))
        SCs = ssc.enter_context(nc.sbuf_tensor("SCs", [128, 64], F32))
        PENR = ssc.enter_context(nc.sbuf_tensor("PENR", [128, 128], F32))
        sPS = ssc.enter_context(nc.psum_tensor("sPS", [128, 512], F32))
        sPT = ssc.enter_context(nc.psum_tensor("sPT", [128, 128], F32))

        runs = []
        R = lambda k, f: runs.append((k, f))

        def f_bridge(v):
            v.wait_ge(SV, SV_END)
            v.wait_ge(SGp, SGP_END)
            v.memset(PENR[:, :], 1.0)
            v.drain()
            return v.nop()
        R("v", f_bridge)

        R("g", mk_gdma([(WES[:, :].rearrange("p (c j) -> p c j", c=8),
                        exprwT_in.rearrange("(c p) j -> p c j", p=128))]))
        for jc in range(8):
            for tsp in range(4):
                def fe_t(t, jc=jc, tsp=tsp):
                    t0 = tsp * 512
                    for c in range(8):
                        ins = t.matmul(sPS[:, :], WES[:, c * D + jc * 128:c * D + jc * 128 + 128],
                                       XNT[:, c * S + t0:c * S + t0 + 512],
                                       start=(c == 0), stop=(c == 7))
                    return ins
                R("t", fe_t)

                def fe_v(v, jc=jc, tsp=tsp):
                    t0 = tsp * 512
                    return v.tensor_copy(EXT[:, jc * S + t0:jc * S + t0 + 512], sPS[:, :])
                R("v", fe_v)

        for tt in range(NT):
            def f_prod(v, tt=tt):
                for e in range(8):
                    for f in range(8):
                        a = HST[:, e * TP1 + 1 + tt * 128:e * TP1 + 1 + (tt + 1) * 128]
                        b = HST[:, f * TP1 + 1 + tt * 128:f * TP1 + 1 + (tt + 1) * 128]
                        v.tensor_tensor(PRD[:, (e * 8 + f) * 128:(e * 8 + f) * 128 + 128],
                                        a, b, op=ALU.mult)
                for e in range(8):
                    a = HST[:, e * TP1 + 1 + tt * 128:e * TP1 + 1 + (tt + 1) * 128]
                    xb = EXT[:, e * S + tt * 128:e * S + (tt + 1) * 128]
                    v.tensor_tensor(PRD[:, (64 + e) * 128:(64 + e) * 128 + 128],
                                    a, xb, op=ALU.mult)
                    v.tensor_tensor(PRD[:, (72 + e) * 128:(72 + e) * 128 + 128],
                                    xb, xb, op=ALU.mult)
                v.drain()
                return v.nop()
            R("v", f_prod)

            def f_sums(t):
                for idx in range(80):
                    ins = t.matmul(sPT[:, idx:idx + 1],
                                   PRD[:, idx * 128:(idx + 1) * 128], ONEB[:, :],
                                   start=True, stop=True)
                return ins
            R("t", f_sums)

            def f_ev(v):
                return v.tensor_copy(SSB[:, :], sPT[:, 0:80])
            R("v", f_ev)

            def f_sq(s):
                s.activation(MISC[:, 8:16], SSB[:, 0:72:9], AF.Sqrt)
                s.drain()
                return s.activation(MISC[:, 16:24], SSB[:, 72:80], AF.Sqrt)
            R("s", f_sq)

            def f_elt(v, tt=tt):
                v.tensor_scalar_max(MISC[:, 8:16], MISC[:, 8:16], 1e-12)
                v.drain()
                v.reciprocal(MISC[:, 24:32], MISC[:, 8:16])
                v.drain()
                for f in range(8):
                    v.tensor_scalar(G2B[:, f:64:8], SSB[:, f:64:8],
                                    MISC[:, 24 + f:25 + f], None, op0=ALU.mult)
                v.drain()
                for e in range(8):
                    v.tensor_scalar(G2B[:, 8 * e:8 * e + 8], G2B[:, 8 * e:8 * e + 8],
                                    MISC[:, 24 + e:25 + e], None, op0=ALU.mult)
                v.drain()
                v.tensor_scalar(G2B[:, 0:64:9], G2B[:, 0:64:9], 1.0, None, op0=ALU.subtract)
                v.drain()
                v.tensor_tensor(G2B[:, :], G2B[:, :], G2B[:, :], op=ALU.mult)
                v.drain()
                for e in range(8):
                    v.reduce_sum(SCs[:, e:e + 1], G2B[:, 8 * e:8 * e + 8], axis=AX)
                v.drain()
                v.tensor_scalar_max(SCs[:, 8:16], SCs[:, 0:8], 1e-24)
                v.drain()
                v.reciprocal(SCs[:, 16:24], SCs[:, 8:16])
                v.drain()
                v.tensor_tensor(SCs[:, 24:32], SCs[:, 0:8], SCs[:, 16:24], op=ALU.mult)
                v.drain()
                v.reduce_sum(PEN[:, tt:tt + 1], SCs[:, 24:32], axis=AX)
                v.drain()
                v.tensor_tensor(SCs[:, 32:40], SSB[:, 64:72], MISC[:, 24:32], op=ALU.mult)
                v.drain()
                v.tensor_tensor(SCs[:, 40:48], MISC[:, 8:16], MISC[:, 24:32], op=ALU.mult)
                v.drain()
                v.tensor_scalar_max(SCs[:, 40:48], SCs[:, 40:48], 1e-8)
                v.drain()
                v.tensor_scalar_max(SCs[:, 48:56], MISC[:, 16:24], 1e-8)
                v.drain()
                v.tensor_tensor(SCs[:, 56:64], SCs[:, 40:48], SCs[:, 48:56], op=ALU.mult)
                v.drain()
                v.reciprocal(SCs[:, 40:48], SCs[:, 56:64])
                v.drain()
                v.tensor_tensor(SCs[:, 48:56], SCs[:, 32:40], SCs[:, 40:48], op=ALU.mult)
                v.drain()
                v.tensor_scalar(CW[:, tt * 8:tt * 8 + 8], SCs[:, 48:56], -1.0, 1.0,
                                op0=ALU.mult, op1=ALU.add)
                v.drain()
                return v.nop()
            R("v", f_elt)

        def f_pt(t):
            return t.matmul(sPT[0:1, 0:16], ONE1[:, :], PEN[:, :], start=True, stop=True)
        R("t", f_pt)

        def f_pv(v):
            v.tensor_copy(MISC[0:1, 0:16], sPT[0:1, 0:16])
            v.drain()
            v.reduce_sum(MISC[0:1, 16:17], MISC[0:1, 0:16], axis=AX)
            v.drain()
            v.tensor_scalar_mul(MISC[0:1, 16:17], MISC[0:1, 16:17], 1.0 / (8.0 * S))
            v.drain()
            return v.nop()
        R("v", f_pv)

        def f_pb(v):
            v.tensor_scalar(PENR[0:1, 0:128], PENR[0:1, 0:128],
                            MISC[0:1, 16:17], None, op0=ALU.mult)
            v.drain()
            return v.nop()
        R("v", f_pb)
        R("y", mk_sdma([(pen_in_d, PENR[0:1, 0:128])]))

        def f_cc(g):
            g.collective_compute("AllReduce", ALU.add,
                                 replica_groups=[list(range(8))],
                                 ins=[pen_in_d], outs=[pen_out_d]).then_inc(CCS, 1)
            g.wait_ge(CCS, 1)
            return g.nop()
        R("g", f_cc)
        R("y", mk_sdma([(MISC[:, 17:18], pen_out_d.rearrange("o p -> p o"))], slow=True))

        def f_fac(v):
            v.tensor_scalar(MISC[:, 18:19], MISC[:, 17:18], 1.0, None, op0=ALU.add)
            v.drain()
            return v.nop()
        R("v", f_fac)

        for tt in range(NT):
            def f_cw1(v, tt=tt):
                cd = CW[:, tt * 8:tt * 8 + 8]
                v.reduce_max(MISC[:, 20:21], cd, axis=AX)
                v.drain()
                v.tensor_scalar(SCs[:, 0:8], cd, MISC[:, 20:21], None, op0=ALU.is_equal)
                v.drain()
                v.tensor_scalar(SCs[:, 8:16], SCs[:, 0:8], 1e30, None, op0=ALU.mult)
                v.drain()
                v.tensor_tensor(SCs[:, 16:24], cd, SCs[:, 8:16], op=ALU.subtract)
                v.drain()
                v.reduce_max(MISC[:, 21:22], SCs[:, 16:24], axis=AX)
                v.drain()
                v.tensor_scalar(SCs[:, 24:32], cd, MISC[:, 21:22], None, op0=ALU.is_equal)
                v.drain()
                v.tensor_tensor(MISC[:, 22:23], MISC[:, 20:21], MISC[:, 21:22], op=ALU.subtract)
                v.drain()
                return v.nop()
            R("v", f_cw1)

            def f_cw2(s):
                return s.activation(MISC[:, 23:24], MISC[:, 22:23], AF.Sigmoid,
                                    scale=MISC[:, 18:19])
            R("s", f_cw2)

            def f_cw3(v, tt=tt):
                v.tensor_scalar(MISC[:, 24:25], MISC[:, 23:24], -1.0, 1.0,
                                op0=ALU.mult, op1=ALU.add)
                v.drain()
                v.tensor_scalar(SCs[:, 32:40], SCs[:, 0:8], MISC[:, 23:24], None, op0=ALU.mult)
                v.drain()
                v.tensor_scalar(SCs[:, 40:48], SCs[:, 24:32], MISC[:, 24:25], None, op0=ALU.mult)
                v.drain()
                v.tensor_tensor(CW[:, tt * 8:tt * 8 + 8], SCs[:, 32:40],
                                SCs[:, 40:48], op=ALU.add)
                v.drain()
                return v.nop()
            R("v", f_cw3)
        emit_block(runs)
        ssc.close()
        g4.close()

        msc = contextlib.ExitStack()
        Y = msc.enter_context(nc.sbuf_tensor("Yacc", [128, 8 * 1024], F32))
        AT = msc.enter_context(nc.sbuf_tensor("ATm", [128, 16 * 1024], BF16))
        GUW = msc.enter_context(nc.sbuf_tensor("GUW", [128, 4 * 2048], BF16))
        DWN = msc.enter_context(nc.sbuf_tensor("DWN", [128, 4 * 2048], BF16))
        CWTR = msc.enter_context(nc.sbuf_tensor("CWTR", [128, S], BF16))
        CWBE = msc.enter_context(nc.sbuf_tensor("CWBE", [128, 1024], F32))
        SEL8 = msc.enter_context(nc.sbuf_tensor("SEL8", [128, 8 * 128], BF16))
        U12 = msc.enter_context(nc.sbuf_tensor("U12", [128, 3072], F32))
        CWB8 = msc.enter_context(nc.sbuf_tensor("CWB8", [128, 8], BF16))
        EYEF = msc.enter_context(nc.sbuf_tensor("EYEF", [128, 128], F32))
        psc = contextlib.ExitStack()
        tPS = psc.enter_context(nc.psum_tensor("tPS", [128, 128], BF16))

        runs = []
        R = lambda k, f: runs.append((k, f))
        consume_run = []
        chunk_dmas = []
        NCH = 2 * NE * 24

        R("y", mk_sdma([(EYEF[:, :], eyef_in), (SEL8[:, :], sel8_in)]))

        def f_cz(v):
            v.memset(CWTR[:, :], 0.0)
            v.drain()
            return v.nop()
        R("v", f_cz)
        for tt in range(NT):
            def f_ct1(v, tt=tt):
                v.tensor_copy(CWB8[:, :], CW[:, tt * 8:tt * 8 + 8])
                v.drain()
                return v.nop()
            R("v", f_ct1)

            def f_ct2(t):
                return t.matmul(tPS[0:8, 0:128], CWB8[:, :], EYEB[:, :],
                                start=True, stop=True, is_transpose=True)
            R("t", f_ct2)

            def f_ct3(v, tt=tt):
                return v.tensor_copy(CWTR[0:8, tt * 128:(tt + 1) * 128], tPS[0:8, 0:128])
            R("v", f_ct3)

        P_moe = emit_block(runs) + len(runs)
        psc.close()
        mPS8 = [msc.enter_context(nc.psum_tensor(f"mPS8_{p}", [128, 512], F32)) for p in range(8)]

        NST = es.enter_context(nc.semaphore("NST"))
        NSL = es.enter_context(nc.semaphore("NSL"))
        NAU = es.enter_context(nc.semaphore("NAU"))
        NTD = es.enter_context(nc.semaphore("NTD"))
        NDV = es.enter_context(nc.semaphore("NDV"))
        NCB = es.enter_context(nc.semaphore("NCB"))
        NXN = es.enter_context(nc.semaphore("NXN"))
        NEP = es.enter_context(nc.semaphore("NEP"))
        WDS = [es.enter_context(nc.semaphore(f"WDS{k}")) for k in range(8)]

        sched = []
        for mb in range(2):
            for e in range(NE):
                sched.append((mb, e))
        chunk_marks = []
        gi = 0
        di = 0
        for mb, e in sched:
            for fc in range(16):
                chunk_marks.append(("G", gi + 1))
                chunk_dmas.append(2)
                gi += 1
            for hc in range(8):
                chunk_marks.append(("D", di + 1))
                chunk_dmas.append(1)
                di += 1
        assert len(chunk_marks) == NCH
        wd_cum.extend(np.cumsum(chunk_dmas).tolist())
        slot_cum = []
        _cnt = [0] * 8
        for mb, e in sched:
            for fc in range(16):
                _cnt[fc % 4] += 2
                slot_cum.append((fc % 4, _cnt[fc % 4]))
            for hc in range(8):
                _cnt[4 + hc % 4] += 1
                slot_cum.append((4 + hc % 4, _cnt[4 + hc % 4]))
        EP_STEPS = 7
        EP_MB = 8 * EP_STEPS

        with nc.Block() as blk:
            @blk.tensor
            def _(t):
                t.wait_ge(SER, P_moe)
                gi = 0
                di = 0
                ci = 0
                xn = 0
                for mb, e in sched:
                    dn_before = di
                    if e < 8:
                        if gi >= 2:
                            t.wait_ge(NAU, gi - 1)
                        t.wait_ge(NDV, dn_before)
                        if mb == 1 and e == 0:
                            t.wait_ge(NEP, EP_MB)
                        for sb in range(2):
                            ins = t.matmul(mPS8[sb][:, :], SEL8[:, e * 128:(e + 1) * 128],
                                           CWTR[:, mb * 1024 + sb * 512:mb * 1024 + sb * 512 + 512],
                                           start=True, stop=True)
                        ins.then_inc(NCB, 1)
                        xn += 1
                    for fc in range(16):
                        par = fc % 2
                        s4 = fc % 4
                        t.wait_ge(WDS[slot_cum[ci][0]], 16 * slot_cum[ci][1])
                        if gi >= 2:
                            t.wait_ge(NAU, gi - 1)
                        if fc == 0:
                            t.wait_ge(NDV, dn_before)
                            if e < 8:
                                t.wait_ge(NXN, xn)
                            if mb == 1 and e == 0:
                                t.wait_ge(NEP, EP_MB)
                        for gu in range(2):
                            for half in range(2):
                                ps = mPS8[4 * par + gu * 2 + half]
                                for c in range(8):
                                    rhs = XNT[:, c * S + mb * 1024 + half * 512:
                                              c * S + mb * 1024 + half * 512 + 512]
                                    ins = t.matmul(
                                        ps[:, :],
                                        GUW[:, s4 * 2048 + gu * 1024 + c * 128:
                                             s4 * 2048 + gu * 1024 + c * 128 + 128],
                                        rhs, start=(c == 0), stop=(c == 7))
                        ins.then_inc(NST, 1)
                        gi += 1
                        ci += 1
                    aulast = gi
                    for hc in range(8):
                        q = hc % 2
                        q4 = hc % 4
                        t.wait_ge(WDS[slot_cum[ci][0]], 16 * slot_cum[ci][1])
                        if hc == 0:
                            t.wait_ge(NAU, aulast)
                        if di >= 2:
                            t.wait_ge(NDV, di - 1)
                        for half in range(2):
                            ps = mPS8[2 * q + half]
                            for fcc in range(16):
                                ins = t.matmul(ps[:, :],
                                               DWN[:, q4 * 2048 + fcc * 128:q4 * 2048 + fcc * 128 + 128],
                                               AT[:, fcc * 1024 + half * 512:fcc * 1024 + half * 512 + 512],
                                               start=(fcc == 0), stop=(fcc == 15))
                        ins.then_inc(NTD, 1)
                        di += 1
                        ci += 1
                    if e == NE - 1:
                        for lt in range(8):
                            base_i = mb * EP_MB + lt * EP_STEPS
                            for chalf in range(2):
                                t.wait_ge(NEP, base_i + 2 + chalf * 2)
                                for i in range(4):
                                    c = chalf * 4 + i
                                    ins = t.matmul(mPS8[i][:, 0:128],
                                                   U12[:, c * 128:(c + 1) * 128],
                                                   EYEF[:, :], start=True, stop=True)
                                ins.then_inc(NEP, 1)

            @blk.scalar
            def _(s):
                s.wait_ge(SER, P_moe)
                for gi in range(2 * NE * 16):
                    par = gi % 2
                    s.wait_ge(NST, gi + 1)
                    if gi >= 2:
                        s.wait_ge(NAU, gi - 1)
                    s.activation(U12[:, par * 1024:par * 1024 + 512], mPS8[4 * par][:, :], AF.Silu)
                    s.activation(U12[:, par * 1024 + 512:par * 1024 + 1024],
                                 mPS8[4 * par + 1][:, :], AF.Silu).then_inc(NSL, 1)

            @blk.vector
            def _(v):
                v.wait_ge(SER, P_moe)
                gi = 0
                di = 0
                cb = 0
                for mb, e in sched:
                    if e < 8:
                        cb += 1
                        v.wait_ge(NCB, cb)
                        v.tensor_copy(CWBE[:, 0:512], mPS8[0][:, :])
                        ins = v.tensor_copy(CWBE[:, 512:1024], mPS8[1][:, :])
                        ins.then_inc(NXN, 1)
                    for fc in range(16):
                        par = fc % 2
                        v.wait_ge(NSL, gi + 1)
                        if e < 8:
                            for half in range(2):
                                v.tensor_tensor(
                                    U12[:, 2048 + half * 512:2048 + half * 512 + 512],
                                    U12[:, par * 1024 + half * 512:par * 1024 + half * 512 + 512],
                                    mPS8[4 * par + 2 + half][:, :], op=ALU.mult)
                            v.drain()
                            for half in range(2):
                                ins = v.tensor_tensor(
                                    AT[:, fc * 1024 + half * 512:fc * 1024 + half * 512 + 512],
                                    U12[:, 2048 + half * 512:2048 + half * 512 + 512],
                                    CWBE[:, half * 512:half * 512 + 512], op=ALU.mult)
                        else:
                            for half in range(2):
                                ins = v.tensor_tensor(
                                    AT[:, fc * 1024 + half * 512:fc * 1024 + half * 512 + 512],
                                    U12[:, par * 1024 + half * 512:par * 1024 + half * 512 + 512],
                                    mPS8[4 * par + 2 + half][:, :], op=ALU.mult)
                        ins.then_inc(NAU, 1)
                        gi += 1
                    v.drain()
                    for hc in range(8):
                        q = hc % 2
                        v.wait_ge(NTD, di + 1)
                        for half in range(2):
                            dst = Y[:, hc * 1024 + half * 512:hc * 1024 + half * 512 + 512]
                            if e == 0:
                                ins = v.tensor_copy(dst, mPS8[2 * q + half][:, :])
                            else:
                                ins = v.tensor_tensor(dst, dst, mPS8[2 * q + half][:, :], op=ALU.add)
                        ins.then_inc(NDV, 1)
                        di += 1
                    if e == NE - 1:
                        for lt in range(8):
                            tt = mb * 8 + lt
                            base_i = mb * EP_MB + lt * EP_STEPS
                            v.wait_ge(NEP, base_i + 1)
                            for c in range(8):
                                v.tensor_copy(U12[:, c * 128:(c + 1) * 128],
                                              ATT[:, c * S + tt * 128:c * S + (tt + 1) * 128])
                            v.drain()
                            for c in range(8):
                                v.tensor_tensor(U12[:, c * 128:(c + 1) * 128],
                                                U12[:, c * 128:(c + 1) * 128],
                                                Y[:, c * 1024 + lt * 128:c * 1024 + (lt + 1) * 128],
                                                op=ALU.add)
                            v.drain()
                            v.nop().then_inc(NEP, 1)
                            for chalf in range(2):
                                v.wait_ge(NEP, base_i + 3 + chalf * 2)
                                for i in range(4):
                                    c = chalf * 4 + i
                                    ins = v.tensor_tensor(U12[:, 1024 + c * 128:1024 + (c + 1) * 128],
                                                          mPS8[i][:, 0:128],
                                                          U12[:, 2048 + c * 128:2048 + (c + 1) * 128],
                                                          op=ALU.add)
                                ins.then_inc(NEP, 1)

            @blk.sync
            def _(y):
                y.wait_ge(SER, P_moe)
                for mb in range(2):
                    for lt in range(8):
                        tt = mb * 8 + lt
                        base_i = mb * EP_MB + lt * EP_STEPS
                        y.wait_ge(NEP, base_i)
                        if lt == 0:
                            y.wait_ge(NDV, 72 * (mb + 1))
                        y.dma_start(out=U12[:, 2048:3072],
                                    in_=x_in[tt * 128:(tt + 1) * 128, :]).then_inc(LS, 16)
                        ls_cnt[0] += 16
                        y.wait_ge(LS, ls_cnt[0])
                        y.nop().then_inc(NEP, 1)
                        y.wait_ge(NEP, base_i + 6)
                        y.dma_start(out=out_d[tt * 128:(tt + 1) * 128, :],
                                    in_=U12[:, 1024:2048]).then_inc(LS, 16)
                        ls_cnt[0] += 16
                        y.wait_ge(LS, ls_cnt[0])
                        y.nop().then_inc(NEP, 1)

            @blk.gpsimd
            def _(g):
                g.wait_ge(SER, P_moe)
                g.wait_ge(PCD, 16 * 3 * NE)
                ci = 0
                for mb, e in sched:
                    for fc in range(16):
                        s4 = fc % 4
                        if ci >= 4:
                            kind, val = chunk_marks[ci - 4]
                            g.wait_ge(NST if kind == "G" else NTD, val)
                        g.dma_start(out=GUW[:, s4 * 2048:s4 * 2048 + 1024],
                                    in_=gB_d[e * 16 + fc]).then_inc(WDS[s4], 16)
                        g.dma_start(out=GUW[:, s4 * 2048 + 1024:s4 * 2048 + 2048],
                                    in_=uB_d[e * 16 + fc]).then_inc(WDS[s4], 16)
                        ci += 1
                    for hc in range(8):
                        q4 = hc % 4
                        if ci >= 4:
                            kind, val = chunk_marks[ci - 4]
                            g.wait_ge(NST if kind == "G" else NTD, val)
                        g.dma_start(out=DWN[:, q4 * 2048:q4 * 2048 + 2048],
                                    in_=dB_d[e * 8 + hc]).then_inc(WDS[4 + q4], 16)
                        ci += 1
        msc.close()
    return nc


PERM_W = _perm_w()
PERM_XW = _perm_xw()


def kernel(**inputs):
    xfull = np.asarray(inputs["L0_attn_x"])
    f32 = np.float32
    asc = np.ascontiguousarray
    wqT = asc(np.asarray(inputs["L0_attn_Wq"]).T.astype(f32))
    wkT = asc(np.asarray(inputs["L0_attn_Wk"]).T.astype(f32))
    wvT = asc(np.asarray(inputs["L0_attn_Wv"]).T.astype(f32))
    woT = asc(np.asarray(inputs["L0_attn_Wo"]).T.astype(f32))
    bq = np.asarray(inputs["L0_attn_bq"]).astype(f32)
    bk = np.asarray(inputs["L0_attn_bk"]).astype(f32)
    bv = np.asarray(inputs["L0_attn_bv"]).astype(f32)
    bo = np.asarray(inputs["L0_attn_bo"]).astype(f32)
    brow = asc(np.stack([bq, bk, bv, bo]))
    bo_f = asc(bo.reshape(8, 128).T)
    whhT = asc(np.asarray(inputs["L0_ffn_gru_w_hh"]).T.astype(f32)[:, PERM_W])
    wihT = asc(np.asarray(inputs["L0_ffn_gru_w_ih"]).T.astype(f32)[:, PERM_XW])
    ctxwT = asc(np.asarray(inputs["L0_ffn_ctx_w"]).T.astype(f32))
    ctxbf = asc(np.asarray(inputs["L0_ffn_ctx_b"]).astype(f32).reshape(8, 128).T)
    exprwT = asc(np.asarray(inputs["L0_ffn_expr_w"]).T.astype(f32))
    g9 = np.concatenate([np.asarray(inputs["L0_ffn_gate_w"]),
                         np.asarray(inputs["L0_ffn_shared_gate_w"])[None]],
                        axis=0).transpose(0, 2, 1).astype(f32)
    u9 = np.concatenate([np.asarray(inputs["L0_ffn_up_w"]),
                         np.asarray(inputs["L0_ffn_shared_up_w"])[None]],
                        axis=0).transpose(0, 2, 1).astype(f32)
    d9 = np.concatenate([np.asarray(inputs["L0_ffn_down_w"]),
                         np.asarray(inputs["L0_ffn_shared_down_w"])[None]],
                        axis=0).transpose(0, 2, 1).astype(f32)
    gL = asc(g9.reshape(NE, 8, 128, 16, 128).transpose(0, 3, 2, 1, 4).reshape(NE * 16, 128, 1024))
    uL = asc(u9.reshape(NE, 8, 128, 16, 128).transpose(0, 3, 2, 1, 4).reshape(NE * 16, 128, 1024))
    dL = asc(d9.reshape(NE, 16, 128, 8, 128).transpose(0, 3, 2, 1, 4).reshape(NE * 8, 128, 2048))

    inv = 1.0 / (10000.0 ** (np.arange(0, 64, 2, dtype=np.float64) / 64.0))
    ang = np.arange(S, dtype=np.float64)[:, None] * inv[None, :]
    cosE = asc(np.tile(np.cos(ang).astype(f32), (1, 8)))
    sinE = asc(np.tile(np.sin(ang).astype(f32), (1, 8)))
    eyeb = asc(np.eye(128, dtype=f32).astype(BF))
    eyef = asc(np.eye(128, dtype=f32))
    ku = np.arange(128)
    umask = asc(np.where(ku[:, None] > ku[None, :], np.float32(-1e9),
                         np.float32(0.0)).astype(BF))
    sel8 = np.zeros((128, 8 * 128), dtype=np.float32)
    for e in range(8):
        sel8[e, e * 128:(e + 1) * 128] = 1.0
    sel8 = asc(sel8.astype(BF))

    nc = build()
    in_maps = []
    for i in range(8):
        in_maps.append({
            "x": asc(xfull[i].astype(f32)),
            "wqT": wqT, "wkT": wkT, "wvT": wvT, "woT": woT,
            "brow": brow, "bo_f": bo_f,
            "whhT": whhT, "wihT": wihT,
            "ctxwT": ctxwT, "ctxbf": ctxbf,
            "exprwT": exprwT,
            "g9L": gL, "u9L": uL, "d9L": dL,
            "cosE": cosE, "sinE": sinE,
            "eyeb": eyeb, "eyef": eyef, "umask": umask, "sel8": sel8,
        })
    res = run_bass_kernel_spmd(nc, in_maps, core_ids=list(range(8)))
    out = np.stack([np.asarray(r["y"]) for r in res.results], axis=0)
    return out


if __name__ == "__main__":
    nc = build()
    print("built ok")
